# version 03
# baseline (speedup 1.0000x reference)
import math
import numpy as np
import ml_dtypes

import concourse.bass as bass
import concourse.mybir as mybir
from concourse.bass_utils import run_bass_kernel_spmd
import concourse.dve_ops as dve_ops
from concourse.dve_op import DveOp
from concourse.dve_spec import Spec, Src0, Src1, C0, C1, C2, One, lower, maxx, minn

F32 = mybir.dt.float32
BF16 = mybir.dt.bfloat16
AX = mybir.AxisListType.X
OP = mybir.AluOpType
AF = mybir.ActivationFunctionType

MAGIC = 12582912.0
NCORES = 8
S = 2048

def _register(name, spec, reference):
    for o in dve_ops.OPS:
        if o.name == name:
            return o
    ver = "v3"
    lowered = lower(spec, ver=ver, perf_en=False, name=name)
    op = DveOp(name, spec, subdim=False, uops_sha={ver: lowered.sha(ver)},
               reference=reference)
    dve_ops.OPS.append(op)
    dve_ops._SUB_OPCODE_FOR_NAME[name] = (
        dve_ops._CUSTOM_DVE_ROW_BASE + len(dve_ops.OPS) - 1)
    dve_ops.CUSTOM_DVE_SPECS[name] = spec
    dve_ops.CUSTOM_DVE_REFS[name] = reference
    return op

RNDQ = _register("ANT_RNDQ", Spec(body=(Src0 * C0 + C2) - C2),
                 lambda in0, in1, s0, s1, imm2: np.round(in0 * s0))
RNDM = _register("ANT_RNDM", Spec(body=(Src0 * Src1 + C2) - C2),
                 lambda in0, in1, s0, s1, imm2: np.round(in0 * in1))

ROPED = [h for h in range(16) if (h % 4) in (2, 3)]

def _head_perm64():
    order = []
    for half in range(2):
        base = 32 * half
        order += [base + 2 * f for f in range(16)]
        order += [base + 2 * f + 1 for f in range(16)]
    return order

def _tile_lhw(inputs, kvw):
    f32 = np.float32
    wql = np.asarray(inputs["L0_ffn_wq_light"], f32)
    wkvl = np.asarray(inputs["L0_ffn_wkv_light"], f32)
    wqh = np.asarray(inputs["L0_ffn_wq_heavy"], f32)
    wkh = kvw[:, :64].reshape(512, 1024).T
    wvh = kvw[:, 64:].reshape(512, 1024).T
    cols = np.concatenate([wql, wkvl[:, :512], wkvl[:, 512:],
                           wqh, wkh, wvh], axis=1)
    return np.ascontiguousarray(cols.reshape(8, 128, 24, 128).transpose(2, 0, 1, 3))

def host_prep(inputs):
    f32 = np.float32
    x = np.asarray(inputs["L0_attn_x"], f32)
    wqkv = np.asarray(inputs["L0_attn_w_qkv"], f32).copy()
    perm = _head_perm64()
    for h in ROPED:
        for sec in (0, 1024):
            base = sec + h * 64
            wqkv[:, base:base + 64] = wqkv[:, [base + p for p in perm]]
    qkvw_t = np.ascontiguousarray(
        wqkv.reshape(8, 128, 24, 128).transpose(2, 0, 1, 3))
    wo = np.asarray(inputs["L0_attn_w_o"], f32)
    lxi = np.asarray(inputs["L0_attn_log_xi"], f32).reshape(1, 16)

    import jax
    import jax.numpy as jnp
    with jax.default_device(jax.devices('cpu')[0]):
        pos_j = jnp.arange(S, dtype=jnp.float32)
        inv_freq_j = 10000.0 ** (-(jnp.arange(0, 64, 2, dtype=jnp.float32)
                                   / 64))
        theta_j = pos_j[:, None] * inv_freq_j[None, :]
        cos_t = np.asarray(jnp.cos(theta_j))
        sin_t = np.asarray(jnp.sin(theta_j))
    rows_c = np.zeros((128, S), f32)
    rows_s = np.zeros((128, S), f32)
    for r in range(128):
        rr = r % 64
        fr = 16 * (rr // 32) + (rr % 16)
        sg = np.float32(-1.0 if (rr % 32) < 16 else 1.0)
        rows_c[r] = cos_t[:, fr]
        rows_s[r] = sin_t[:, fr] * sg
    cos_hi, sin_hi = rows_c, rows_s
    cos_lo = np.zeros_like(cos_hi)
    sin_lo = np.zeros_like(sin_hi)

    JJ = np.zeros((128, 16), f32)
    for t in range(16):
        JJ[:, t] = 128 * t + np.arange(128)
    ident = np.eye(128, dtype=f32)
    identb = np.eye(128, dtype=ml_dtypes.bfloat16)
    triu = np.triu(np.ones((128, 128))).astype(ml_dtypes.bfloat16)

    gl = np.ascontiguousarray(
        np.asarray(inputs["L0_ffn_gamma_light"], f32).reshape(8, 128).T)
    gh = np.ascontiguousarray(
        np.asarray(inputs["L0_ffn_gamma_heavy"], f32).reshape(8, 128).T)
    qrt = np.ascontiguousarray(
        np.asarray(inputs["L0_ffn_q_routing_token"], f32).reshape(8, 128).T)
    kvw = np.asarray(inputs["L0_ffn_wkv_heavy"], f32).reshape(8, 128, 1024)
    shared = dict(
        qkvw_t=qkvw_t, wo=wo, lxi=lxi,
        cos_hi=cos_hi, cos_lo=cos_lo, sin_hi=sin_hi, sin_lo=sin_lo,
        JJ=JJ, ident=ident, identb=identb, triu=triu,
        gl=gl, gh=gh, qrt=qrt,
        lhw_t=_tile_lhw(inputs, kvw),
        wol_t=np.ascontiguousarray(
            np.asarray(inputs["L0_ffn_wo_light"], f32).reshape(4, 128, 1024)),
        woh_t=np.ascontiguousarray(
            np.asarray(inputs["L0_ffn_wo_heavy"], f32).reshape(4, 128, 1024)),
        W1T_t=np.ascontiguousarray(np.asarray(inputs["L1_attn_W"], f32).T
            .reshape(8, 128, 24, 128).transpose(2, 0, 1, 3)),
        bvrow=np.ascontiguousarray(
            np.asarray(inputs["L1_attn_b"], f32)[None, 2048:]),
        b24=np.ascontiguousarray(
            np.asarray(inputs["L1_attn_b"], f32).reshape(24, 128).T),
        w1T=np.ascontiguousarray(np.asarray(inputs["L1_ffn_w1"], f32).T),
        w2T=np.ascontiguousarray(np.asarray(inputs["L1_ffn_w2"], f32).T),
        convw=np.ascontiguousarray(
            np.asarray(inputs["L1_ffn_conv_w"], f32)[:, 0, :]
            .reshape(32, 128, 3).transpose(1, 0, 2)),
        convb=np.ascontiguousarray(
            np.asarray(inputs["L1_ffn_conv_b"], f32).reshape(32, 128).T),
    )
    in_maps = []
    for b in range(NCORES):
        m = dict(shared)
        m["xT"] = np.ascontiguousarray(x[b].T)
        in_maps.append(m)
    return in_maps

ENGINES = ("sync", "tensor", "vector", "scalar", "gpsimd")
ESEM = {"sync": "LDX", "gpsimd": "GPM", "tensor": "TE", "vector": "VE",
        "scalar": "SE"}
SEM_NAMES = ("LDX", "LWQ", "LWK", "LWV", "LWO", "GSM", "GAS", "GOUT", "GPM", "TE", "VE", "SE")

class KB:
    def __init__(self):
        self.ops = {e: [] for e in ENGINES}
        self.cnt = {s: 0 for s in SEM_NAMES}
        self.sems = {}

    def emit(self, eng, fn):
        self.ops[eng].append([fn, None])

    def dma(self, q, dst, src, sem=None, slow=False):
        if sem is None:
            sem = "LWQ" if q == "sync" else "GAS"
        self.cnt[sem] += 16
        tgt = self.cnt[sem]
        self.ops[q].append([
            lambda e, d=dst, s=src, sl=slow: e.dma_start(
                out=d, in_=s, allow_slow_non_contiguous=sl), (sem, 16)])
        return (sem, tgt)

    def wait(self, eng, tok):
        sem, v = tok
        def f(e, s=sem, vv=v):
            return e.wait_ge(self.sems[s], vv)
        f._is_wait = True
        self.emit(eng, f)

    def mark(self, eng):
        sem = ESEM[eng]
        step = 16 if sem not in ("TE", "VE", "SE", "GPM") else 1
        lst = self.ops[eng]
        i = len(lst) - 1
        while i >= 0 and self._is_wait(lst[i][0]):
            i -= 1
        if i >= 0 and lst[i][1] is not None and lst[i][1][0] == sem:
            return (sem, self.cnt[sem])
        self.cnt[sem] += step
        tgt = self.cnt[sem]
        if lst and lst[-1][1] is None and not self._is_wait(lst[-1][0]):
            lst[-1][1] = (sem, step)
        else:
            self.ops[eng].append([lambda e: e.nop(nofuse=True), (sem, step)])
        return (sem, tgt)

    @staticmethod
    def _is_wait(fn):
        return getattr(fn, "_is_wait", False)

    def selfsync(self, eng):
        tok = self.mark(eng)
        self.wait(eng, tok)

    def barrier(self):
        toks = [self.mark(e) for e in ("tensor", "vector", "scalar")]
        for s in ("LDX", "LWQ", "LWK", "LWV", "LWO", "GSM", "GAS", "GOUT"):
            toks.append((s, self.cnt[s]))
        for e in ENGINES:
            for t in toks:
                if t[1] > 0:
                    self.wait(e, t)

    def replay(self, eng_name, eng):
        for fn, inc in self.ops[eng_name]:
            inst = fn(eng)
            if inc is not None:
                inst.then_inc(self.sems[inc[0]], inc[1])

def finish_emit(nc, es, kb):
    with es:
        with nc.Block() as block:
            @block.sync
            def _(eng):
                kb.replay("sync", eng)
            @block.tensor
            def _(eng):
                kb.replay("tensor", eng)
            @block.vector
            def _(eng):
                kb.replay("vector", eng)
            @block.scalar
            def _(eng):
                kb.replay("scalar", eng)
            @block.gpsimd
            def _(eng):
                kb.replay("gpsimd", eng)
    return nc

DUMP = None
SKIP1 = False
DBG_STAGE = 4

def build_nc():
    nc = bass.Bass("TRN2", target_bir_lowering=False)
    dr = {}
    def dram(name, shape, dt=F32, kind="ExternalInput"):
        dr[name] = nc.dram_tensor(name, shape, dt, kind=kind).ap()
        return dr[name]
    dram("xT", [1024, 2048])
    dram("qkvw_t", [24, 8, 128, 128])
    dram("wo", [1024, 1024])
    dram("lxi", [1, 16])
    for n in ("cos_hi", "cos_lo", "sin_hi", "sin_lo"):
        dram(n, [128, 2048])
    dram("JJ", [128, 16])
    dram("ident", [128, 128]); dram("identb", [128, 128], BF16)
    dram("triu", [128, 128], BF16)
    dram("gl", [128, 8]); dram("gh", [128, 8]); dram("qrt", [128, 8])
    dram("lhw_t", [24, 8, 128, 128])
    dram("wol_t", [4, 128, 1024]); dram("woh_t", [4, 128, 1024])
    dram("W1T_t", [24, 8, 128, 128]); dram("b24", [128, 24])
    dram("bvrow", [1, 1024])
    dram("w1T", [1024, 4096]); dram("w2T", [4096, 1024])
    dram("convw", [128, 32, 3]); dram("convb", [128, 32])
    dram("x3s", [2048, 1024], F32, kind="Internal")
    dram("x2s", [2048, 1024], F32, kind="Internal")
    out = dram("out", [1024, 2048] if DBG_STAGE in (1, 2) else [2048, 1024],
               F32, kind="ExternalOutput")

    from contextlib import ExitStack
    es = ExitStack()
    def sb(name, shape, dt):
        return es.enter_context(nc.sbuf_tensor(name, shape, dt))

    xT   = sb("xT_s", [128, 8, 2048], F32)
    xT_h = xT
    xB   = sb("xB", [128, 8, 2048], BF16)
    COSS = sb("COSS", [128, 2, 2048], F32)
    ARF  = sb("ARF", [128, 7744], F32)
    AB   = sb("AB", [128, 32832], BF16)
    idf  = sb("idf", [128, 128], F32)
    idb  = sb("idb", [128, 128], BF16)
    triu = sb("triu_s", [128, 128], BF16)
    qrtB = sb("qrtB", [128, 8], BF16)

    def carve(arena, off, n, shape=None):
        ap = arena[:, off:off + n]
        if shape is not None:
            import string
            letters = string.ascii_lowercase
            dims = " ".join(letters[i] for i in range(len(shape)))
            kw = {letters[i]: shape[i] for i in range(len(shape))}
            ap = ap.rearrange(f"p ({dims}) -> p {dims}", **kw)
        return ap

    cosT = COSS[:, 0, :]
    sinT = COSS[:, 1, :]
    o = 0
    stg  = carve(ARF, o, 2048); o += 2048
    stg2 = carve(ARF, o, 2048); o += 2048
    wqf  = carve(ARF, o, 2048, (2, 8, 128)); o += 2048
    RAW  = carve(ARF, o, 1056, (16, 66)); o += 1056
    RDEN = carve(ARF, o, 16); o += 16
    JJ   = carve(ARF, o, 16); o += 16
    ivb  = carve(ARF, o, 16); o += 16
    biasH = carve(ARF, o, 128, (8, 16)); o += 128
    colsc = carve(ARF, o, 6); o += 6
    lxi  = carve(ARF, o, 16)[0:1, :]; o += 16
    ivr  = carve(ARF, o, 16)[0:1, :]; o += 16
    nivr = carve(ARF, o, 16)[0:1, :]; o += 16
    ones128 = carve(ARF, o, 128)[0:1, :]; o += 128
    onescol = carve(ARF, o, 2)[:, 0:1]; o += 2
    rcol = carve(ARF, o, 16); o += 16
    rcells = carve(ARF, o, 8); o += 8
    gl = carve(ARF, o, 8); o += 8
    gh = carve(ARF, o, 8); o += 8
    qrt = carve(ARF, o, 8); o += 8
    PADB = carve(ARF, o, 128); o += 128
    cc0 = carve(ARF, o, 2); o += 2
    cc1 = carve(ARF, o, 2); o += 2
    ccm32 = carve(ARF, o, 2); o += 2
    cln32 = carve(ARF, o, 2); o += 2
    assert o <= 7744, o
    nc.const_aps.register(F32, 0.0, cc0[:, 0:1])
    nc.const_aps.register(F32, 1.0, cc1[:, 0:1])
    nc.const_aps.register(F32, -32.0, ccm32[:, 0:1])
    nc.const_aps.register(F32, 3.4657359027997265, cln32[:, 0:1])
    SUMT = carve(ARF, 0, 1024, (8, 128))
    XTOK = carve(ARF, 1024, 1024)
    XC   = carve(ARF, 2048, 1024)
    SQ   = carve(ARF, 3072, 1024)
    WOF  = carve(ARF, 4096, 1024)
    o = 0
    wqb  = carve(AB, o, 2048, (2, 8, 128)); o += 2048
    SBQ  = carve(AB, o, 2048); o += 2048
    SBK  = carve(AB, o, 2048); o += 2048
    QXc  = carve(AB, o, 2048); o += 2048
    KXc  = carve(AB, o, 2048); o += 2048
    PR   = carve(AB, o, 4096, (2, 2048)); o += 4096
    VA   = carve(AB, o, 2112, (16, 2, 66)); o += 2112
    ATTF = carve(AB, o, 16384, (8, 2048)); o += 16384
    assert o <= 32832, o
    QX = QXc[0:65, :]
    KX = KXc[0:65, :]
    woB = carve(AB, 0, 8192, (8, 1024))
    wolB = carve(AB, 2048, 4096, (4, 1024))
    ATT = carve(AB, 8192, 1024, (8, 128))

    def sbuf_alias(name, shape, dtype, base_handle, off_bytes):
        h = nc.virtual_sbuf_tensor(name, shape, dtype)
        nc._tensor(h.name, h.shape, h.dtype, type="SB")
        mloc = nc.lookup_mloc(h)
        base = nc.lookup_mloc(base_handle).addr
        mloc.allocated = not nc.target_bir_lowering
        mloc.addr = base + off_bytes
        return h

    SBT = sbuf_alias("SBT", [128, 16, 2048], BF16, xT, 0)
    CB16 = sbuf_alias("CB16", [128, 8192], BF16, COSS, 0)
    QT = carve(AB, 2048, 16384, (8, 2048))
    KT_lo = carve(AB, 18432, 14336, (7, 2048))
    KT_hi = CB16[:, 0:2048]
    VT_lo = carve(AB, 18432, 14336, (14, 1024))
    VT_hi = CB16[:, 2048:4096].rearrange("p (a b) -> p a b", a=2, b=1024)

    AF16 = sbuf_alias("AF16", [128, 6144], BF16, ARF, 4112 * 4)
    HBUF = carve(ARF, 0, 4160, (32, 130))
    CF32 = sbuf_alias("CF32", [128, 4096], F32, COSS, 0)
    X3B = CF32[:, 0:1024]
    XT4 = CF32[:, 1024:2048]
    XC4 = CF32[:, 2048:3072]
    SQ4 = X3B
    CONVO = CF32[:, 3072:3204]
    M1B = CF32[:, 3204:3336]
    SHB = CF32[:, 3336:3468]
    NXH = CF32[:, 3468:3532].rearrange("p (a b) -> p a b", a=32, b=2)
    C4 = carve(ARF, 7584, 8)
    INTX = AF16[:, 0:1024]
    XTB4 = AF16[:, 0:1024].rearrange("p (a b) -> p a b", a=8, b=128)
    INTH = AF16[:, 1024:5248].rearrange("p (a b) -> p a b", a=32, b=132)
    convw4 = carve(ARF, 7248, 96, (32, 3))
    convb4 = carve(ARF, 7344, 32)
    w1q = sbuf_alias("w1q", [128, 8, 4096], BF16, xT, 0)
    w2q = carve(AB, 0, 32768, (32, 1024))

    def kt_ap(dt):
        return KT_lo[:, dt, :] if dt < 7 else KT_hi

    def vt_ap(s16):
        return VT_lo[:, s16, :] if s16 < 14 else VT_hi[:, s16 - 14, :]

    C = [es.enter_context(nc.psum_tensor(f"C{j}", [128, 512], F32))
         for j in range(4)]
    PAV = es.enter_context(nc.psum_tensor("PAV", [128, 16, 128], F32))

    kb = KB()
    for s in SEM_NAMES:
        kb.sems[s] = es.enter_context(nc.semaphore(s))
    TE, VE, SE = "tensor", "vector", "scalar"

    bankfree = [None, None, None, None]

    def use_bank(j, eng=TE):
        if bankfree[j] is not None:
            kb.wait(eng, bankfree[j])

    def dump(rows):
        kb.barrier()
        toks = []
        for off, ap in rows:
            P, F = ap.shape[0], ap.shape[-1] if len(ap.shape) == 2 else None
            if F is None:
                F = 1
                for d in ap.shape[1:]:
                    F *= d
            toks.append(kb.dma("gpsimd", dr["out"][off:off+P, 0:F], ap, sem="GOUT"))
        kb.wait("gpsimd", toks[-1])
        return finish_emit(nc, es, kb)

    t_x = [kb.dma("sync", xT[:, t, :], dr["xT"][128*t:128*(t+1), :], sem="LDX")
           for t in range(8)]
    small = []
    small.append(kb.dma("gpsimd", cosT[:], dr["cos_hi"], sem="GSM"))
    small.append(kb.dma("gpsimd", sinT[:], dr["sin_hi"], sem="GSM"))
    small.append(kb.dma("gpsimd", stg[:], dr["cos_lo"], sem="GSM"))
    small.append(kb.dma("gpsimd", stg2[:], dr["sin_lo"], sem="GSM"))
    small.append(kb.dma("gpsimd", JJ[:], dr["JJ"], sem="GSM"))
    small.append(kb.dma("gpsimd", lxi[:], dr["lxi"], sem="GSM"))
    small.append(kb.dma("gpsimd", idf[:], dr["ident"], sem="GSM"))
    small.append(kb.dma("gpsimd", idb[:], dr["identb"], sem="GSM"))
    small.append(kb.dma("gpsimd", triu[:], dr["triu"], sem="GSM"))
    small.append(kb.dma("gpsimd", gl[:], dr["gl"], sem="GSM"))
    small.append(kb.dma("gpsimd", gh[:], dr["gh"], sem="GSM"))
    small.append(kb.dma("gpsimd", qrt[:], dr["qrt"], sem="GSM"))
    kb.emit(VE, lambda e: e.memset(cc0[:], 0.0))
    kb.emit(VE, lambda e: e.memset(cc1[:], 1.0))
    kb.emit(VE, lambda e: e.memset(ccm32[:], -32.0))
    kb.emit(VE, lambda e: e.memset(cln32[:], 3.4657359027997265))
    kb.emit(VE, lambda e: e.memset(ones128[:], 1.0))
    kb.emit(VE, lambda e: e.memset(onescol[:], 1.0))
    ve_cc = kb.mark(VE)
    kb.wait(VE, small[-1])
    kb.emit(VE, lambda e: e.tensor_add(out=cosT[:], in0=cosT[:], in1=stg[:]))
    kb.emit(VE, lambda e: e.tensor_add(out=sinT[:], in0=sinT[:], in1=stg2[:]))
    kb.wait(SE, small[-1])
    kb.wait(SE, ve_cc)
    kb.emit(SE, lambda e: e.activation(out=ivr[:], in_=lxi[:], func=AF.Exp,
                                       scale=-1.0))
    se_ivr = kb.mark(SE)
    kb.wait(VE, se_ivr)
    kb.emit(VE, lambda e: e.tensor_scalar(out=nivr[:], in0=ivr[:], scalar1=-1.0,
                                          scalar2=None, op0=OP.mult))
    kb.selfsync(VE)
    kb.wait(VE, t_x[-1])
    for t in range(8):
        kb.emit(VE, lambda e, t=t: e.tensor_copy(out=xB[:, t, :], in_=xT[:, t, :]))
    ve_init = kb.mark(VE)
    kb.wait(TE, ve_init)
    kb.emit(TE, lambda e: e.matmul(out=PAV[:, 0, 0:16], lhsT=ones128[:],
                                   rhs=ivr[:], start=True, stop=True))
    te_ivb = kb.mark(TE)
    kb.wait(VE, te_ivb)
    kb.emit(VE, lambda e: e.tensor_copy(out=ivb[:], in_=PAV[:, 0, 0:16]))
    kb.selfsync(VE)
    for h in range(1, 16, 2):
        kb.emit(VE, lambda e, h=h: e.tensor_scalar(
            out=biasH[:, h // 2, :], in0=JJ[:], scalar1=ivb[:, h:h+1], scalar2=None,
            op0=OP.mult))
    kb.emit(VE, lambda e: e.memset(VA[:, :, :, 64:66], 1.0))
    kb.emit(VE, lambda e: e.tensor_copy(out=qrtB[:], in_=qrt[:]))
    kb.barrier()

    RUN1 = not SKIP1
    SCALE = 0.125
    shuffle_mask = list(range(16, 32)) + list(range(16))
    last_cast = {0: None, 1: None}
    prev_pair_qk = None
    prev_last_exp = None

    def load_pair_w(colblk, par, sem):
        if last_cast[par] is not None:
            kb.wait("sync", last_cast[par])
        return [kb.dma("sync", wqf[:, par, dt, :], dr["qkvw_t"][colblk, dt], sem=sem)
                for dt in range(8)]

    def cast_pair_w(toks, par):
        kb.wait(VE, toks[-1])
        if last_cast[par] is not None:
            kb.wait(VE, last_cast[par])
        kb.emit(VE, lambda e, par=par: e.tensor_copy(
            out=wqb[:, par, :, :], in_=wqf[:, par, :, :]))
        return kb.mark(VE)

    def proj_pair(par, cast_tok, out_stg, scale, wait_se=None):
        kb.wait(TE, cast_tok)
        marks = []
        for j in range(4):
            use_bank(j)
            for dt in range(8):
                kb.emit(TE, lambda e, j=j, dt=dt, par=par: e.matmul(
                    out=C[j][:], lhsT=wqb[:, par, dt, :],
                    rhs=xB[:, dt, 512*j:512*(j+1)],
                    start=(dt == 0), stop=(dt == 7)))
            marks.append(kb.mark(TE))
        if wait_se is not None:
            kb.wait(SE, wait_se)
        for j in range(4):
            kb.wait(SE, marks[j])
            kb.emit(SE, lambda e, j=j, o=out_stg, sc=scale: e.activation(
                out=o[:, 512*j:512*(j+1)], in_=C[j][:], func=AF.Copy, scale=sc))
            bankfree[j] = kb.mark(SE)
        return kb.mark(SE), marks[-1]

    def rope(src, dst_bf):
        kb.selfsync(VE)
        kb.emit(VE, lambda e, s=src: e.stream_shuffle(out=stg2[:], in_=s[:],
                                                      mask=shuffle_mask))
        kb.selfsync(VE)
        kb.emit(VE, lambda e, s=src: e.tensor_mul(out=s[:], in0=s[:], in1=cosT[:]))
        kb.emit(VE, lambda e: e.tensor_mul(out=stg2[:], in0=stg2[:], in1=sinT[:]))
        kb.selfsync(VE)
        kb.emit(VE, lambda e, s=src, d=dst_bf: e.tensor_add(out=d[:], in0=s[:],
                                                            in1=stg2[:]))
        return kb.mark(VE)

    prev_rope_q = None
    prev_rope_k = None
    prev_raw = None
    for p in range(8 if RUN1 else 0):
        roped = (p % 2) == 1
        h_even, h_odd = 2 * p, 2 * p + 1
        tq = load_pair_w(p, 0, "LWQ")
        tk = load_pair_w(8 + p, 1, "LWK")
        cq = cast_pair_w(tq, 0)
        ck = cast_pair_w(tk, 1)
        se_q, te_q = proj_pair(0, cq, stg, SCALE, wait_se=prev_rope_k)
        last_cast[0] = te_q
        if DUMP == "proj0" and p == 0:
            return dump([(0, stg)])
        kb.wait(VE, se_q)
        if prev_pair_qk is not None:
            kb.wait(VE, prev_pair_qk)
        if roped:
            ve_q = rope(stg, SBQ)
        else:
            kb.emit(VE, lambda e: e.tensor_copy(out=SBQ[:], in_=stg[:]))
            ve_q = kb.mark(VE)
        prev_rope_q = ve_q
        se_k, te_k = proj_pair(1, ck, stg, 1.0, wait_se=ve_q)
        last_cast[1] = te_k
        kb.wait(VE, se_k)
        if roped:
            ve_k = rope(stg, SBK)
        else:
            kb.emit(VE, lambda e: e.tensor_copy(out=SBK[:], in_=stg[:]))
            ve_k = kb.mark(VE)
        prev_rope_k = ve_k
        kb.wait("gpsimd", ve_q)
        if prev_pair_qk is not None:
            kb.wait("gpsimd", prev_pair_qk)
        g_q = kb.dma("gpsimd", QX[0:64, :], SBQ[64:128, :])
        kb.wait("gpsimd", ve_k)
        g_k = kb.dma("gpsimd", KX[0:64, :], SBK[64:128, :])
        kb.emit("gpsimd", lambda e: e.iota(
            QX[64:65, :].rearrange("p (a b) -> p a b", a=16, b=128),
            pattern=[[128, 16], [0, 128]], channel_multiplier=0,
            allow_small_or_imprecise_dtypes=True))
        g_i = kb.mark("gpsimd")
        kb.wait(VE, g_i)
        kb.wait(VE, se_ivr)
        kb.emit(VE, lambda e, h=h_odd: e.tensor_scalar(
            out=QX[64:65, :], in0=QX[64:65, :], scalar1=nivr[0:1, h:h+1],
            scalar2=-32.0, op0=OP.mult, op1=OP.add))
        kb.emit(VE, lambda e: e.memset(KX[64:65, :], 1.0))
        ve_x = kb.mark(VE)

        if DUMP == "sbqk0" and p == 0:
            return dump([(0, SBQ), (128, SBK)])
        if DUMP == "sbqk1" and p == 1:
            return dump([(0, SBQ), (128, SBK)])
        if DUMP == "qxkx1" and p == 1:
            kb.wait("gpsimd", ve_x); kb.wait("gpsimd", g_k)
            return dump([(0, QX), (128, KX)])
        tv = load_pair_w(16 + p, 0, "LWV")
        cv = cast_pair_w(tv, 0)
        kb.wait(TE, cv)
        if prev_raw is not None:
            kb.wait(TE, prev_raw)
        last_v_te = None
        for g in range(4):
            for k in range(4):
                s16 = 4 * g + k
                for dt in range(8):
                    kb.emit(TE, lambda e, s16=s16, k=k, dt=dt: e.matmul(
                        out=PAV[:, k, :], lhsT=xB[:, dt, 128*s16:128*(s16+1)],
                        rhs=wqb[:, 0, dt, :], start=(dt == 0), stop=(dt == 7)))
            m = kb.mark(TE)
            kb.wait(VE, m)
            for k in range(4):
                s16 = 4 * g + k
                kb.emit(VE, lambda e, s16=s16, k=k, h=h_even: e.tensor_copy(
                    out=VA[:, s16, 0, 0:64], in_=PAV[:, k, 0:64]))
                kb.emit(VE, lambda e, s16=s16, k=k, h=h_odd: e.tensor_copy(
                    out=VA[:, s16, 1, 0:64], in_=PAV[:, k, 64:128]))
            ve_m = kb.mark(VE)
            kb.wait(TE, ve_m)
        last_cast[0] = kb.mark(TE)
        ve_v = ve_m

        if DUMP == "va0" and p == 0:
            return dump([(0, VA[:, :, 0, :].rearrange("p a b -> p (a b)")),
                         (128, VA[:, :, 1, :].rearrange("p a b -> p (a b)"))])
        for hh in range(2):
            h = 2 * p + hh
            odd = hh == 1
            if odd:
                kb.wait(TE, g_k); kb.wait(TE, ve_x)
            else:
                kb.wait(TE, ve_q); kb.wait(TE, ve_k)
            kb.wait(TE, ve_v)
            if prev_last_exp is not None:
                kb.wait(TE, prev_last_exp)
            exp_marks = {}
            av_done = {}
            for t in range(16):
                L = 2048 - 128 * t
                nch = (L + 511) // 512
                par = t % 2
                qk_marks = []
                for j in range(nch):
                    cl = min(512, L - 512 * j)
                    use_bank(j)
                    if odd:
                        kb.emit(TE, lambda e, t=t, j=j, cl=cl: e.matmul(
                            out=C[j][:, 0:cl], lhsT=KX[0:65, 128*t:128*(t+1)],
                            rhs=QX[0:65, 128*t+512*j:128*t+512*j+cl],
                            start=True, stop=True))
                    else:
                        kb.emit(TE, lambda e, t=t, j=j, cl=cl: e.matmul(
                            out=C[j][:, 0:cl], lhsT=SBK[0:64, 128*t:128*(t+1)],
                            rhs=SBQ[0:64, 128*t+512*j:128*t+512*j+cl],
                            start=True, stop=True))
                    qk_marks.append(kb.mark(TE))
                for j in range(nch):
                    cl = min(512, L - 512 * j)
                    kb.wait(SE, qk_marks[j])
                    if t >= 2 and j == 0:
                        kb.wait(SE, av_done[t - 2])
                    if odd:
                        kb.emit(SE, lambda e, t=t, j=j, cl=cl, par=par, h=h:
                                e.activation(
                            out=PR[:, par, 128*t+512*j:128*t+512*j+cl],
                            in_=C[j][:, 0:cl], func=AF.Exp,
                            bias=biasH[:, h // 2, t:t+1]))
                    else:
                        kb.emit(SE, lambda e, t=t, j=j, cl=cl, par=par:
                                e.activation(
                            out=PR[:, par, 128*t+512*j:128*t+512*j+cl],
                            in_=C[j][:, 0:cl], func=AF.Exp, bias=-32.0))
                    exp_marks[(t, j)] = kb.mark(SE)
                    bankfree[j] = exp_marks[(t, j)]
                kb.wait(VE, exp_marks[(t, 0)])
                kb.emit(VE, lambda e, t=t, par=par: e.tensor_mul(
                    out=PR[:, par, 128*t:128*(t+1)],
                    in0=PR[:, par, 128*t:128*(t+1)], in1=triu[:]))
                mask_m = kb.mark(VE)
                for s2 in range(t, 16):
                    j = (128 * (s2 - t)) // 512
                    kb.wait(TE, exp_marks[(t, j)])
                    if s2 == t:
                        kb.wait(TE, mask_m)
                    kb.emit(TE, lambda e, t=t, s2=s2, par=par, hh=hh: e.matmul(
                        out=PAV[:, s2, 0:66], lhsT=PR[:, par, 128*s2:128*(s2+1)],
                        rhs=VA[:, t, hh, :],
                        start=(t == 0 and s2 % 4 == 0),
                        stop=(t == s2 and s2 % 4 == 3)))
                av_done[t] = kb.mark(TE)
            kb.wait(VE, av_done[15])
            for s2 in range(16):
                kb.emit(VE, lambda e, s2=s2: e.tensor_copy(
                    out=RAW[:, s2, :], in_=PAV[:, s2, 0:66]))
            raw_m = kb.mark(VE)
            prev_raw = raw_m
            kb.wait(TE, raw_m)
            kb.selfsync(VE)
            kb.emit(VE, lambda e: e.reciprocal(out=RDEN[:], in_=RAW[:, :, 64]))
            kb.selfsync(VE)
            for s2 in range(16):
                kb.emit(VE, lambda e, s2=s2: e.tensor_scalar(
                    out=RAW[:, s2, 0:64], in0=RAW[:, s2, 0:64],
                    scalar1=RDEN[:, s2:s2+1], scalar2=None, op0=OP.mult))
            norm_m = kb.mark(VE)
            kb.wait(VE, norm_m)
            kb.wait(TE, norm_m)
            prow = 64 * (h % 2)
            ptile = h // 2
            for s2 in range(16):
                k = s2 % 4
                use_bank(k)
                if odd:
                    kb.emit(VE, lambda e, s2=s2: e.tensor_copy(
                        out=PADB[:, 64:128], in_=RAW[:, s2, 0:64]))
                    ve_p = kb.mark(VE)
                    kb.wait(TE, ve_p)
                    kb.emit(TE, lambda e, k=k: e.transpose(
                        out=C[k][:, 0:128], in_=PADB[:], identity=idf[:]))
                else:
                    kb.emit(TE, lambda e, s2=s2, k=k: e.transpose(
                        out=C[k][0:64, 0:128], in_=RAW[:, s2, 0:64],
                        identity=idf[:]))
                tm2 = kb.mark(TE)
                kb.wait(VE, tm2)
                kb.emit(VE, lambda e, s2=s2, k=k, prow=prow, ptile=ptile:
                        e.tensor_copy(
                    out=ATTF[prow:prow+64, ptile, 128*s2:128*(s2+1)],
                    in_=C[k][prow:prow+64, 0:128]))
                bankfree[k] = kb.mark(VE)
            kb.wait(SE, raw_m)
            prev_last_exp = exp_marks[(15, 0)]
            if DUMP == f"raw{h}":
                return dump([(0, RAW.rearrange("p a b -> p (a b)"))])
        prev_pair_qk = av_done[15]
        if DUMP == f"pair{p}":
            return dump([(0, ATTF[:, p, :])])

    kb.barrier()

    def transpose4(src_fn, dst_fn, idmat, n=8):
        for grp in range((n + 3) // 4):
            tg = []
            for k in range(min(4, n - 4 * grp)):
                kk = 4 * grp + k
                use_bank(k)
                kb.emit(TE, lambda e, kk=kk, k=k: e.transpose(
                    out=C[k][:, 0:128], in_=src_fn(kk), identity=idmat[:]))
                tg.append(kb.mark(TE))
            for k in range(min(4, n - 4 * grp)):
                kk = 4 * grp + k
                kb.wait(VE, tg[k])
                kb.emit(VE, lambda e, kk=kk, k=k: e.tensor_copy(
                    out=dst_fn(kk), in_=C[k][:, 0:128]))
                bankfree[k] = kb.mark(VE)
            ve_t = kb.mark(VE)
        return ve_t

    def ln_slice(s2, write_xtok_cb=None):
        transpose4(lambda kk: SUMT[:, kk, :],
                   lambda kk: XTOK[:, 128*kk:128*(kk+1)], idf)
        ln_slice_stats(s2, write_xtok_cb)

    def ln3_slice(s2):
        kb.selfsync(VE)
        kb.emit(VE, lambda e: e.reduce_sum(out=colsc[:, 0:1], in_=XTOK[:], axis=AX))
        kb.selfsync(VE)
        kb.emit(VE, lambda e: e.tensor_scalar(out=colsc[:, 1:2], in0=colsc[:, 0:1],
            scalar1=1.0/1024.0, scalar2=None, op0=OP.mult))
        kb.selfsync(VE)
        kb.emit(VE, lambda e: e.tensor_scalar(out=XC[:], in0=XTOK[:],
            scalar1=colsc[:, 1:2], scalar2=None, op0=OP.subtract))
        ve_c = kb.mark(VE)
        kb.wait(SE, ve_c)
        kb.emit(SE, lambda e: e.activation(out=SQ[:], in_=XC[:], func=AF.Square))
        se_sq = kb.mark(SE)
        kb.wait(VE, se_sq)
        kb.emit(VE, lambda e: e.reduce_sum(out=colsc[:, 2:3], in_=SQ[:], axis=AX))
        kb.selfsync(VE)
        kb.emit(VE, lambda e: e.tensor_scalar(out=colsc[:, 2:3], in0=colsc[:, 2:3],
            scalar1=1.0/1024.0, scalar2=1e-5, op0=OP.mult, op1=OP.add))
        ve_v2 = kb.mark(VE)
        kb.wait(SE, ve_v2)
        kb.emit(SE, lambda e: e.activation(out=colsc[:, 3:4], in_=colsc[:, 2:3],
                                           func=AF.Ln))
        kb.selfsync(SE)
        kb.emit(SE, lambda e: e.activation(out=colsc[:, 3:4], in_=colsc[:, 3:4],
                                           func=AF.Exp, scale=-0.5))
        se_r = kb.mark(SE)
        kb.wait(VE, se_r)
        kb.emit(VE, lambda e: e.tensor_scalar(out=XC[:], in0=XC[:],
            scalar1=colsc[:, 3:4], scalar2=None, op0=OP.mult))
        kb.selfsync(VE)
        kb.emit(VE, lambda e: e.reduce_max(out=colsc[:, 0:1], in_=XC[:], axis=AX,
                                           apply_absolute_value=True))
        kb.selfsync(VE)
        kb.emit(VE, lambda e: e.tensor_scalar_max(out=colsc[:, 0:1],
            in0=colsc[:, 0:1], scalar1=1e-5))
        kb.selfsync(VE)
        kb.emit(VE, lambda e: e.reciprocal(out=colsc[:, 1:2], in_=colsc[:, 0:1]))
        kb.selfsync(VE)
        kb.emit(VE, lambda e, s2=s2: e.tensor_scalar(out=SXC[:, s2:s2+1],
            in0=colsc[:, 1:2], scalar1=127.0, scalar2=None, op0=OP.mult))
        ve_x3 = kb.mark(VE)
        kb.wait("gpsimd", ve_x3)
        tok = kb.dma("gpsimd", dr["x3s"][128*s2:128*(s2+1), :], XC[:], sem="GAS")
        kb.wait(VE, tok)

    def ln_slice_stats(s2, write_xtok_cb=None):
        kb.selfsync(VE)
        kb.emit(VE, lambda e: e.reduce_sum(out=colsc[:, 0:1], in_=XTOK[:], axis=AX))
        kb.selfsync(VE)
        kb.emit(VE, lambda e: e.tensor_scalar(out=colsc[:, 1:2], in0=colsc[:, 0:1],
            scalar1=1.0/1024.0, scalar2=None, op0=OP.mult))
        kb.selfsync(VE)
        kb.emit(VE, lambda e: e.tensor_scalar(out=XC[:], in0=XTOK[:],
            scalar1=colsc[:, 1:2], scalar2=None, op0=OP.subtract))
        ve_c = kb.mark(VE)
        kb.wait(SE, ve_c)
        kb.emit(SE, lambda e: e.activation(out=SQ[:], in_=XC[:], func=AF.Square))
        se_sq = kb.mark(SE)
        kb.wait(VE, se_sq)
        kb.emit(VE, lambda e: e.reduce_sum(out=colsc[:, 2:3], in_=SQ[:], axis=AX))
        kb.selfsync(VE)
        kb.emit(VE, lambda e: e.tensor_scalar(out=colsc[:, 2:3], in0=colsc[:, 2:3],
            scalar1=1.0/1024.0, scalar2=1e-5, op0=OP.mult, op1=OP.add))
        ve_v2 = kb.mark(VE)
        kb.wait(SE, ve_v2)
        kb.emit(SE, lambda e: e.activation(out=colsc[:, 3:4], in_=colsc[:, 2:3],
                                           func=AF.Ln))
        kb.selfsync(SE)
        kb.emit(SE, lambda e: e.activation(out=colsc[:, 3:4], in_=colsc[:, 3:4],
                                           func=AF.Exp, scale=-0.5))
        se_r = kb.mark(SE)
        kb.wait(VE, se_r)
        kb.emit(VE, lambda e: e.tensor_scalar(out=XC[:], in0=XC[:],
            scalar1=colsc[:, 3:4], scalar2=None, op0=OP.mult))
        if write_xtok_cb is not None:
            write_xtok_cb()
        ve_n = kb.mark(VE)
        kb.wait(TE, ve_n)
        transpose4(lambda kk: XC[:, 128*kk:128*(kk+1)],
                   lambda kk: xT[:, kk, 128*s2:128*(s2+1)], idf)

    for t in range(8 if RUN1 else 0):
        tok = kb.dma("sync", WOF[:], dr["wo"][128*t:128*(t+1), :], sem="LWO")
        kb.wait(VE, tok)
        kb.emit(VE, lambda e, t=t: e.tensor_copy(out=woB[:, t, :], in_=WOF[:]))
        ve_w = kb.mark(VE)
        kb.wait("sync", ve_w)
    if RUN1:
        kb.wait(TE, ve_w)

    if DUMP == "attf":
        return dump([(128*t2, ATTF[:, t2, :]) for t2 in range(8)])
    for s2 in range(16 if RUN1 else 0):
        for dt in range(8):
            for ht in range(8):
                kb.emit(TE, lambda e, dt=dt, ht=ht, s2=s2: e.matmul(
                    out=PAV[:, dt, :], lhsT=woB[:, ht, 128*dt:128*(dt+1)],
                    rhs=ATTF[:, ht, 128*s2:128*(s2+1)],
                    start=(ht == 0), stop=(ht == 7)))
        wo_m = kb.mark(TE)
        kb.wait(VE, wo_m)
        for dt in range(8):
            kb.emit(VE, lambda e, dt=dt, s2=s2: e.tensor_add(
                out=SUMT[:, dt, :], in0=PAV[:, dt, :],
                in1=xT[:, dt, 128*s2:128*(s2+1)]))
        sum_m = kb.mark(VE)
        kb.wait(TE, sum_m)
        ln_slice(s2)
    kb.barrier()

    for t in range(8):
        kb.emit(VE, lambda e, t=t: e.tensor_copy(out=xB[:, t, :], in_=xT[:, t, :]))
    kb.barrier()

    if DBG_STAGE == 1:
        st = [kb.dma("gpsimd", dr["out"][128*t:128*(t+1), :], xT[:, t, :], sem="GOUT")
              for t in range(8)]
        kb.wait("gpsimd", st[-1])
        return finish_emit(nc, es, kb)

    Rb = COSS[:, 0, :]
    MB = COSS[:, 1, :]
    rrow = COSS[0:1, 0, :]
    srow = COSS[0:1, 1, :]
    t1row = stg[0:1, :]
    erow = stg2[0:1, :]
    sq_marks = []
    for dt in range(8):
        if dt > 0:
            kb.wait(SE, sq_marks[-1])
        kb.emit(SE, lambda e, dt=dt: e.activation(out=stg2[:], in_=xT[:, dt, :],
                                                  func=AF.Square))
        m_sq = kb.mark(SE)
        kb.wait(TE, m_sq)
        for j in range(4):
            kb.emit(TE, lambda e, dt=dt, j=j: e.matmul(
                out=C[j][0:1, 0:512], lhsT=onescol[:], rhs=stg2[:, 512*j:512*(j+1)],
                start=(dt == 0), stop=(dt == 7)))
        sq_marks.append(kb.mark(TE))
    kb.wait(SE, sq_marks[-1])
    for j in range(4):
        kb.emit(SE, lambda e, j=j: e.activation(
            out=erow[:, 512*j:512*(j+1)], in_=C[j][0:1, 0:512], func=AF.Ln))
    kb.selfsync(SE)
    for j in range(4):
        kb.emit(SE, lambda e, j=j: e.activation(
            out=rrow[:, 512*j:512*(j+1)], in_=erow[:, 512*j:512*(j+1)], func=AF.Exp,
            scale=-0.5))
    kb.selfsync(SE)
    for j in range(4):
        kb.emit(SE, lambda e, j=j: e.activation(
            out=rrow[:, 512*j:512*(j+1)], in_=rrow[:, 512*j:512*(j+1)],
            func=AF.Copy, scale=32.0))
    se_r2 = kb.mark(SE)
    for j in range(4):
        bankfree[j] = se_r2
    kb.wait(TE, se_r2)
    rb_m = []
    for j in range(4):
        kb.emit(TE, lambda e, j=j: e.matmul(
            out=C[j][:, 0:512], lhsT=ones128[:], rhs=rrow[:, 512*j:512*(j+1)],
            start=True, stop=True))
        rb_m.append(kb.mark(TE))
    for j in range(4):
        kb.wait(VE, rb_m[j])
        kb.emit(VE, lambda e, j=j: e.tensor_copy(out=Rb[:, 512*j:512*(j+1)],
                                                 in_=C[j][:, 0:512]))
        bankfree[j] = kb.mark(VE)
    ve_rb = kb.mark(VE)
    kb.wait("gpsimd", ve_rb)
    g_rc0 = kb.dma("gpsimd", PADB[0:16, 0:128], rrow[0:1, :])
    kb.wait(TE, g_rc0)
    kb.emit(TE, lambda e: e.transpose(out=PAV[:, 2, 0:16], in_=PADB[0:16, 0:128],
                                      identity=idf[0:16, 0:16]))
    m_rc = kb.mark(TE)
    kb.wait(VE, m_rc)
    kb.emit(VE, lambda e: e.tensor_copy(out=rcol[:], in_=PAV[:, 2, 0:16]))
    g_rc = kb.mark(VE)
    if DUMP == "rc0":
        kb.wait(VE, g_rc0)
        return dump([(0, PADB)])
    if DUMP == "rc1":
        return dump([(0, rcol)])
    if DUMP == "rms":
        return dump([(0, Rb)])
    s_marks = []
    for j in range(4):
        use_bank(j)
        for dt in range(8):
            kb.emit(TE, lambda e, j=j, dt=dt: e.matmul(
                out=C[j][0:1, 0:512], lhsT=qrtB[:, dt:dt+1],
                rhs=xB[:, dt, 512*j:512*(j+1)],
                start=(dt == 0), stop=(dt == 7)))
        s_marks.append(kb.mark(TE))
    for j in range(4):
        kb.wait(VE, s_marks[j])
        kb.emit(VE, lambda e, j=j: e.tensor_copy(
            out=srow[:, 512*j:512*(j+1)], in_=C[j][0:1, 0:512]))
        bankfree[j] = kb.mark(VE)
    kb.selfsync(VE)

    if DUMP == "srowd":
        return dump([(0, srow)])
    LOGK = float(np.log(np.float64(1152.0)))
    LOGK_HI = float(np.float32(LOGK)); LOGK_LO = float(np.float32(LOGK - np.float64(LOGK_HI)))
    A1 = float(np.float32(np.float32(LOGK_HI) + np.float32(
        np.float32(-np.log(np.float64(2048.0))))))
    cells = colsc
    c_nega, c_mt, c_tmp, c_lnv = [cells[0:1, k:k+1] for k in range(4)]
    c_maxs = rcells[0:1, 0:1]
    c_logk = rcells[0:1, 1:2]
    kb.emit(VE, lambda e: e.memset(c_logk[:], LOGK_HI))
    kb.selfsync(VE)
    kb.emit(VE, lambda e: e.tensor_scalar(out=c_logk[:], in0=c_logk[:],
        scalar1=LOGK_LO, scalar2=None, op0=OP.add))
    kb.emit(VE, lambda e: e.memset(c_nega[:], -A1))
    kb.emit(VE, lambda e: e.reduce_max(out=c_maxs[:], in_=srow[:], axis=AX))
    kb.selfsync(VE)

    route_state = {"iter": 0, "done": False}
    def route_step():
        k = route_state["iter"]
        if k >= 49 or route_state["done"]:
            return
        route_state["iter"] += 1
        kb.selfsync(VE)
        kb.emit(VE, lambda e: e.tensor_scalar(out=t1row[:], in0=srow[:],
            scalar1=c_nega[:], scalar2=None, op0=OP.min))
        kb.emit(VE, lambda e: e.tensor_scalar(out=c_mt[:], in0=c_nega[:],
            scalar1=c_maxs[:], scalar2=-1.0, op0=OP.min, op1=OP.mult))
        m1 = kb.mark(VE)
        kb.wait(SE, m1)
        kb.emit(SE, lambda e: e.activation(out=erow[:], in_=t1row[:],
            func=AF.Exp, bias=c_mt[:]))
        m2 = kb.mark(SE)
        kb.wait(VE, m2)
        kb.emit(VE, lambda e: e.reduce_sum(out=c_tmp[:], in_=erow[:], axis=AX))
        m3 = kb.mark(VE)
        kb.wait(SE, m3)
        kb.emit(SE, lambda e: e.activation(out=c_lnv[:], in_=c_tmp[:], func=AF.Ln))
        m4 = kb.mark(SE)
        kb.wait(VE, m4)
        kb.emit(VE, lambda e: e.tensor_tensor(out=c_tmp[:], in0=c_logk[:],
            in1=c_mt[:], op=OP.add))
        kb.selfsync(VE)
        kb.emit(VE, lambda e: e.tensor_tensor(out=c_tmp[:], in0=c_tmp[:],
            in1=c_lnv[:], op=OP.subtract))
        kb.selfsync(VE)
        kb.emit(VE, lambda e: e.tensor_scalar(out=c_nega[:], in0=c_tmp[:],
            scalar1=-1.0, scalar2=None, op0=OP.mult))

    def route_finish():
        while route_state["iter"] < 49:
            route_step()
        route_state["done"] = True
        kb.selfsync(VE)
        kb.emit(VE, lambda e: e.tensor_scalar(out=t1row[:], in0=srow[:],
            scalar1=c_nega[:], scalar2=0.0, op0=OP.subtract, op1=OP.is_ge))
        kb.selfsync(VE)
        kb.emit(VE, lambda e: e.tensor_tensor_scan(out=erow[:], data0=t1row[:],
            data1=t1row[:], initial=0.0, op0=OP.add, op1=OP.bypass))
        kb.selfsync(VE)
        kb.emit(VE, lambda e: e.tensor_scalar(out=erow[:], in0=erow[:],
            scalar1=1024.5, scalar2=None, op0=OP.is_le))
        kb.selfsync(VE)
        kb.emit(VE, lambda e: e.tensor_mul(out=srow[:], in0=t1row[:], in1=erow[:]))
        return kb.mark(VE)

    if DUMP == "maskfast":
        mask_ve = route_finish()
        kb.wait("gpsimd", mask_ve)
        return dump([(0, srow)])
    def cast_pair_g(toks, par, gcol):
        kb.wait(VE, toks[-1])
        if last_cast[par] is not None:
            kb.wait(VE, last_cast[par])
        for dt in range(8):
            kb.emit(VE, lambda e, par=par, dt=dt, g=gcol: e.tensor_scalar(
                out=wqb[:, par, dt, :], in0=wqf[:, par, dt, :],
                scalar1=g[:, dt:dt+1], scalar2=None, op0=OP.mult))
        return kb.mark(VE)

    def load_blk(colblk, par, sem):
        if last_cast[par] is not None:
            kb.wait("sync", last_cast[par])
        return [kb.dma("sync", wqf[:, par, dt, :], dr["lhw_t"][colblk, dt], sem=sem)
                for dt in range(8)]

    def proj_pair2(par, cast_tok, dst_bf, qscale):
        kb.wait(TE, cast_tok)
        marks = []
        for j in range(4):
            use_bank(j)
            for dt in range(8):
                kb.emit(TE, lambda e, j=j, dt=dt, par=par: e.matmul(
                    out=C[j][:], lhsT=wqb[:, par, dt, :],
                    rhs=xB[:, dt, 512*j:512*(j+1)],
                    start=(dt == 0), stop=(dt == 7)))
            marks.append(kb.mark(TE))
        for j in range(4):
            kb.wait(VE, marks[j])
            if qscale is None:
                kb.emit(VE, lambda e, j=j, d=dst_bf: e.tensor_mul(
                    out=d[:, 512*j:512*(j+1)], in0=C[j][:],
                    in1=Rb[:, 512*j:512*(j+1)]))
            else:
                kb.emit(VE, lambda e, j=j, d=dst_bf, qs=qscale:
                        e.scalar_tensor_tensor(
                    out=d[:, 512*j:512*(j+1)], in0=C[j][:], scalar=qs,
                    in1=Rb[:, 512*j:512*(j+1)], op0=OP.mult, op1=OP.mult))
            bankfree[j] = kb.mark(VE)
        return kb.mark(VE), marks[-1]

    def proj_v2(par, cast_tok):
        kb.wait(TE, cast_tok)
        if prev_raw is not None:
            kb.wait(TE, prev_raw)
        for g in range(4):
            for k in range(4):
                s16 = 4 * g + k
                for dt in range(8):
                    kb.emit(TE, lambda e, s16=s16, k=k, dt=dt, par=par: e.matmul(
                        out=PAV[:, k, :], lhsT=xB[:, dt, 128*s16:128*(s16+1)],
                        rhs=wqb[:, par, dt, :], start=(dt == 0), stop=(dt == 7)))
            m = kb.mark(TE)
            kb.wait(VE, m)
            kb.wait(VE, g_rc)
            for k in range(4):
                s16 = 4 * g + k
                kb.emit(VE, lambda e, s16=s16, k=k: e.tensor_scalar(
                    out=VA[:, s16, 0, 0:64], in0=PAV[:, k, 0:64],
                    scalar1=rcol[:, s16:s16+1], scalar2=None, op0=OP.mult))
                kb.emit(VE, lambda e, s16=s16, k=k: e.tensor_scalar(
                    out=VA[:, s16, 1, 0:64], in0=PAV[:, k, 64:128],
                    scalar1=rcol[:, s16:s16+1], scalar2=None, op0=OP.mult))
            ve_m = kb.mark(VE)
            kb.wait(TE, ve_m)
        return kb.mark(TE), ve_m

    def head_tail(h_parity, ptile, prow):
        kb.selfsync(VE)
        kb.emit(VE, lambda e: e.reciprocal(out=RDEN[:], in_=RAW[:, :, 64]))
        kb.selfsync(VE)
        for s2 in range(16):
            kb.emit(VE, lambda e, s2=s2: e.tensor_scalar(
                out=RAW[:, s2, 0:64], in0=RAW[:, s2, 0:64],
                scalar1=RDEN[:, s2:s2+1], scalar2=None, op0=OP.mult))
        norm_m = kb.mark(VE)
        kb.wait(VE, norm_m)
        kb.wait(TE, norm_m)
        for s2 in range(16):
            k = s2 % 4
            use_bank(k)
            if prow:
                kb.emit(VE, lambda e, s2=s2: e.tensor_copy(
                    out=PADB[:, 64:128], in_=RAW[:, s2, 0:64]))
                ve_p = kb.mark(VE)
                kb.wait(TE, ve_p)
                kb.emit(TE, lambda e, k=k: e.transpose(
                    out=C[k][:, 0:128], in_=PADB[:], identity=idf[:]))
            else:
                kb.emit(TE, lambda e, s2=s2, k=k: e.transpose(
                    out=C[k][0:64, 0:128], in_=RAW[:, s2, 0:64],
                    identity=idf[:]))
            tm2 = kb.mark(TE)
            kb.wait(VE, tm2)
            kb.emit(VE, lambda e, s2=s2, k=k, prow=prow, ptile=ptile:
                    e.tensor_copy(
                out=ATTF[prow:prow+64, ptile, 128*s2:128*(s2+1)],
                in_=C[k][prow:prow+64, 0:128]))
            bankfree[k] = kb.mark(VE)
        return kb.mark(VE)

    prev_last_exp = None
    for p4 in range(4):
        tq = load_blk(p4, 0, "LWQ")
        tk = load_blk(4 + p4, 1, "LWK")
        cq = cast_pair_g(tq, 0, gl)
        ck = cast_pair_g(tk, 1, gl)
        se_q, te_q = proj_pair2(0, cq, SBQ, 0.125)
        last_cast[0] = te_q
        se_k, te_k = proj_pair2(1, ck, SBK, None)
        last_cast[1] = te_k
        tv = load_blk(8 + p4, 0, "LWV")
        cv = cast_pair_g(tv, 0, gl)
        te_v, ve_v = proj_v2(0, cv)
        last_cast[0] = te_v
        for hh in range(2):
            h = 2 * p4 + hh
            rows = slice(64 * hh, 64 * hh + 64)
            kb.wait(TE, se_q); kb.wait(TE, se_k); kb.wait(TE, ve_v)
            if prev_last_exp is not None:
                kb.wait(TE, prev_last_exp)
            exp_m = {}
            av_m = {}
            for w in range(16):
                par = w % 2
                kw = w % 4
                use_bank(kw)
                kb.emit(TE, lambda e, w=w, kw=kw, rows=rows: e.matmul(
                    out=C[kw][:, 0:128], lhsT=SBK[rows, 128*w:128*(w+1)],
                    rhs=SBQ[rows, 128*w:128*(w+1)], start=True, stop=True))
                qkm = kb.mark(TE)
                kb.wait(SE, qkm)
                if w >= 2:
                    kb.wait(SE, av_m[w - 2])
                kb.emit(SE, lambda e, w=w, kw=kw, par=par: e.activation(
                    out=PR[:, par, 128*w:128*(w+1)], in_=C[kw][:, 0:128],
                    func=AF.Exp, bias=-32.0))
                exp_m[w] = kb.mark(SE)
                bankfree[kw] = exp_m[w]
                kb.wait(TE, exp_m[w])
                kb.emit(TE, lambda e, w=w, par=par, hh=hh: e.matmul(
                    out=PAV[:, w, 0:66], lhsT=PR[:, par, 128*w:128*(w+1)],
                    rhs=VA[:, w, hh, :], start=True, stop=True))
                av_m[w] = kb.mark(TE)
                if w % 4 == 3:
                    route_step()
            kb.wait(VE, av_m[15])
            for s2 in range(16):
                kb.emit(VE, lambda e, s2=s2: e.tensor_copy(
                    out=RAW[:, s2, :], in_=PAV[:, s2, 0:66]))
            raw_m = kb.mark(VE)
            prev_raw = raw_m
            kb.wait(TE, raw_m)
            head_tail(hh, h // 2, 64 * (h % 2))
            kb.wait(SE, raw_m)
            prev_last_exp = exp_m[15]
    if DUMP == "lattf":
        return dump([(128*t2, ATTF[:, t2, :]) for t2 in range(4)])
    kb.barrier()
    lw_tok = None
    for t4 in range(4):
        tok = kb.dma("sync", WOF[:], dr["wol_t"][t4], sem="LWO")
        kb.wait(VE, tok)
        if lw_tok is not None:
            kb.wait(VE, lw_tok)
        kb.emit(VE, lambda e, t4=t4: e.tensor_copy(out=wolB[:, t4, :], in_=WOF[:]))
        lw_tok = kb.mark(VE)
        kb.wait("sync", lw_tok)
    kb.wait(TE, lw_tok)
    for s2 in range(16):
        for dt in range(8):
            for ht in range(4):
                kb.emit(TE, lambda e, dt=dt, ht=ht, s2=s2: e.matmul(
                    out=PAV[:, dt, :], lhsT=wolB[:, ht, 128*dt:128*(dt+1)],
                    rhs=ATTF[:, ht, 128*s2:128*(s2+1)],
                    start=(ht == 0), stop=(ht == 3)))
        wo_m = kb.mark(TE)
        kb.wait(VE, wo_m)
        for dt in range(8):
            kb.emit(VE, lambda e, dt=dt, s2=s2: e.tensor_add(
                out=xT[:, dt, 128*s2:128*(s2+1)], in0=PAV[:, dt, :],
                in1=xT[:, dt, 128*s2:128*(s2+1)]))
        sum_m = kb.mark(VE)
        kb.wait(TE, sum_m)
        route_step()
    if DUMP == "postlight":
        return dump([(128*t, xT[:, t, :]) for t in range(8)])
    kb.barrier()
    prev_last_exp = None
    for p4 in range(4):
        tq = load_blk(12 + p4, 0, "LWQ")
        tk = load_blk(16 + p4, 1, "LWK")
        cq = cast_pair_g(tq, 0, gh)
        ck = cast_pair_g(tk, 1, gh)
        se_q, te_q = proj_pair2(0, cq, SBQ, 0.125)
        last_cast[0] = te_q
        se_k, te_k = proj_pair2(1, ck, SBK, None)
        last_cast[1] = te_k
        tv = load_blk(20 + p4, 0, "LWV")
        cv = cast_pair_g(tv, 0, gh)
        te_v, ve_v = proj_v2(0, cv)
        last_cast[0] = te_v
        for hh in range(2):
            h = 2 * p4 + hh
            rows = slice(64 * hh, 64 * hh + 64)
            kb.wait(TE, se_q); kb.wait(TE, se_k); kb.wait(TE, ve_v)
            if prev_last_exp is not None:
                kb.wait(TE, prev_last_exp)
            exp_m = {}
            av_done = {}
            for t in range(16):
                par = t % 2
                qk_marks = []
                for j in range(4):
                    use_bank(j)
                    kb.emit(TE, lambda e, t=t, j=j, rows=rows: e.matmul(
                        out=C[j][:], lhsT=SBK[rows, 128*t:128*(t+1)],
                        rhs=SBQ[rows, 512*j:512*(j+1)], start=True, stop=True))
                    qk_marks.append(kb.mark(TE))
                for j in range(4):
                    kb.wait(SE, qk_marks[j])
                    if t >= 2 and j == 0:
                        kb.wait(SE, av_done[t - 2])
                    kb.emit(SE, lambda e, t=t, j=j, par=par: e.activation(
                        out=PR[:, par, 512*j:512*(j+1)], in_=C[j][:],
                        func=AF.Exp, bias=-32.0))
                    exp_m[(t, j)] = kb.mark(SE)
                    bankfree[j] = exp_m[(t, j)]
                for s2 in range(16):
                    j = s2 // 4
                    kb.wait(TE, exp_m[(t, j)])
                    kb.emit(TE, lambda e, t=t, s2=s2, par=par, hh=hh: e.matmul(
                        out=PAV[:, s2, 0:66], lhsT=PR[:, par, 128*s2:128*(s2+1)],
                        rhs=VA[:, t, hh, :],
                        start=(t == 0 and s2 % 4 == 0),
                        stop=(t == 15 and s2 % 4 == 3)))
                av_done[t] = kb.mark(TE)
                if t % 4 == 3:
                    route_step()
            kb.wait(VE, av_done[15])
            for s2 in range(16):
                kb.emit(VE, lambda e, s2=s2: e.tensor_copy(
                    out=RAW[:, s2, :], in_=PAV[:, s2, 0:66]))
            raw_m = kb.mark(VE)
            prev_raw = raw_m
            kb.wait(TE, raw_m)
            head_tail(hh, 4 + h // 2, 64 * (h % 2))
            kb.wait(SE, raw_m)
            prev_last_exp = exp_m[(15, 0)]
    mask_ve = route_finish()
    kb.selfsync(VE)
    kb.wait(TE, mask_ve)
    mb_m = []
    for j in range(4):
        use_bank(j)
        kb.emit(TE, lambda e, j=j: e.matmul(
            out=C[j][:, 0:512], lhsT=ones128[:], rhs=srow[:, 512*j:512*(j+1)],
            start=True, stop=True))
        mb_m.append(kb.mark(TE))
    for j in range(4):
        kb.wait(VE, mb_m[j])
        kb.emit(VE, lambda e, j=j: e.tensor_copy(out=MB[:, 512*j:512*(j+1)],
                                                 in_=C[j][:, 0:512]))
        bankfree[j] = kb.mark(VE)
    ve_mb = kb.mark(VE)
    if DUMP == "maskd":
        kb.wait("gpsimd", ve_mb)
        return dump([(0, MB)])
    if DUMP == "hattf":
        return dump([(128*t2, ATTF[:, 4+t2, :]) for t2 in range(4)])
    kb.barrier()
    hw_tok = None
    for t4 in range(4):
        tok = kb.dma("sync", WOF[:], dr["woh_t"][t4], sem="LWO")
        kb.wait(VE, tok)
        if hw_tok is not None:
            kb.wait(VE, hw_tok)
        kb.emit(VE, lambda e, t4=t4: e.tensor_copy(out=wolB[:, t4, :], in_=WOF[:]))
        hw_tok = kb.mark(VE)
        kb.wait("sync", hw_tok)
    kb.wait(TE, hw_tok)
    for s2 in range(16):
        for dt in range(8):
            for ht in range(4):
                kb.emit(TE, lambda e, dt=dt, ht=ht, s2=s2: e.matmul(
                    out=PAV[:, dt, :], lhsT=wolB[:, ht, 128*dt:128*(dt+1)],
                    rhs=ATTF[:, 4 + ht, 128*s2:128*(s2+1)],
                    start=(ht == 0), stop=(ht == 3)))
        wo_m = kb.mark(TE)
        kb.wait(VE, wo_m)
        for dt in range(8):
            kb.emit(VE, lambda e, dt=dt, s2=s2: e.tensor_mul(
                out=SUMT[:, dt, :], in0=PAV[:, dt, :],
                in1=MB[:, 128*s2:128*(s2+1)]))
        kb.selfsync(VE)
        for dt in range(8):
            kb.emit(VE, lambda e, dt=dt, s2=s2: e.tensor_add(
                out=xT[:, dt, 128*s2:128*(s2+1)], in0=SUMT[:, dt, :],
                in1=xT[:, dt, 128*s2:128*(s2+1)]))
        sum_m = kb.mark(VE)
        kb.wait(TE, sum_m)
    kb.barrier()
    for s2 in range(16):
        transpose4(lambda kk, s2=s2: xT[:, kk, 128*s2:128*(s2+1)],
                   lambda kk: XTOK[:, 128*kk:128*(kk+1)], idf)
        ln_slice_stats(s2)
    kb.barrier()
    for t in range(8):
        kb.emit(VE, lambda e, t=t: e.tensor_copy(out=xB[:, t, :], in_=xT[:, t, :]))
    kb.barrier()

    if DBG_STAGE == 2:
        st = [kb.dma("gpsimd", dr["out"][128*t:128*(t+1), :], xT[:, t, :], sem="GOUT")
              for t in range(8)]
        kb.wait("gpsimd", st[-1])
        return finish_emit(nc, es, kb)

    SXC = rcol
    b24s = carve(ARF, 6144, 24)
    b24f = carve(ARF, 6144 + 24, 24)
    bvr = stg2[0:1, 0:1024]
    x2buf = carve(ARF, 4096, 2048).rearrange("p (a b) -> p a b", a=2, b=1024)
    t_b = kb.dma("gpsimd", b24f[:], dr["b24"], sem="GSM")
    t_bv = kb.dma("gpsimd", bvr[:], dr["bvrow"], sem="GSM")
    kb.wait(VE, t_bv)
    kb.emit(VE, lambda e: e.tensor_scalar(out=b24s[:], in0=b24f[:],
        scalar1=1.0/1024.0, scalar2=None, op0=OP.mult))
    ve_b = kb.mark(VE)
    for s2 in range(16):
        transpose4(lambda kk, s2=s2: xT[:, kk, 128*s2:128*(s2+1)],
                   lambda kk: XTOK[:, 128*kk:128*(kk+1)], idf)
        ve_t = kb.mark(VE)
        kb.wait("gpsimd", ve_t)
        tok = kb.dma("gpsimd", dr["x2s"][128*s2:128*(s2+1), :], XTOK[:], sem="GAS")
        kb.wait(VE, tok)
    kb.barrier()

    def l1_proj_block(blk, par, cast_tok, dst, bias_col, scale):
        kb.wait(TE, cast_tok)
        marks = []
        for j in range(4):
            use_bank(j)
            for dt in range(8):
                kb.emit(TE, lambda e, j=j, dt=dt, par=par: e.matmul(
                    out=C[j][:], lhsT=wqb[:, par, dt, :],
                    rhs=xB[:, dt, 512*j:512*(j+1)],
                    start=(dt == 0), stop=(dt == 7)))
            marks.append(kb.mark(TE))
        for j in range(4):
            kb.wait(VE, marks[j])
            kb.emit(VE, lambda e, j=j, d=dst, bc=bias_col, sc=scale:
                    e.tensor_scalar(
                out=d[:, 512*j:512*(j+1)], in0=C[j][:], scalar1=bc,
                scalar2=sc, op0=OP.add, op1=OP.mult))
            bankfree[j] = kb.mark(VE)
        return kb.mark(VE), marks[-1]

    def load_w1(colblk, par):
        if last_cast[par] is not None:
            kb.wait("sync", last_cast[par])
        return [kb.dma("sync", wqf[:, par, dt, :], dr["W1T_t"][colblk, dt],
                       sem=("LWQ" if par == 0 else "LWK"))
                for dt in range(8)]

    kb.wait(VE, ve_b)
    for blk in range(8):
        par = blk % 2
        t_w = load_w1(blk, par)
        c_w = cast_pair_w(t_w, par)
        se_m, te_m = l1_proj_block(blk, par, c_w, QT[:, blk, :],
                                   b24s[:, blk:blk+1], 1.0/1024.0)
        last_cast[par] = te_m
    for blk in range(8):
        par = blk % 2
        t_w = load_w1(8 + blk, par)
        c_w = cast_pair_w(t_w, par)
        se_m, te_m = l1_proj_block(8 + blk, par, c_w, kt_ap(blk),
                                   b24f[:, 8+blk:8+blk+1], 1.0)
        last_cast[par] = te_m
    kb.barrier()
    prev_exp = None
    for t in range(16):
        qk_marks = []
        for j in range(4):
            use_bank(j)
            for dt in range(8):
                kb.emit(TE, lambda e, t=t, j=j, dt=dt: e.matmul(
                    out=C[j][:], lhsT=kt_ap(dt)[:, 128*t:128*(t+1)],
                    rhs=QT[:, dt, 512*j:512*(j+1)],
                    start=(dt == 0), stop=(dt == 7)))
            qk_marks.append(kb.mark(TE))
        kb.wait(VE, qk_marks[-1])
        for j in range(4):
            kb.emit(VE, lambda e, j=j: e.reduce_max(
                out=colsc[:, j:j+1], in_=C[j][:], axis=AX, negate=True))
        kb.selfsync(VE)
        kb.emit(VE, lambda e: e.tensor_tensor(out=colsc[:, 0:1],
            in0=colsc[:, 0:1], in1=colsc[:, 1:2], op=OP.min))
        kb.emit(VE, lambda e: e.tensor_tensor(out=colsc[:, 2:3],
            in0=colsc[:, 2:3], in1=colsc[:, 3:4], op=OP.min))
        kb.selfsync(VE)
        kb.emit(VE, lambda e: e.tensor_tensor(out=colsc[:, 0:1],
            in0=colsc[:, 0:1], in1=colsc[:, 2:3], op=OP.min))
        nm = kb.mark(VE)
        kb.wait(SE, nm)
        for j in range(4):
            kb.emit(SE, lambda e, t=t, j=j: e.activation(
                out=SBT[:, t, 512*j:512*(j+1)], in_=C[j][:], func=AF.Exp,
                bias=colsc[:, 0:1],
                accum_out=colsc[:, 4+(j % 2):5+(j % 2)] if False else None))
            bankfree[j] = kb.mark(SE)
        se_e = kb.mark(SE)
        kb.wait(VE, se_e)
        kb.selfsync(VE)
        for j in range(4):
            kb.emit(VE, lambda e, t=t, j=j: e.reduce_sum(
                out=colsc[:, j:j+1], in_=SBT[:, t, 512*j:512*(j+1)], axis=AX))
        kb.selfsync(VE)
        kb.emit(VE, lambda e: e.tensor_add(out=colsc[:, 0:1], in0=colsc[:, 0:1],
                                           in1=colsc[:, 1:2]))
        kb.emit(VE, lambda e: e.tensor_add(out=colsc[:, 2:3], in0=colsc[:, 2:3],
                                           in1=colsc[:, 3:4]))
        kb.selfsync(VE)
        kb.emit(VE, lambda e, t=t: e.tensor_add(out=RDEN[:, t:t+1],
            in0=colsc[:, 0:1], in1=colsc[:, 2:3]))
        kb.selfsync(VE)
    kb.selfsync(VE)
    kb.emit(VE, lambda e: e.reciprocal(out=RDEN[:], in_=RDEN[:]))
    kb.selfsync(VE)
    ve_den = kb.mark(VE)
    kb.barrier()
    prev_vproj = None
    for vb in range(8):
        par = vb % 2
        t_w = load_w1(16 + vb, par)
        c_w = cast_pair_w(t_w, par)
        kb.wait(TE, c_w)
        kb.wait(TE, ve_den)
        for g in range(4):
            for k in range(4):
                s16 = 4 * g + k
                for dt in range(8):
                    kb.emit(TE, lambda e, s16=s16, k=k, dt=dt, par=par: e.matmul(
                        out=PAV[:, k, :], lhsT=xB[:, dt, 128*s16:128*(s16+1)],
                        rhs=wqb[:, par, dt, :], start=(dt == 0), stop=False))
                kb.emit(TE, lambda e, s16=s16, k=k, vb=vb: e.matmul(
                    out=PAV[:, k, :], lhsT=ones128[:],
                    rhs=bvr[:, 128*vb:128*(vb+1)], start=False, stop=True))
            m = kb.mark(TE)
            kb.wait(VE, m)
            for k in range(4):
                s16 = 4 * g + k
                kb.emit(VE, lambda e, s16=s16, k=k, vb=vb: e.tensor_scalar(
                    out=vt_ap(s16)[:, 128*vb:128*(vb+1)], in0=PAV[:, k, :],
                    scalar1=RDEN[:, s16:s16+1], scalar2=None, op0=OP.mult))
            ve_m = kb.mark(VE)
            kb.wait(TE, ve_m)
        last_cast[par] = kb.mark(TE)
    kb.barrier()
    x2toks = {}
    x2free = {}
    for s2 in range(2):
        x2toks[s2] = kb.dma("sync", x2buf[:, s2 % 2, :],
                            dr["x2s"][128*s2:128*(s2+1), :],
                            sem=("LWO" if s2 % 2 == 0 else "LWV"))
    for s2 in range(16):
        for half in range(2):
            for jt in range(16):
                kb.emit(TE, lambda e, s2=s2, half=half, jt=jt: e.matmul(
                    out=PAV[:, 4*half:4*half+4, :].rearrange("p a b -> p (a b)"),
                    lhsT=SBT[:, jt, 128*s2:128*(s2+1)],
                    rhs=vt_ap(jt)[:, 512*half:512*(half+1)],
                    start=(jt == 0), stop=(jt == 15)))
        av_m = kb.mark(TE)
        kb.wait(VE, av_m)
        kb.wait(VE, x2toks[s2])
        for half in range(2):
            kb.emit(VE, lambda e, s2=s2, half=half: e.tensor_add(
                out=XTOK[:, 512*half:512*(half+1)],
                in0=PAV[:, 4*half:4*half+4, :].rearrange("p a b -> p (a b)"),
                in1=x2buf[:, s2 % 2, 512*half:512*(half+1)]))
        sum_m = kb.mark(VE)
        x2free[s2] = sum_m
        if s2 + 2 < 16:
            kb.wait("sync", sum_m)
            x2toks[s2 + 2] = kb.dma("sync", x2buf[:, s2 % 2, :],
                dr["x2s"][128*(s2+2):128*(s2+3), :],
                sem=("LWO" if s2 % 2 == 0 else "LWV"))
        kb.wait(TE, sum_m)
        ln3_slice(s2)
    kb.barrier()

    WSTa = carve(ARF, 0, 4096)
    WSTb = CF32[:, 0:4096]
    t_cw = kb.dma("gpsimd", convw4[:], dr["convw"], sem="GAS")
    t_cb = kb.dma("gpsimd", convb4[:], dr["convb"], sem="GAS")
    kb.wait(VE, t_cb)

    def quant_weights(dram_ap, ntiles, F, dst_fn, mcell, scell):
        asum = colsc[:, 5:6]
        kb.emit(VE, lambda e: e.memset(asum[:], 0.0))
        kb.selfsync(VE)
        toks = {}
        vem = {}
        for tt in range(min(2, ntiles)):
            buf = WSTa if tt % 2 == 0 else WSTb
            toks[tt] = kb.dma("sync", buf[:, 0:F], dram_ap[128*tt:128*(tt+1), :],
                              sem=("LWQ" if tt % 2 == 0 else "LWK"))
        for tt in range(ntiles):
            buf = WSTa if tt % 2 == 0 else WSTb
            kb.wait(VE, toks[tt])
            kb.selfsync(VE)
            kb.emit(VE, lambda e, buf=buf, F=F: e.reduce_sum(out=colsc[:, 0:1],
                in_=buf[:, 0:F], axis=AX, apply_absolute_value=True))
            kb.selfsync(VE)
            kb.emit(VE, lambda e: e.tensor_add(out=asum[:], in0=asum[:],
                                               in1=colsc[:, 0:1]))
            vem[tt] = kb.mark(VE)
            if tt + 2 < ntiles:
                kb.wait("sync", vem[tt])
                buf2 = WSTa if tt % 2 == 0 else WSTb
                toks[tt + 2] = kb.dma("sync", buf2[:, 0:F],
                    dram_ap[128*(tt+2):128*(tt+3), :],
                    sem=("LWQ" if tt % 2 == 0 else "LWK"))
        kb.selfsync(VE)
        ve_a = kb.mark(VE)
        kb.wait(TE, ve_a)
        kb.emit(TE, lambda e: e.matmul(out=PAV[0:1, 0, 0:1], lhsT=onescol[:],
                                       rhs=asum[:], start=True, stop=True))
        m1 = kb.mark(TE)
        kb.wait(VE, m1)
        N = ntiles * 128 * F
        kb.emit(VE, lambda e, N=N: e.tensor_scalar(out=colsc[0:1, 0:1],
            in0=PAV[0:1, 0, 0:1], scalar1=1.0/N, scalar2=1e-5,
            op0=OP.mult, op1=OP.max))
        kb.selfsync(VE)
        kb.emit(VE, lambda e: e.reciprocal(out=colsc[0:1, 1:2],
                                           in_=colsc[0:1, 0:1]))
        kb.selfsync(VE)
        ve_c = kb.mark(VE)
        kb.wait(TE, ve_c)
        kb.emit(TE, lambda e: e.matmul(out=PAV[:, 1, 0:2], lhsT=ones128[:],
                                       rhs=colsc[0:1, 0:2], start=True, stop=True))
        m2 = kb.mark(TE)
        kb.wait(VE, m2)
        kb.emit(VE, lambda e, mc=mcell: e.tensor_copy(out=C4[:, mc:mc+1],
                                                      in_=PAV[:, 1, 0:1]))
        kb.emit(VE, lambda e, sc=scell: e.tensor_copy(out=C4[:, sc:sc+1],
                                                      in_=PAV[:, 1, 1:2]))
        kb.selfsync(VE)
        kb.wait("sync", ve_a)
        toks = {}
        for tt in range(min(2, ntiles)):
            buf = WSTa if tt % 2 == 0 else WSTb
            toks[tt] = kb.dma("sync", buf[:, 0:F], dram_ap[128*tt:128*(tt+1), :],
                              sem=("LWQ" if tt % 2 == 0 else "LWK"))
        for tt in range(ntiles):
            buf = WSTa if tt % 2 == 0 else WSTb
            kb.wait(VE, toks[tt])
            kb.emit(VE, lambda e, tt=tt, buf=buf, F=F, sc=scell: e._custom_dve(
                RNDQ, out=dst_fn(tt), in0=buf[:, 0:F],
                s0=C4[:, sc:sc+1], imm2=MAGIC))
            kb.selfsync(VE)
            kb.emit(VE, lambda e, tt=tt, F=F: e.tensor_scalar(
                out=dst_fn(tt), in0=dst_fn(tt), scalar1=1.0, scalar2=-1.0,
                op0=OP.min, op1=OP.max))
            vq = kb.mark(VE)
            if tt + 2 < ntiles:
                kb.wait("sync", vq)
                buf2 = WSTa if tt % 2 == 0 else WSTb
                toks[tt + 2] = kb.dma("sync", buf2[:, 0:F],
                    dram_ap[128*(tt+2):128*(tt+3), :],
                    sem=("LWQ" if tt % 2 == 0 else "LWK"))
        return kb.mark(VE)

    if DUMP == "preq":
        return dump([(0, SXC)])
    quant_weights(dr["w1T"], 8, 4096, lambda tt: w1q[:, tt, :], 4, 5)
    kb.barrier()
    if DUMP == "wq1":
        return dump([(128*t2, w1q[:, t2, :]) for t2 in range(8)])
    quant_weights(dr["w2T"], 32, 1024, lambda tt: w2q[:, tt, :], 0, 1)
    kb.barrier()
    if DUMP == "wq2":
        return dump([(0, C4)])

    sq4_free = [None]

    def out_section(o_lo, OL, ioff, t_r, src_lo=0, nrows=None):
        if nrows is None:
            nrows = OL
        for dto in range(8):
            ps = dto % 4
            for ct in range(32):
                kb.emit(TE, lambda e, dto=dto, ct=ct, ps=ps, OL=OL, ioff=ioff:
                        e.matmul(
                    out=PAV[:, 2*ps:2*ps+1, :]
                        .rearrange("p a b -> p (a b)")[:, 0:OL],
                    lhsT=w2q[:, ct, 128*dto:128*(dto+1)],
                    rhs=INTH[:, ct, ioff:ioff+OL],
                    start=(ct == 0), stop=(ct == 31)))
            tm_o = kb.mark(TE)
            kb.wait(VE, tm_o)
            kb.emit(VE, lambda e, ps=ps, OL=OL: e.tensor_copy(
                out=PADB[:, 0:OL],
                in_=PAV[:, 2*ps:2*ps+1, :].rearrange("p a b -> p (a b)")[:, 0:OL]))
            ve_o = kb.mark(VE)
            use_bank(2)
            kb.wait(TE, ve_o)
            kb.emit(TE, lambda e, OL=OL: e.transpose(
                out=C[2][0:OL, 0:128], in_=PADB[:, 0:OL], identity=idf[:]))
            tm_t = kb.mark(TE)
            kb.wait(VE, tm_t)
            if t_r is not None:
                kb.wait(VE, t_r)
            kb.emit(VE, lambda e, dto=dto, OL=OL: e.scalar_tensor_tensor(
                out=XT4[0:OL, 128*dto:128*(dto+1)], in0=C[2][0:OL, 0:128],
                scalar=colsc[0:OL, 2:3], in1=XT4[0:OL, 128*dto:128*(dto+1)],
                op0=OP.mult, op1=OP.add))
            bankfree[2] = kb.mark(VE)
        kb.selfsync(VE)
        kb.emit(VE, lambda e, OL=OL: e.reduce_sum(out=colsc[0:OL, 0:1],
            in_=XT4[0:OL, :], axis=AX))
        kb.selfsync(VE)
        kb.emit(VE, lambda e, OL=OL: e.tensor_scalar(out=colsc[0:OL, 1:2],
            in0=colsc[0:OL, 0:1],
            scalar1=1.0/1024.0, scalar2=None, op0=OP.mult))
        kb.selfsync(VE)
        kb.emit(VE, lambda e, OL=OL: e.tensor_scalar(out=XC4[0:OL, :],
            in0=XT4[0:OL, :], scalar1=colsc[0:OL, 1:2], scalar2=None,
            op0=OP.subtract))
        ve_c4 = kb.mark(VE)
        kb.wait(SE, ve_c4)
        kb.emit(SE, lambda e, OL=OL: e.activation(out=SQ4[0:OL, :],
            in_=XC4[0:OL, :], func=AF.Square))
        se_s4 = kb.mark(SE)
        kb.wait(VE, se_s4)
        kb.emit(VE, lambda e, OL=OL: e.reduce_sum(out=colsc[0:OL, 2:3],
            in_=SQ4[0:OL, :], axis=AX))
        sq4_free[0] = kb.mark(VE)
        kb.selfsync(VE)
        kb.emit(VE, lambda e, OL=OL: e.tensor_scalar(out=colsc[0:OL, 2:3],
            in0=colsc[0:OL, 2:3],
            scalar1=1.0/1024.0, scalar2=1e-5, op0=OP.mult, op1=OP.add))
        ve_v4 = kb.mark(VE)
        kb.wait(SE, ve_v4)
        kb.emit(SE, lambda e, OL=OL: e.activation(out=colsc[0:OL, 3:4],
            in_=colsc[0:OL, 2:3], func=AF.Ln))
        kb.selfsync(SE)
        kb.emit(SE, lambda e, OL=OL: e.activation(out=colsc[0:OL, 3:4],
            in_=colsc[0:OL, 3:4], func=AF.Exp, scale=-0.5))
        se_r4 = kb.mark(SE)
        kb.wait(VE, se_r4)
        kb.emit(VE, lambda e, OL=OL: e.tensor_scalar(out=XC4[0:OL, :],
            in0=XC4[0:OL, :], scalar1=colsc[0:OL, 3:4], scalar2=None,
            op0=OP.mult))
        ve_o4 = kb.mark(VE)
        kb.wait("gpsimd", ve_o4)
        tko = kb.dma("gpsimd", dr["out"][o_lo:o_lo+nrows, :],
                     XC4[src_lo:src_lo+nrows, :], sem="GOUT")
        kb.wait(VE, tko)
        return tko

    def sh_section(moff, OL):
        use_bank(0)
        kb.emit(TE, lambda e, moff=moff, OL=OL: e.transpose(
            out=C[0][0:OL, 0:128], in_=M1B[:, moff:moff+OL], identity=idf[:]))
        tm_m = kb.mark(TE)
        kb.wait(VE, tm_m)
        kb.emit(VE, lambda e, OL=OL: e.reduce_max(out=colsc[0:OL, 0:1],
            in_=C[0][0:OL, 0:128], axis=AX))
        bankfree[0] = kb.mark(VE)
        kb.selfsync(VE)
        kb.emit(VE, lambda e, OL=OL: e.tensor_scalar_max(out=colsc[0:OL, 0:1],
            in0=colsc[0:OL, 0:1], scalar1=1e-5))
        kb.selfsync(VE)
        kb.emit(VE, lambda e, OL=OL: e.reciprocal(out=colsc[0:OL, 1:2],
            in_=colsc[0:OL, 0:1]))
        kb.selfsync(VE)
        kb.emit(VE, lambda e, OL=OL: e.tensor_scalar(out=colsc[0:OL, 1:2],
            in0=colsc[0:OL, 1:2], scalar1=127.0, scalar2=None, op0=OP.mult))
        kb.emit(VE, lambda e, OL=OL: e.tensor_scalar(out=colsc[0:OL, 2:3],
            in0=colsc[0:OL, 0:1], scalar1=1.0/127.0, scalar2=None, op0=OP.mult))
        kb.selfsync(VE)
        kb.emit(VE, lambda e, OL=OL: e.tensor_mul(out=colsc[0:OL, 2:3],
            in0=colsc[0:OL, 2:3], in1=C4[0:OL, 0:1]))
        kb.selfsync(VE)
        return kb.mark(VE)

    def shb_tile(OL, src=None):
        if src is None:
            src = colsc[0:OL, 1:2]
        kb.selfsync(VE)
        use_bank(1)
        ve_sh = kb.mark(VE)
        kb.wait(TE, ve_sh)
        kb.emit(TE, lambda e, OL=OL, src=src: e.transpose(
            out=C[1][0:1, 0:OL], in_=src, identity=idf[0:OL, 0:OL]))
        tm_s = kb.mark(TE)
        kb.wait(VE, tm_s)
        kb.emit(VE, lambda e, OL=OL: e.tensor_copy(out=SHB[0:1, 0:OL],
                                                   in_=C[1][0:1, 0:OL]))
        bankfree[1] = kb.mark(VE)
        kb.selfsync(VE)
        ve_r = kb.mark(VE)
        use_bank(1)
        kb.wait(TE, ve_r)
        kb.emit(TE, lambda e, OL=OL: e.matmul(out=C[1][:, 0:OL], lhsT=ones128[:],
            rhs=SHB[0:1, 0:OL], start=True, stop=True))
        tm_b = kb.mark(TE)
        kb.wait(VE, tm_b)
        kb.emit(VE, lambda e, OL=OL: e.tensor_copy(out=SHB[:, 0:OL],
                                                   in_=C[1][:, 0:OL]))
        bankfree[1] = kb.mark(VE)
        kb.selfsync(VE)
        return kb.mark(VE)

    kb.emit(VE, lambda e: e.memset(HBUF[:, :, 0:2], 0.0))
    ve_h0 = kb.mark(VE)
    kb.wait(VE, ve_h0)
    kb.wait(TE, ve_h0)
    last_out = None
    for c in range(16):
        h_lo = 128 * c
        if sq4_free[0] is not None:
            kb.wait("sync", sq4_free[0])
        t_x3 = kb.dma("sync", X3B[:], dr["x3s"][h_lo:h_lo+128, :], sem="LWO")
        kb.wait(VE, t_x3)
        kb.emit(VE, lambda e, c=c: e._custom_dve(RNDQ, out=XC4[:], in0=X3B[:],
            s0=SXC[:, c:c+1], imm2=MAGIC))
        ve_ix = kb.mark(VE)
        kb.wait(VE, ve_ix)
        kb.wait(TE, ve_ix)
        if DUMP == "xc40" and c == 0:
            return dump([(0, XC4)])
        if DUMP == "xc41" and c == 1:
            return dump([(0, XC4), (128, X3B)])
        if DUMP == "w1qt0" and c == 0:
            kb.emit(VE, lambda e: e.tensor_copy(out=X3B[:], in_=w1q[:, 0, 0:1024]))
            kb.selfsync(VE)
            return dump([(0, X3B)])
        for k8 in range(8):
            j = k8 % 4
            use_bank(j)
            kb.emit(TE, lambda e, k8=k8, j=j: e.transpose(
                out=C[j][:, 0:128], in_=XC4[:, 128*k8:128*(k8+1)],
                identity=idf[:]))
            tmx = kb.mark(TE)
            kb.wait(VE, tmx)
            kb.emit(VE, lambda e, k8=k8, j=j: e.tensor_copy(
                out=XTB4[:, k8, :], in_=C[j][:, 0:128]))
            bankfree[j] = kb.mark(VE)
        ve_xt = kb.mark(VE)
        kb.wait(TE, ve_xt)
        if DUMP == "xtb40" and c == 0:
            kb.emit(VE, lambda e: e.tensor_copy(
                out=X3B[:], in_=XTB4.rearrange("p a b -> p (a b)")))
            kb.selfsync(VE)
            return dump([(0, X3B)])
        kb.emit(VE, lambda e: e.memset(M1B[:], 0.0))
        kb.emit(VE, lambda e, c=c: e.reciprocal(out=colsc[:, 4:5],
                                                in_=SXC[:, c:c+1]))
        kb.selfsync(VE)
        kb.emit(VE, lambda e: e.tensor_mul(out=colsc[:, 4:5], in0=colsc[:, 4:5],
                                           in1=C4[:, 4:5]))
        kb.selfsync(VE)
        shb_tile(128, colsc[0:128, 4:5])
        ve_hc = kb.mark(VE)
        NC = 129 if c == 15 else 128
        co = 1 if c == 0 else 0
        for ct in range(32):
            ps = ct % 8
            for dt in range(8):
                kb.emit(TE, lambda e, ct=ct, dt=dt, ps=ps: e.matmul(
                    out=PAV[:, 2*ps:2*ps+1, :].rearrange("p a b -> p (a b)"),
                    lhsT=w1q[:, dt, 128*ct:128*(ct+1)], rhs=XTB4[:, dt, :],
                    start=(dt == 0), stop=(dt == 7)))
            tm_h = kb.mark(TE)
            kb.wait(VE, tm_h)
            if ct == 0:
                kb.wait(VE, ve_hc)
            kb.emit(VE, lambda e, ct=ct, ps=ps: e.tensor_tensor(
                out=HBUF[:, ct, 2:130],
                in0=PAV[:, 2*ps:2*ps+1, :].rearrange("p a b -> p (a b)"),
                in1=SHB[:, 0:128], op=OP.mult))
            hv = kb.mark(VE)
            kb.wait(TE, hv)
            kb.selfsync(VE)
            kb.emit(VE, lambda e, ct=ct: e.tensor_scalar(
                out=CONVO[:, 0:128], in0=HBUF[:, ct, 0:128],
                scalar1=convw4[:, ct, 0:1], scalar2=None, op0=OP.mult))
            kb.selfsync(VE)
            kb.emit(VE, lambda e, ct=ct: e.scalar_tensor_tensor(
                out=CONVO[:, 0:128], in0=HBUF[:, ct, 1:129],
                scalar=convw4[:, ct, 1:2], in1=CONVO[:, 0:128],
                op0=OP.mult, op1=OP.add))
            kb.selfsync(VE)
            kb.emit(VE, lambda e, ct=ct: e.scalar_tensor_tensor(
                out=CONVO[:, 0:128], in0=HBUF[:, ct, 2:130],
                scalar=convw4[:, ct, 2:3], in1=CONVO[:, 0:128],
                op0=OP.mult, op1=OP.add))
            kb.selfsync(VE)
            kb.emit(VE, lambda e, ct=ct: e.tensor_scalar(
                out=CONVO[:, 0:128], in0=CONVO[:, 0:128],
                scalar1=convb4[:, ct:ct+1], scalar2=None, op0=OP.add))
            if c == 15:
                kb.emit(VE, lambda e, ct=ct: e.tensor_scalar(
                    out=CONVO[:, 128:129], in0=HBUF[:, ct, 128:129],
                    scalar1=convw4[:, ct, 0:1], scalar2=None, op0=OP.mult))
                kb.selfsync(VE)
                kb.emit(VE, lambda e, ct=ct: e.scalar_tensor_tensor(
                    out=CONVO[:, 128:129], in0=HBUF[:, ct, 129:130],
                    scalar=convw4[:, ct, 1:2], in1=CONVO[:, 128:129],
                    op0=OP.mult, op1=OP.add))
                kb.selfsync(VE)
                kb.emit(VE, lambda e, ct=ct: e.tensor_scalar(
                    out=CONVO[:, 128:129], in0=CONVO[:, 128:129],
                    scalar1=convb4[:, ct:ct+1], scalar2=None, op0=OP.add))
            cm = kb.mark(VE)
            if c < 15:
                kb.emit(VE, lambda e, ct=ct: e.tensor_copy(
                    out=NXH[:, ct, :], in_=HBUF[:, ct, 128:130]))
            kb.wait(SE, cm)
            kb.emit(SE, lambda e, ct=ct, NC=NC: e.activation(
                out=HBUF[:, ct, 0:NC], in_=CONVO[:, 0:NC], func=AF.Silu))
            sm = kb.mark(SE)
            kb.wait(VE, sm)
            kb.emit(VE, lambda e, ct=ct, NC=NC: e.tensor_tensor(
                out=M1B[:, 0:NC], in0=M1B[:, 0:NC], in1=HBUF[:, ct, 0:NC],
                op=OP.max))
            kb.selfsync(VE)
            kb.emit(VE, lambda e, ct=ct, NC=NC: e.scalar_tensor_tensor(
                out=M1B[:, 0:NC], in0=HBUF[:, ct, 0:NC], scalar=-1.0,
                in1=M1B[:, 0:NC], op0=OP.mult, op1=OP.max))
            kb.selfsync(VE)
        mm_m = kb.mark(VE)
        kb.wait(TE, mm_m)
        if DUMP == "hck0" and c == 0:
            return dump([(0, HBUF[:, 0:7, :].rearrange("p a b -> p (a b)"))])
        if DUMP == "hck1" and c == 1:
            return dump([(0, HBUF[:, 0:7, :].rearrange("p a b -> p (a b)"))])
        def do_out(o_lo, moff, r_lo, n_out):
            OL = 128
            sh_section(moff, OL)
            if DUMP == "sh0" and c == 0:
                kb.barrier()
                return dump([(0, colsc)])
            shb_tile(OL)
            if DUMP == "shb0" and c == 0:
                kb.barrier()
                return dump([(0, SHB)])
            for ct in range(32):
                kb.emit(VE, lambda e, ct=ct, moff=moff, OL=OL: e._custom_dve(
                    RNDM, out=INTH[:, ct, moff:moff+OL],
                    in0=HBUF[:, ct, moff:moff+OL], in1=SHB[:, 0:OL], imm2=MAGIC))
            ve_ih = kb.mark(VE)
            kb.wait(TE, ve_ih)
            kb.wait("sync", ve_ih)
            if DUMP == "rndm0" and c == 0:
                kb.barrier()
                return dump([(0, SHB)])
            t_r = kb.dma("sync", XT4[:], dr["x3s"][r_lo:r_lo+128, :], sem="LWV")
            return out_section(o_lo, OL, moff, t_r, src_lo=0, nrows=n_out)
        if c == 0:
            r = do_out(0, 1, 0, 127)
            if DUMP in ("sh0", "shb0", "rndm0"):
                return r
            last_out = r
            if DUMP == "oc0":
                kb.barrier()
                return dump([(0, XT4)])
        elif c < 15:
            last_out = do_out(128*c - 1, 0, 128*c - 1, 128)
        else:
            last_out = do_out(128*15 - 1, 0, 128*15 - 1, 128)
            last_out = do_out(1920, 1, 1920, 128)
        if c < 15:
            kb.selfsync(VE)
            kb.emit(VE, lambda e: e.tensor_copy(
                out=HBUF[:, :, 0:2], in_=NXH[:]))
            ve_halo = kb.mark(VE)
            kb.wait(VE, ve_halo)
            kb.wait(TE, ve_halo)
    kb.wait("gpsimd", last_out)
    kb.barrier()

    if DBG_STAGE == 3:
        for t2 in range(16):
            tok = kb.dma("sync", XTOK[:], dr["x3s"][128*t2:128*(t2+1), :],
                         sem="LWO")
            kb.wait("gpsimd", tok)
            tok2 = kb.dma("gpsimd", dr["out"][128*t2:128*(t2+1), :], XTOK[:],
                          sem="GOUT")
            kb.wait("sync", tok2)
        kb.wait("gpsimd", ("GOUT", kb.cnt["GOUT"]))

    return finish_emit(nc, es, kb)

def kernel(**inputs):
    in_maps = host_prep(inputs)
    nc = build_nc()
    mybir.codegen_inst_isa_subclasses(nc)
    res = run_bass_kernel_spmd(nc, in_maps, core_ids=list(range(NCORES)))
    outs = [np.asarray(res.results[i]["out"]) for i in range(NCORES)]
    if DBG_STAGE in (1, 2):
        return np.stack([o.T for o in outs], 0)
    return np.stack(outs, 0)


# version 04
# speedup vs baseline: 1.0424x; 1.0424x over previous
import math
import numpy as np
import ml_dtypes

import concourse.bass as bass
import concourse.mybir as mybir
from concourse.bass_utils import run_bass_kernel_spmd
import concourse.dve_ops as dve_ops
from concourse.dve_op import DveOp
from concourse.dve_spec import Spec, Src0, Src1, C0, C1, C2, One, lower, maxx, minn

F32 = mybir.dt.float32
BF16 = mybir.dt.bfloat16
AX = mybir.AxisListType.X
OP = mybir.AluOpType
AF = mybir.ActivationFunctionType

MAGIC = 12582912.0
NCORES = 8
S = 2048

def _register(name, spec, reference):
    for o in dve_ops.OPS:
        if o.name == name:
            return o
    ver = "v3"
    lowered = lower(spec, ver=ver, perf_en=False, name=name)
    op = DveOp(name, spec, subdim=False, uops_sha={ver: lowered.sha(ver)},
               reference=reference)
    dve_ops.OPS.append(op)
    dve_ops._SUB_OPCODE_FOR_NAME[name] = (
        dve_ops._CUSTOM_DVE_ROW_BASE + len(dve_ops.OPS) - 1)
    dve_ops.CUSTOM_DVE_SPECS[name] = spec
    dve_ops.CUSTOM_DVE_REFS[name] = reference
    return op

RNDQ = _register("ANT_RNDQ", Spec(body=(Src0 * C0 + C2) - C2),
                 lambda in0, in1, s0, s1, imm2: np.round(in0 * s0))
RNDM = _register("ANT_RNDM", Spec(body=(Src0 * Src1 + C2) - C2),
                 lambda in0, in1, s0, s1, imm2: np.round(in0 * in1))

ROPED = [h for h in range(16) if (h % 4) in (2, 3)]

def _head_perm64():
    order = []
    for half in range(2):
        base = 32 * half
        order += [base + 2 * f for f in range(16)]
        order += [base + 2 * f + 1 for f in range(16)]
    return order

def _tile_lhw(inputs, kvw):
    f32 = np.float32
    wql = np.asarray(inputs["L0_ffn_wq_light"], f32)
    wkvl = np.asarray(inputs["L0_ffn_wkv_light"], f32)
    wqh = np.asarray(inputs["L0_ffn_wq_heavy"], f32)
    wkh = kvw[:, :64].reshape(512, 1024).T
    wvh = kvw[:, 64:].reshape(512, 1024).T
    cols = np.concatenate([wql, wkvl[:, :512], wkvl[:, 512:],
                           wqh, wkh, wvh], axis=1)
    return np.ascontiguousarray(cols.reshape(8, 128, 24, 128).transpose(2, 0, 1, 3))

def host_prep(inputs):
    f32 = np.float32
    x = np.asarray(inputs["L0_attn_x"], f32)
    wqkv = np.asarray(inputs["L0_attn_w_qkv"], f32).copy()
    perm = _head_perm64()
    for h in ROPED:
        for sec in (0, 1024):
            base = sec + h * 64
            wqkv[:, base:base + 64] = wqkv[:, [base + p for p in perm]]
    qkvw_t = np.ascontiguousarray(
        wqkv.reshape(8, 128, 24, 128).transpose(2, 0, 1, 3))
    wo = np.asarray(inputs["L0_attn_w_o"], f32)
    lxi = np.asarray(inputs["L0_attn_log_xi"], f32).reshape(1, 16)

    import jax
    import jax.numpy as jnp
    with jax.default_device(jax.devices('cpu')[0]):
        pos_j = jnp.arange(S, dtype=jnp.float32)
        inv_freq_j = 10000.0 ** (-(jnp.arange(0, 64, 2, dtype=jnp.float32)
                                   / 64))
        theta_j = pos_j[:, None] * inv_freq_j[None, :]
        cos_t = np.asarray(jnp.cos(theta_j))
        sin_t = np.asarray(jnp.sin(theta_j))
    rows_c = np.zeros((128, S), f32)
    rows_s = np.zeros((128, S), f32)
    for r in range(128):
        rr = r % 64
        fr = 16 * (rr // 32) + (rr % 16)
        sg = np.float32(-1.0 if (rr % 32) < 16 else 1.0)
        rows_c[r] = cos_t[:, fr]
        rows_s[r] = sin_t[:, fr] * sg
    cos_hi, sin_hi = rows_c, rows_s
    cos_lo = np.zeros_like(cos_hi)
    sin_lo = np.zeros_like(sin_hi)

    JJ = np.zeros((128, 16), f32)
    for t in range(16):
        JJ[:, t] = 128 * t + np.arange(128)
    ident = np.eye(128, dtype=f32)
    identb = np.eye(128, dtype=ml_dtypes.bfloat16)
    triu = np.triu(np.ones((128, 128))).astype(ml_dtypes.bfloat16)

    gl = np.ascontiguousarray(
        np.asarray(inputs["L0_ffn_gamma_light"], f32).reshape(8, 128).T)
    gh = np.ascontiguousarray(
        np.asarray(inputs["L0_ffn_gamma_heavy"], f32).reshape(8, 128).T)
    qrt = np.ascontiguousarray(
        np.asarray(inputs["L0_ffn_q_routing_token"], f32).reshape(8, 128).T)
    kvw = np.asarray(inputs["L0_ffn_wkv_heavy"], f32).reshape(8, 128, 1024)
    shared = dict(
        qkvw_t=qkvw_t, wo=wo, lxi=lxi,
        cos_hi=cos_hi, cos_lo=cos_lo, sin_hi=sin_hi, sin_lo=sin_lo,
        JJ=JJ, ident=ident, identb=identb, triu=triu,
        gl=gl, gh=gh, qrt=qrt,
        lhw_t=_tile_lhw(inputs, kvw),
        wol_t=np.ascontiguousarray(
            np.asarray(inputs["L0_ffn_wo_light"], f32).reshape(4, 128, 1024)),
        woh_t=np.ascontiguousarray(
            np.asarray(inputs["L0_ffn_wo_heavy"], f32).reshape(4, 128, 1024)),
        W1T_t=np.ascontiguousarray(np.asarray(inputs["L1_attn_W"], f32).T
            .reshape(8, 128, 24, 128).transpose(2, 0, 1, 3)),
        bvrow=np.ascontiguousarray(
            np.asarray(inputs["L1_attn_b"], f32)[None, 2048:]),
        b24=np.ascontiguousarray(
            np.asarray(inputs["L1_attn_b"], f32).reshape(24, 128).T),
        w1T=np.ascontiguousarray(np.asarray(inputs["L1_ffn_w1"], f32).T),
        w2T=np.ascontiguousarray(np.asarray(inputs["L1_ffn_w2"], f32).T),
        convw=np.ascontiguousarray(
            np.asarray(inputs["L1_ffn_conv_w"], f32)[:, 0, :]
            .reshape(32, 128, 3).transpose(1, 0, 2)),
        convb=np.ascontiguousarray(
            np.asarray(inputs["L1_ffn_conv_b"], f32).reshape(32, 128).T),
    )
    in_maps = []
    for b in range(NCORES):
        m = dict(shared)
        m["xT"] = np.ascontiguousarray(x[b].T)
        in_maps.append(m)
    return in_maps

ENGINES = ("sync", "tensor", "vector", "scalar", "gpsimd")
ESEM = {"sync": "LDX", "gpsimd": "GPM", "tensor": "TE", "vector": "VE",
        "scalar": "SE"}
SEM_NAMES = ("LDX", "LWQ", "LWK", "LWV", "LWO", "GSM", "GAS", "GOUT", "GPM", "TE", "VE", "SE")

class KB:
    def __init__(self):
        self.ops = {e: [] for e in ENGINES}
        self.cnt = {s: 0 for s in SEM_NAMES}
        self.sems = {}

    def emit(self, eng, fn):
        self.ops[eng].append([fn, None])

    def dma(self, q, dst, src, sem=None, slow=False):
        if sem is None:
            sem = "LWQ" if q == "sync" else "GAS"
        self.cnt[sem] += 16
        tgt = self.cnt[sem]
        self.ops[q].append([
            lambda e, d=dst, s=src, sl=slow: e.dma_start(
                out=d, in_=s, allow_slow_non_contiguous=sl), (sem, 16)])
        return (sem, tgt)

    def wait(self, eng, tok):
        sem, v = tok
        def f(e, s=sem, vv=v):
            return e.wait_ge(self.sems[s], vv)
        f._is_wait = True
        self.emit(eng, f)

    def mark(self, eng):
        sem = ESEM[eng]
        step = 16 if sem not in ("TE", "VE", "SE", "GPM") else 1
        lst = self.ops[eng]
        i = len(lst) - 1
        while i >= 0 and self._is_wait(lst[i][0]):
            i -= 1
        if i >= 0 and lst[i][1] is not None and lst[i][1][0] == sem:
            return (sem, self.cnt[sem])
        self.cnt[sem] += step
        tgt = self.cnt[sem]
        if lst and lst[-1][1] is None and not self._is_wait(lst[-1][0]):
            lst[-1][1] = (sem, step)
        else:
            self.ops[eng].append([lambda e: e.nop(nofuse=True), (sem, step)])
        return (sem, tgt)

    @staticmethod
    def _is_wait(fn):
        return getattr(fn, "_is_wait", False)

    def selfsync(self, eng):
        tok = self.mark(eng)
        self.wait(eng, tok)

    def barrier(self):
        toks = [self.mark(e) for e in ("tensor", "vector", "scalar")]
        for s in ("LDX", "LWQ", "LWK", "LWV", "LWO", "GSM", "GAS", "GOUT"):
            toks.append((s, self.cnt[s]))
        for e in ENGINES:
            for t in toks:
                if t[1] > 0:
                    self.wait(e, t)

    def replay(self, eng_name, eng):
        for fn, inc in self.ops[eng_name]:
            inst = fn(eng)
            if inc is not None:
                inst.then_inc(self.sems[inc[0]], inc[1])

def finish_emit(nc, es, kb):
    with es:
        with nc.Block() as block:
            @block.sync
            def _(eng):
                kb.replay("sync", eng)
            @block.tensor
            def _(eng):
                kb.replay("tensor", eng)
            @block.vector
            def _(eng):
                kb.replay("vector", eng)
            @block.scalar
            def _(eng):
                kb.replay("scalar", eng)
            @block.gpsimd
            def _(eng):
                kb.replay("gpsimd", eng)
    return nc

DUMP = None
SKIP1 = False
DBG_STAGE = 4

def build_nc():
    nc = bass.Bass("TRN2", target_bir_lowering=False)
    dr = {}
    def dram(name, shape, dt=F32, kind="ExternalInput"):
        dr[name] = nc.dram_tensor(name, shape, dt, kind=kind).ap()
        return dr[name]
    dram("xT", [1024, 2048])
    dram("qkvw_t", [24, 8, 128, 128])
    dram("wo", [1024, 1024])
    dram("lxi", [1, 16])
    for n in ("cos_hi", "cos_lo", "sin_hi", "sin_lo"):
        dram(n, [128, 2048])
    dram("JJ", [128, 16])
    dram("ident", [128, 128]); dram("identb", [128, 128], BF16)
    dram("triu", [128, 128], BF16)
    dram("gl", [128, 8]); dram("gh", [128, 8]); dram("qrt", [128, 8])
    dram("lhw_t", [24, 8, 128, 128])
    dram("wol_t", [4, 128, 1024]); dram("woh_t", [4, 128, 1024])
    dram("W1T_t", [24, 8, 128, 128]); dram("b24", [128, 24])
    dram("bvrow", [1, 1024])
    dram("w1T", [1024, 4096]); dram("w2T", [4096, 1024])
    dram("convw", [128, 32, 3]); dram("convb", [128, 32])
    dram("x3s", [2048, 1024], F32, kind="Internal")
    dram("x2s", [2048, 1024], F32, kind="Internal")
    out = dram("out", [1024, 2048] if DBG_STAGE in (1, 2) else [2048, 1024],
               F32, kind="ExternalOutput")

    from contextlib import ExitStack
    es = ExitStack()
    def sb(name, shape, dt):
        return es.enter_context(nc.sbuf_tensor(name, shape, dt))

    xT   = sb("xT_s", [128, 8, 2048], F32)
    xT_h = xT
    xB   = sb("xB", [128, 8, 2048], BF16)
    COSS = sb("COSS", [128, 2, 2048], F32)
    ARF  = sb("ARF", [128, 7744], F32)
    AB   = sb("AB", [128, 32832], BF16)
    idf  = sb("idf", [128, 128], F32)
    idb  = sb("idb", [128, 128], BF16)
    triu = sb("triu_s", [128, 128], BF16)
    qrtB = sb("qrtB", [128, 8], BF16)

    def carve(arena, off, n, shape=None):
        ap = arena[:, off:off + n]
        if shape is not None:
            import string
            letters = string.ascii_lowercase
            dims = " ".join(letters[i] for i in range(len(shape)))
            kw = {letters[i]: shape[i] for i in range(len(shape))}
            ap = ap.rearrange(f"p ({dims}) -> p {dims}", **kw)
        return ap

    cosT = COSS[:, 0, :]
    sinT = COSS[:, 1, :]
    o = 0
    stg  = carve(ARF, o, 2048); o += 2048
    stg2 = carve(ARF, o, 2048); o += 2048
    wqf  = carve(ARF, o, 2048, (2, 8, 128)); o += 2048
    RAW  = carve(ARF, o, 1056, (16, 66)); o += 1056
    RDEN = carve(ARF, o, 16); o += 16
    JJ   = carve(ARF, o, 16); o += 16
    ivb  = carve(ARF, o, 16); o += 16
    biasH = carve(ARF, o, 128, (8, 16)); o += 128
    colsc = carve(ARF, o, 6); o += 6
    lxi  = carve(ARF, o, 16)[0:1, :]; o += 16
    ivr  = carve(ARF, o, 16)[0:1, :]; o += 16
    nivr = carve(ARF, o, 16)[0:1, :]; o += 16
    ones128 = carve(ARF, o, 128)[0:1, :]; o += 128
    onescol = carve(ARF, o, 2)[:, 0:1]; o += 2
    rcol = carve(ARF, o, 16); o += 16
    rcells = carve(ARF, o, 8); o += 8
    gl = carve(ARF, o, 8); o += 8
    gh = carve(ARF, o, 8); o += 8
    qrt = carve(ARF, o, 8); o += 8
    PADB = carve(ARF, o, 128); o += 128
    cc0 = carve(ARF, o, 2); o += 2
    cc1 = carve(ARF, o, 2); o += 2
    ccm32 = carve(ARF, o, 2); o += 2
    cln32 = carve(ARF, o, 2); o += 2
    assert o <= 7744, o
    nc.const_aps.register(F32, 0.0, cc0[:, 0:1])
    nc.const_aps.register(F32, 1.0, cc1[:, 0:1])
    nc.const_aps.register(F32, -32.0, ccm32[:, 0:1])
    nc.const_aps.register(F32, 3.4657359027997265, cln32[:, 0:1])
    SUMT = carve(ARF, 0, 1024, (8, 128))
    XTOK = carve(ARF, 1024, 1024)
    XC   = carve(ARF, 2048, 1024)
    SQ   = carve(ARF, 3072, 1024)
    WOF  = carve(ARF, 4096, 1024)
    o = 0
    wqb  = carve(AB, o, 2048, (2, 8, 128)); o += 2048
    SBQ  = carve(AB, o, 2048); o += 2048
    SBK  = carve(AB, o, 2048); o += 2048
    QXc  = carve(AB, o, 2048); o += 2048
    KXc  = carve(AB, o, 2048); o += 2048
    PR   = carve(AB, o, 4096, (2, 2048)); o += 4096
    VA   = carve(AB, o, 2112, (16, 2, 66)); o += 2112
    ATTF = carve(AB, o, 16384, (8, 2048)); o += 16384
    assert o <= 32832, o
    QX = QXc[0:65, :]
    KX = KXc[0:65, :]
    woB = carve(AB, 0, 8192, (8, 1024))
    wolB = carve(AB, 2048, 4096, (4, 1024))
    ATT = carve(AB, 8192, 1024, (8, 128))

    def sbuf_alias(name, shape, dtype, base_handle, off_bytes):
        h = nc.virtual_sbuf_tensor(name, shape, dtype)
        nc._tensor(h.name, h.shape, h.dtype, type="SB")
        mloc = nc.lookup_mloc(h)
        base = nc.lookup_mloc(base_handle).addr
        mloc.allocated = not nc.target_bir_lowering
        mloc.addr = base + off_bytes
        return h

    SBT = sbuf_alias("SBT", [128, 16, 2048], BF16, xT, 0)
    CB16 = sbuf_alias("CB16", [128, 8192], BF16, COSS, 0)
    QT = carve(AB, 2048, 16384, (8, 2048))
    KT_lo = carve(AB, 18432, 14336, (7, 2048))
    KT_hi = CB16[:, 0:2048]
    VT_lo = carve(AB, 18432, 14336, (14, 1024))
    VT_hi = CB16[:, 2048:4096].rearrange("p (a b) -> p a b", a=2, b=1024)

    AF16 = sbuf_alias("AF16", [128, 6144], BF16, ARF, 4112 * 4)
    HBUF = carve(ARF, 0, 4160, (32, 130))
    CF32 = sbuf_alias("CF32", [128, 4096], F32, COSS, 0)
    X3B = CF32[:, 0:1024]
    XT4 = CF32[:, 1024:2048]
    XC4 = CF32[:, 2048:3072]
    SQ4 = X3B
    CONVO = CF32[:, 3072:3204]
    CONVO2 = CF32[:, 3540:3672]
    M1B = CF32[:, 3204:3336]
    SHB = CF32[:, 3336:3468]
    NXH = CF32[:, 3468:3532].rearrange("p (a b) -> p a b", a=32, b=2)
    C4 = carve(ARF, 7584, 8)
    INTX = AF16[:, 0:1024]
    XTB4 = AF16[:, 0:1024].rearrange("p (a b) -> p a b", a=8, b=128)
    INTH = AF16[:, 1024:5248].rearrange("p (a b) -> p a b", a=32, b=132)
    convw4 = carve(ARF, 7248, 96, (32, 3))
    convb4 = carve(ARF, 7344, 32)
    w1q = sbuf_alias("w1q", [128, 8, 4096], BF16, xT, 0)
    w2q = carve(AB, 0, 32768, (32, 1024))

    def kt_ap(dt):
        return KT_lo[:, dt, :] if dt < 7 else KT_hi

    def vt_ap(s16):
        return VT_lo[:, s16, :] if s16 < 14 else VT_hi[:, s16 - 14, :]

    C = [es.enter_context(nc.psum_tensor(f"C{j}", [128, 512], F32))
         for j in range(4)]
    PAV = es.enter_context(nc.psum_tensor("PAV", [128, 16, 128], F32))

    kb = KB()
    for s in SEM_NAMES:
        kb.sems[s] = es.enter_context(nc.semaphore(s))
    TE, VE, SE = "tensor", "vector", "scalar"

    bankfree = [None, None, None, None]

    def use_bank(j, eng=TE):
        if bankfree[j] is not None:
            kb.wait(eng, bankfree[j])

    def dump(rows):
        kb.barrier()
        toks = []
        for off, ap in rows:
            P, F = ap.shape[0], ap.shape[-1] if len(ap.shape) == 2 else None
            if F is None:
                F = 1
                for d in ap.shape[1:]:
                    F *= d
            toks.append(kb.dma("gpsimd", dr["out"][off:off+P, 0:F], ap, sem="GOUT"))
        kb.wait("gpsimd", toks[-1])
        return finish_emit(nc, es, kb)

    t_x = [kb.dma("sync", xT[:, t, :], dr["xT"][128*t:128*(t+1), :], sem="LDX")
           for t in range(8)]
    small = []
    small.append(kb.dma("gpsimd", cosT[:], dr["cos_hi"], sem="GSM"))
    small.append(kb.dma("gpsimd", sinT[:], dr["sin_hi"], sem="GSM"))
    small.append(kb.dma("gpsimd", stg[:], dr["cos_lo"], sem="GSM"))
    small.append(kb.dma("gpsimd", stg2[:], dr["sin_lo"], sem="GSM"))
    small.append(kb.dma("gpsimd", JJ[:], dr["JJ"], sem="GSM"))
    small.append(kb.dma("gpsimd", lxi[:], dr["lxi"], sem="GSM"))
    small.append(kb.dma("gpsimd", idf[:], dr["ident"], sem="GSM"))
    small.append(kb.dma("gpsimd", idb[:], dr["identb"], sem="GSM"))
    small.append(kb.dma("gpsimd", triu[:], dr["triu"], sem="GSM"))
    small.append(kb.dma("gpsimd", gl[:], dr["gl"], sem="GSM"))
    small.append(kb.dma("gpsimd", gh[:], dr["gh"], sem="GSM"))
    small.append(kb.dma("gpsimd", qrt[:], dr["qrt"], sem="GSM"))
    kb.emit(VE, lambda e: e.memset(cc0[:], 0.0))
    kb.emit(VE, lambda e: e.memset(cc1[:], 1.0))
    kb.emit(VE, lambda e: e.memset(ccm32[:], -32.0))
    kb.emit(VE, lambda e: e.memset(cln32[:], 3.4657359027997265))
    kb.emit(VE, lambda e: e.memset(ones128[:], 1.0))
    kb.emit(VE, lambda e: e.memset(onescol[:], 1.0))
    ve_cc = kb.mark(VE)
    kb.wait(VE, small[-1])
    kb.emit(VE, lambda e: e.tensor_add(out=cosT[:], in0=cosT[:], in1=stg[:]))
    kb.emit(VE, lambda e: e.tensor_add(out=sinT[:], in0=sinT[:], in1=stg2[:]))
    kb.wait(SE, small[-1])
    kb.wait(SE, ve_cc)
    kb.emit(SE, lambda e: e.activation(out=ivr[:], in_=lxi[:], func=AF.Exp,
                                       scale=-1.0))
    se_ivr = kb.mark(SE)
    kb.wait(VE, se_ivr)
    kb.emit(VE, lambda e: e.tensor_scalar(out=nivr[:], in0=ivr[:], scalar1=-1.0,
                                          scalar2=None, op0=OP.mult))
    kb.selfsync(VE)
    kb.wait(VE, t_x[-1])
    for t in range(8):
        kb.emit(VE, lambda e, t=t: e.tensor_copy(out=xB[:, t, :], in_=xT[:, t, :]))
    ve_init = kb.mark(VE)
    kb.wait(TE, ve_init)
    kb.emit(TE, lambda e: e.matmul(out=PAV[:, 0, 0:16], lhsT=ones128[:],
                                   rhs=ivr[:], start=True, stop=True))
    te_ivb = kb.mark(TE)
    kb.wait(VE, te_ivb)
    kb.emit(VE, lambda e: e.tensor_copy(out=ivb[:], in_=PAV[:, 0, 0:16]))
    kb.selfsync(VE)
    for h in range(1, 16, 2):
        kb.emit(VE, lambda e, h=h: e.tensor_scalar(
            out=biasH[:, h // 2, :], in0=JJ[:], scalar1=ivb[:, h:h+1], scalar2=None,
            op0=OP.mult))
    kb.emit(VE, lambda e: e.memset(VA[:, :, :, 64:66], 1.0))
    kb.emit(VE, lambda e: e.tensor_copy(out=qrtB[:], in_=qrt[:]))
    kb.barrier()

    RUN1 = not SKIP1
    SCALE = 0.125
    shuffle_mask = list(range(16, 32)) + list(range(16))
    last_cast = {0: None, 1: None}
    prev_pair_qk = None
    prev_last_exp = None

    def load_pair_w(colblk, par, sem):
        if last_cast[par] is not None:
            kb.wait("sync", last_cast[par])
        return [kb.dma("sync", wqf[:, par, dt, :], dr["qkvw_t"][colblk, dt], sem=sem)
                for dt in range(8)]

    def cast_pair_w(toks, par):
        kb.wait(VE, toks[-1])
        if last_cast[par] is not None:
            kb.wait(VE, last_cast[par])
        kb.emit(VE, lambda e, par=par: e.tensor_copy(
            out=wqb[:, par, :, :], in_=wqf[:, par, :, :]))
        return kb.mark(VE)

    def proj_pair(par, cast_tok, out_stg, scale, wait_se=None):
        kb.wait(TE, cast_tok)
        marks = []
        for j in range(4):
            use_bank(j)
            for dt in range(8):
                kb.emit(TE, lambda e, j=j, dt=dt, par=par: e.matmul(
                    out=C[j][:], lhsT=wqb[:, par, dt, :],
                    rhs=xB[:, dt, 512*j:512*(j+1)],
                    start=(dt == 0), stop=(dt == 7)))
            marks.append(kb.mark(TE))
        if wait_se is not None:
            kb.wait(SE, wait_se)
        for j in range(4):
            kb.wait(SE, marks[j])
            kb.emit(SE, lambda e, j=j, o=out_stg, sc=scale: e.activation(
                out=o[:, 512*j:512*(j+1)], in_=C[j][:], func=AF.Copy, scale=sc))
            bankfree[j] = kb.mark(SE)
        return kb.mark(SE), marks[-1]

    def rope(src, dst_bf):
        kb.selfsync(VE)
        kb.emit(VE, lambda e, s=src: e.stream_shuffle(out=stg2[:], in_=s[:],
                                                      mask=shuffle_mask))
        kb.selfsync(VE)
        kb.emit(VE, lambda e, s=src: e.tensor_mul(out=s[:], in0=s[:], in1=cosT[:]))
        kb.emit(VE, lambda e: e.tensor_mul(out=stg2[:], in0=stg2[:], in1=sinT[:]))
        kb.selfsync(VE)
        kb.emit(VE, lambda e, s=src, d=dst_bf: e.tensor_add(out=d[:], in0=s[:],
                                                            in1=stg2[:]))
        return kb.mark(VE)

    prev_rope_q = None
    prev_rope_k = None
    prev_raw = None
    for p in range(8 if RUN1 else 0):
        roped = (p % 2) == 1
        h_even, h_odd = 2 * p, 2 * p + 1
        tq = load_pair_w(p, 0, "LWQ")
        tk = load_pair_w(8 + p, 1, "LWK")
        cq = cast_pair_w(tq, 0)
        ck = cast_pair_w(tk, 1)
        se_q, te_q = proj_pair(0, cq, stg, SCALE, wait_se=prev_rope_k)
        last_cast[0] = te_q
        if DUMP == "proj0" and p == 0:
            return dump([(0, stg)])
        kb.wait(VE, se_q)
        if prev_pair_qk is not None:
            kb.wait(VE, prev_pair_qk)
        if roped:
            ve_q = rope(stg, SBQ)
        else:
            kb.emit(VE, lambda e: e.tensor_copy(out=SBQ[:], in_=stg[:]))
            ve_q = kb.mark(VE)
        prev_rope_q = ve_q
        se_k, te_k = proj_pair(1, ck, stg, 1.0, wait_se=ve_q)
        last_cast[1] = te_k
        kb.wait(VE, se_k)
        if roped:
            ve_k = rope(stg, SBK)
        else:
            kb.emit(VE, lambda e: e.tensor_copy(out=SBK[:], in_=stg[:]))
            ve_k = kb.mark(VE)
        prev_rope_k = ve_k
        kb.wait("gpsimd", ve_q)
        if prev_pair_qk is not None:
            kb.wait("gpsimd", prev_pair_qk)
        g_q = kb.dma("gpsimd", QX[0:64, :], SBQ[64:128, :])
        kb.wait("gpsimd", ve_k)
        g_k = kb.dma("gpsimd", KX[0:64, :], SBK[64:128, :])
        kb.emit("gpsimd", lambda e: e.iota(
            QX[64:65, :].rearrange("p (a b) -> p a b", a=16, b=128),
            pattern=[[128, 16], [0, 128]], channel_multiplier=0,
            allow_small_or_imprecise_dtypes=True))
        g_i = kb.mark("gpsimd")
        kb.wait(VE, g_i)
        kb.wait(VE, se_ivr)
        kb.emit(VE, lambda e, h=h_odd: e.tensor_scalar(
            out=QX[64:65, :], in0=QX[64:65, :], scalar1=nivr[0:1, h:h+1],
            scalar2=-32.0, op0=OP.mult, op1=OP.add))
        kb.emit(VE, lambda e: e.memset(KX[64:65, :], 1.0))
        ve_x = kb.mark(VE)

        if DUMP == "sbqk0" and p == 0:
            return dump([(0, SBQ), (128, SBK)])
        if DUMP == "sbqk1" and p == 1:
            return dump([(0, SBQ), (128, SBK)])
        if DUMP == "qxkx1" and p == 1:
            kb.wait("gpsimd", ve_x); kb.wait("gpsimd", g_k)
            return dump([(0, QX), (128, KX)])
        tv = load_pair_w(16 + p, 0, "LWV")
        cv = cast_pair_w(tv, 0)
        kb.wait(TE, cv)
        if prev_raw is not None:
            kb.wait(TE, prev_raw)
        last_v_te = None
        for g in range(4):
            for k in range(4):
                s16 = 4 * g + k
                for dt in range(8):
                    kb.emit(TE, lambda e, s16=s16, k=k, dt=dt: e.matmul(
                        out=PAV[:, k, :], lhsT=xB[:, dt, 128*s16:128*(s16+1)],
                        rhs=wqb[:, 0, dt, :], start=(dt == 0), stop=(dt == 7)))
            m = kb.mark(TE)
            kb.wait(VE, m)
            for k in range(4):
                s16 = 4 * g + k
                kb.emit(VE, lambda e, s16=s16, k=k, h=h_even: e.tensor_copy(
                    out=VA[:, s16, 0, 0:64], in_=PAV[:, k, 0:64]))
                kb.emit(VE, lambda e, s16=s16, k=k, h=h_odd: e.tensor_copy(
                    out=VA[:, s16, 1, 0:64], in_=PAV[:, k, 64:128]))
            ve_m = kb.mark(VE)
            kb.wait(TE, ve_m)
        last_cast[0] = kb.mark(TE)
        ve_v = ve_m

        if DUMP == "va0" and p == 0:
            return dump([(0, VA[:, :, 0, :].rearrange("p a b -> p (a b)")),
                         (128, VA[:, :, 1, :].rearrange("p a b -> p (a b)"))])
        for hh in range(2):
            h = 2 * p + hh
            odd = hh == 1
            if odd:
                kb.wait(TE, g_k); kb.wait(TE, ve_x)
            else:
                kb.wait(TE, ve_q); kb.wait(TE, ve_k)
            kb.wait(TE, ve_v)
            if prev_last_exp is not None:
                kb.wait(TE, prev_last_exp)
            exp_marks = {}
            av_done = {}
            for t in range(16):
                L = 2048 - 128 * t
                nch = (L + 511) // 512
                par = t % 2
                qk_marks = []
                for j in range(nch):
                    cl = min(512, L - 512 * j)
                    use_bank(j)
                    if odd:
                        kb.emit(TE, lambda e, t=t, j=j, cl=cl: e.matmul(
                            out=C[j][:, 0:cl], lhsT=KX[0:65, 128*t:128*(t+1)],
                            rhs=QX[0:65, 128*t+512*j:128*t+512*j+cl],
                            start=True, stop=True))
                    else:
                        kb.emit(TE, lambda e, t=t, j=j, cl=cl: e.matmul(
                            out=C[j][:, 0:cl], lhsT=SBK[0:64, 128*t:128*(t+1)],
                            rhs=SBQ[0:64, 128*t+512*j:128*t+512*j+cl],
                            start=True, stop=True))
                    qk_marks.append(kb.mark(TE))
                for j in range(nch):
                    cl = min(512, L - 512 * j)
                    kb.wait(SE, qk_marks[j])
                    if t >= 2 and j == 0:
                        kb.wait(SE, av_done[t - 2])
                    if odd:
                        kb.emit(SE, lambda e, t=t, j=j, cl=cl, par=par, h=h:
                                e.activation(
                            out=PR[:, par, 128*t+512*j:128*t+512*j+cl],
                            in_=C[j][:, 0:cl], func=AF.Exp,
                            bias=biasH[:, h // 2, t:t+1]))
                    else:
                        kb.emit(SE, lambda e, t=t, j=j, cl=cl, par=par:
                                e.activation(
                            out=PR[:, par, 128*t+512*j:128*t+512*j+cl],
                            in_=C[j][:, 0:cl], func=AF.Exp, bias=-32.0))
                    exp_marks[(t, j)] = kb.mark(SE)
                    bankfree[j] = exp_marks[(t, j)]
                kb.wait(VE, exp_marks[(t, 0)])
                kb.emit(VE, lambda e, t=t, par=par: e.tensor_mul(
                    out=PR[:, par, 128*t:128*(t+1)],
                    in0=PR[:, par, 128*t:128*(t+1)], in1=triu[:]))
                mask_m = kb.mark(VE)
                for s2 in range(t, 16):
                    j = (128 * (s2 - t)) // 512
                    kb.wait(TE, exp_marks[(t, j)])
                    if s2 == t:
                        kb.wait(TE, mask_m)
                    kb.emit(TE, lambda e, t=t, s2=s2, par=par, hh=hh: e.matmul(
                        out=PAV[:, s2, 0:66], lhsT=PR[:, par, 128*s2:128*(s2+1)],
                        rhs=VA[:, t, hh, :],
                        start=(t == 0 and s2 % 4 == 0),
                        stop=(t == s2 and s2 % 4 == 3)))
                av_done[t] = kb.mark(TE)
            kb.wait(VE, av_done[15])
            for s2 in range(16):
                kb.emit(VE, lambda e, s2=s2: e.tensor_copy(
                    out=RAW[:, s2, :], in_=PAV[:, s2, 0:66]))
            raw_m = kb.mark(VE)
            prev_raw = raw_m
            kb.wait(TE, raw_m)
            kb.selfsync(VE)
            kb.emit(VE, lambda e: e.reciprocal(out=RDEN[:], in_=RAW[:, :, 64]))
            kb.selfsync(VE)
            for s2 in range(16):
                kb.emit(VE, lambda e, s2=s2: e.tensor_scalar(
                    out=RAW[:, s2, 0:64], in0=RAW[:, s2, 0:64],
                    scalar1=RDEN[:, s2:s2+1], scalar2=None, op0=OP.mult))
            norm_m = kb.mark(VE)
            kb.wait(VE, norm_m)
            kb.wait(TE, norm_m)
            prow = 64 * (h % 2)
            ptile = h // 2
            for s2 in range(16):
                k = s2 % 4
                use_bank(k)
                if odd:
                    kb.emit(VE, lambda e, s2=s2: e.tensor_copy(
                        out=PADB[:, 64:128], in_=RAW[:, s2, 0:64]))
                    ve_p = kb.mark(VE)
                    kb.wait(TE, ve_p)
                    kb.emit(TE, lambda e, k=k: e.transpose(
                        out=C[k][:, 0:128], in_=PADB[:], identity=idf[:]))
                else:
                    kb.emit(TE, lambda e, s2=s2, k=k: e.transpose(
                        out=C[k][0:64, 0:128], in_=RAW[:, s2, 0:64],
                        identity=idf[:]))
                tm2 = kb.mark(TE)
                kb.wait(VE, tm2)
                kb.emit(VE, lambda e, s2=s2, k=k, prow=prow, ptile=ptile:
                        e.tensor_copy(
                    out=ATTF[prow:prow+64, ptile, 128*s2:128*(s2+1)],
                    in_=C[k][prow:prow+64, 0:128]))
                bankfree[k] = kb.mark(VE)
            kb.wait(SE, raw_m)
            prev_last_exp = exp_marks[(15, 0)]
            if DUMP == f"raw{h}":
                return dump([(0, RAW.rearrange("p a b -> p (a b)"))])
        prev_pair_qk = av_done[15]
        if DUMP == f"pair{p}":
            return dump([(0, ATTF[:, p, :])])

    kb.barrier()

    def transpose4(src_fn, dst_fn, idmat, n=8):
        for grp in range((n + 3) // 4):
            tg = []
            for k in range(min(4, n - 4 * grp)):
                kk = 4 * grp + k
                use_bank(k)
                kb.emit(TE, lambda e, kk=kk, k=k: e.transpose(
                    out=C[k][:, 0:128], in_=src_fn(kk), identity=idmat[:]))
                tg.append(kb.mark(TE))
            for k in range(min(4, n - 4 * grp)):
                kk = 4 * grp + k
                kb.wait(VE, tg[k])
                kb.emit(VE, lambda e, kk=kk, k=k: e.tensor_copy(
                    out=dst_fn(kk), in_=C[k][:, 0:128]))
                bankfree[k] = kb.mark(VE)
            ve_t = kb.mark(VE)
        return ve_t

    def ln_slice(s2, write_xtok_cb=None):
        transpose4(lambda kk: SUMT[:, kk, :],
                   lambda kk: XTOK[:, 128*kk:128*(kk+1)], idf)
        ln_slice_stats(s2, write_xtok_cb)

    def ln3_slice(s2):
        kb.selfsync(VE)
        kb.emit(VE, lambda e: e.reduce_sum(out=colsc[:, 0:1], in_=XTOK[:], axis=AX))
        kb.selfsync(VE)
        kb.emit(VE, lambda e: e.tensor_scalar(out=colsc[:, 1:2], in0=colsc[:, 0:1],
            scalar1=1.0/1024.0, scalar2=None, op0=OP.mult))
        kb.selfsync(VE)
        kb.emit(VE, lambda e: e.tensor_scalar(out=XC[:], in0=XTOK[:],
            scalar1=colsc[:, 1:2], scalar2=None, op0=OP.subtract))
        ve_c = kb.mark(VE)
        kb.wait(SE, ve_c)
        kb.emit(SE, lambda e: e.activation(out=SQ[:], in_=XC[:], func=AF.Square))
        se_sq = kb.mark(SE)
        kb.wait(VE, se_sq)
        kb.emit(VE, lambda e: e.reduce_sum(out=colsc[:, 2:3], in_=SQ[:], axis=AX))
        kb.selfsync(VE)
        kb.emit(VE, lambda e: e.tensor_scalar(out=colsc[:, 2:3], in0=colsc[:, 2:3],
            scalar1=1.0/1024.0, scalar2=1e-5, op0=OP.mult, op1=OP.add))
        ve_v2 = kb.mark(VE)
        kb.wait(SE, ve_v2)
        kb.emit(SE, lambda e: e.activation(out=colsc[:, 3:4], in_=colsc[:, 2:3],
                                           func=AF.Ln))
        kb.selfsync(SE)
        kb.emit(SE, lambda e: e.activation(out=colsc[:, 3:4], in_=colsc[:, 3:4],
                                           func=AF.Exp, scale=-0.5))
        se_r = kb.mark(SE)
        kb.wait(VE, se_r)
        kb.emit(VE, lambda e: e.tensor_scalar(out=XC[:], in0=XC[:],
            scalar1=colsc[:, 3:4], scalar2=None, op0=OP.mult))
        kb.selfsync(VE)
        kb.emit(VE, lambda e: e.reduce_max(out=colsc[:, 0:1], in_=XC[:], axis=AX,
                                           apply_absolute_value=True))
        kb.selfsync(VE)
        kb.emit(VE, lambda e: e.tensor_scalar_max(out=colsc[:, 0:1],
            in0=colsc[:, 0:1], scalar1=1e-5))
        kb.selfsync(VE)
        kb.emit(VE, lambda e: e.reciprocal(out=colsc[:, 1:2], in_=colsc[:, 0:1]))
        kb.selfsync(VE)
        kb.emit(VE, lambda e, s2=s2: e.tensor_scalar(out=SXC[:, s2:s2+1],
            in0=colsc[:, 1:2], scalar1=127.0, scalar2=None, op0=OP.mult))
        ve_x3 = kb.mark(VE)
        kb.wait("gpsimd", ve_x3)
        tok = kb.dma("gpsimd", dr["x3s"][128*s2:128*(s2+1), :], XC[:], sem="GAS")
        kb.wait(VE, tok)

    def ln_slice_stats(s2, write_xtok_cb=None):
        kb.selfsync(VE)
        kb.emit(VE, lambda e: e.reduce_sum(out=colsc[:, 0:1], in_=XTOK[:], axis=AX))
        kb.selfsync(VE)
        kb.emit(VE, lambda e: e.tensor_scalar(out=colsc[:, 1:2], in0=colsc[:, 0:1],
            scalar1=1.0/1024.0, scalar2=None, op0=OP.mult))
        kb.selfsync(VE)
        kb.emit(VE, lambda e: e.tensor_scalar(out=XC[:], in0=XTOK[:],
            scalar1=colsc[:, 1:2], scalar2=None, op0=OP.subtract))
        ve_c = kb.mark(VE)
        kb.wait(SE, ve_c)
        kb.emit(SE, lambda e: e.activation(out=SQ[:], in_=XC[:], func=AF.Square))
        se_sq = kb.mark(SE)
        kb.wait(VE, se_sq)
        kb.emit(VE, lambda e: e.reduce_sum(out=colsc[:, 2:3], in_=SQ[:], axis=AX))
        kb.selfsync(VE)
        kb.emit(VE, lambda e: e.tensor_scalar(out=colsc[:, 2:3], in0=colsc[:, 2:3],
            scalar1=1.0/1024.0, scalar2=1e-5, op0=OP.mult, op1=OP.add))
        ve_v2 = kb.mark(VE)
        kb.wait(SE, ve_v2)
        kb.emit(SE, lambda e: e.activation(out=colsc[:, 3:4], in_=colsc[:, 2:3],
                                           func=AF.Ln))
        kb.selfsync(SE)
        kb.emit(SE, lambda e: e.activation(out=colsc[:, 3:4], in_=colsc[:, 3:4],
                                           func=AF.Exp, scale=-0.5))
        se_r = kb.mark(SE)
        kb.wait(VE, se_r)
        kb.emit(VE, lambda e: e.tensor_scalar(out=XC[:], in0=XC[:],
            scalar1=colsc[:, 3:4], scalar2=None, op0=OP.mult))
        if write_xtok_cb is not None:
            write_xtok_cb()
        ve_n = kb.mark(VE)
        kb.wait(TE, ve_n)
        transpose4(lambda kk: XC[:, 128*kk:128*(kk+1)],
                   lambda kk: xT[:, kk, 128*s2:128*(s2+1)], idf)

    for t in range(8 if RUN1 else 0):
        tok = kb.dma("sync", WOF[:], dr["wo"][128*t:128*(t+1), :], sem="LWO")
        kb.wait(VE, tok)
        kb.emit(VE, lambda e, t=t: e.tensor_copy(out=woB[:, t, :], in_=WOF[:]))
        ve_w = kb.mark(VE)
        kb.wait("sync", ve_w)
    if RUN1:
        kb.wait(TE, ve_w)

    if DUMP == "attf":
        return dump([(128*t2, ATTF[:, t2, :]) for t2 in range(8)])
    for s2 in range(16 if RUN1 else 0):
        for dt in range(8):
            for ht in range(8):
                kb.emit(TE, lambda e, dt=dt, ht=ht, s2=s2: e.matmul(
                    out=PAV[:, dt, :], lhsT=woB[:, ht, 128*dt:128*(dt+1)],
                    rhs=ATTF[:, ht, 128*s2:128*(s2+1)],
                    start=(ht == 0), stop=(ht == 7)))
        wo_m = kb.mark(TE)
        kb.wait(VE, wo_m)
        for dt in range(8):
            kb.emit(VE, lambda e, dt=dt, s2=s2: e.tensor_add(
                out=SUMT[:, dt, :], in0=PAV[:, dt, :],
                in1=xT[:, dt, 128*s2:128*(s2+1)]))
        sum_m = kb.mark(VE)
        kb.wait(TE, sum_m)
        ln_slice(s2)
    kb.barrier()

    for t in range(8):
        kb.emit(VE, lambda e, t=t: e.tensor_copy(out=xB[:, t, :], in_=xT[:, t, :]))
    kb.barrier()

    if DBG_STAGE == 1:
        st = [kb.dma("gpsimd", dr["out"][128*t:128*(t+1), :], xT[:, t, :], sem="GOUT")
              for t in range(8)]
        kb.wait("gpsimd", st[-1])
        return finish_emit(nc, es, kb)

    Rb = COSS[:, 0, :]
    MB = COSS[:, 1, :]
    rrow = COSS[0:1, 0, :]
    srow = COSS[0:1, 1, :]
    t1row = stg[0:1, :]
    erow = stg2[0:1, :]
    sq_marks = []
    for dt in range(8):
        if dt > 0:
            kb.wait(SE, sq_marks[-1])
        kb.emit(SE, lambda e, dt=dt: e.activation(out=stg2[:], in_=xT[:, dt, :],
                                                  func=AF.Square))
        m_sq = kb.mark(SE)
        kb.wait(TE, m_sq)
        for j in range(4):
            kb.emit(TE, lambda e, dt=dt, j=j: e.matmul(
                out=C[j][0:1, 0:512], lhsT=onescol[:], rhs=stg2[:, 512*j:512*(j+1)],
                start=(dt == 0), stop=(dt == 7)))
        sq_marks.append(kb.mark(TE))
    kb.wait(SE, sq_marks[-1])
    for j in range(4):
        kb.emit(SE, lambda e, j=j: e.activation(
            out=erow[:, 512*j:512*(j+1)], in_=C[j][0:1, 0:512], func=AF.Ln))
    kb.selfsync(SE)
    for j in range(4):
        kb.emit(SE, lambda e, j=j: e.activation(
            out=rrow[:, 512*j:512*(j+1)], in_=erow[:, 512*j:512*(j+1)], func=AF.Exp,
            scale=-0.5))
    kb.selfsync(SE)
    for j in range(4):
        kb.emit(SE, lambda e, j=j: e.activation(
            out=rrow[:, 512*j:512*(j+1)], in_=rrow[:, 512*j:512*(j+1)],
            func=AF.Copy, scale=32.0))
    se_r2 = kb.mark(SE)
    for j in range(4):
        bankfree[j] = se_r2
    kb.wait(TE, se_r2)
    rb_m = []
    for j in range(4):
        kb.emit(TE, lambda e, j=j: e.matmul(
            out=C[j][:, 0:512], lhsT=ones128[:], rhs=rrow[:, 512*j:512*(j+1)],
            start=True, stop=True))
        rb_m.append(kb.mark(TE))
    for j in range(4):
        kb.wait(VE, rb_m[j])
        kb.emit(VE, lambda e, j=j: e.tensor_copy(out=Rb[:, 512*j:512*(j+1)],
                                                 in_=C[j][:, 0:512]))
        bankfree[j] = kb.mark(VE)
    ve_rb = kb.mark(VE)
    kb.wait("gpsimd", ve_rb)
    g_rc0 = kb.dma("gpsimd", PADB[0:16, 0:128], rrow[0:1, :])
    kb.wait(TE, g_rc0)
    kb.emit(TE, lambda e: e.transpose(out=PAV[:, 2, 0:16], in_=PADB[0:16, 0:128],
                                      identity=idf[0:16, 0:16]))
    m_rc = kb.mark(TE)
    kb.wait(VE, m_rc)
    kb.emit(VE, lambda e: e.tensor_copy(out=rcol[:], in_=PAV[:, 2, 0:16]))
    g_rc = kb.mark(VE)
    if DUMP == "rc0":
        kb.wait(VE, g_rc0)
        return dump([(0, PADB)])
    if DUMP == "rc1":
        return dump([(0, rcol)])
    if DUMP == "rms":
        return dump([(0, Rb)])
    s_marks = []
    for j in range(4):
        use_bank(j)
        for dt in range(8):
            kb.emit(TE, lambda e, j=j, dt=dt: e.matmul(
                out=C[j][0:1, 0:512], lhsT=qrtB[:, dt:dt+1],
                rhs=xB[:, dt, 512*j:512*(j+1)],
                start=(dt == 0), stop=(dt == 7)))
        s_marks.append(kb.mark(TE))
    for j in range(4):
        kb.wait(VE, s_marks[j])
        kb.emit(VE, lambda e, j=j: e.tensor_copy(
            out=srow[:, 512*j:512*(j+1)], in_=C[j][0:1, 0:512]))
        bankfree[j] = kb.mark(VE)
    kb.selfsync(VE)

    if DUMP == "srowd":
        return dump([(0, srow)])
    LOGK = float(np.log(np.float64(1152.0)))
    LOGK_HI = float(np.float32(LOGK)); LOGK_LO = float(np.float32(LOGK - np.float64(LOGK_HI)))
    A1 = float(np.float32(np.float32(LOGK_HI) + np.float32(
        np.float32(-np.log(np.float64(2048.0))))))
    cells = colsc
    c_nega, c_mt, c_tmp, c_lnv = [cells[0:1, k:k+1] for k in range(4)]
    c_maxs = rcells[0:1, 0:1]
    c_logk = rcells[0:1, 1:2]
    kb.emit(VE, lambda e: e.memset(c_logk[:], LOGK_HI))
    kb.selfsync(VE)
    kb.emit(VE, lambda e: e.tensor_scalar(out=c_logk[:], in0=c_logk[:],
        scalar1=LOGK_LO, scalar2=None, op0=OP.add))
    kb.emit(VE, lambda e: e.memset(c_nega[:], -A1))
    kb.emit(VE, lambda e: e.reduce_max(out=c_maxs[:], in_=srow[:], axis=AX))
    kb.selfsync(VE)

    route_state = {"iter": 0, "done": False}
    def route_step():
        k = route_state["iter"]
        if k >= 49 or route_state["done"]:
            return
        route_state["iter"] += 1
        kb.selfsync(VE)
        kb.emit(VE, lambda e: e.tensor_scalar(out=t1row[:], in0=srow[:],
            scalar1=c_nega[:], scalar2=None, op0=OP.min))
        kb.emit(VE, lambda e: e.tensor_scalar(out=c_mt[:], in0=c_nega[:],
            scalar1=c_maxs[:], scalar2=-1.0, op0=OP.min, op1=OP.mult))
        m1 = kb.mark(VE)
        kb.wait(SE, m1)
        kb.emit(SE, lambda e: e.activation(out=erow[:], in_=t1row[:],
            func=AF.Exp, bias=c_mt[:]))
        m2 = kb.mark(SE)
        kb.wait(VE, m2)
        kb.emit(VE, lambda e: e.reduce_sum(out=c_tmp[:], in_=erow[:], axis=AX))
        m3 = kb.mark(VE)
        kb.wait(SE, m3)
        kb.emit(SE, lambda e: e.activation(out=c_lnv[:], in_=c_tmp[:], func=AF.Ln))
        m4 = kb.mark(SE)
        kb.wait(VE, m4)
        kb.emit(VE, lambda e: e.tensor_tensor(out=c_tmp[:], in0=c_logk[:],
            in1=c_mt[:], op=OP.add))
        kb.selfsync(VE)
        kb.emit(VE, lambda e: e.tensor_tensor(out=c_tmp[:], in0=c_tmp[:],
            in1=c_lnv[:], op=OP.subtract))
        kb.selfsync(VE)
        kb.emit(VE, lambda e: e.tensor_scalar(out=c_nega[:], in0=c_tmp[:],
            scalar1=-1.0, scalar2=None, op0=OP.mult))

    def route_finish():
        while route_state["iter"] < 49:
            route_step()
        route_state["done"] = True
        kb.selfsync(VE)
        kb.emit(VE, lambda e: e.tensor_scalar(out=t1row[:], in0=srow[:],
            scalar1=c_nega[:], scalar2=0.0, op0=OP.subtract, op1=OP.is_ge))
        kb.selfsync(VE)
        kb.emit(VE, lambda e: e.tensor_tensor_scan(out=erow[:], data0=t1row[:],
            data1=t1row[:], initial=0.0, op0=OP.add, op1=OP.bypass))
        kb.selfsync(VE)
        kb.emit(VE, lambda e: e.tensor_scalar(out=erow[:], in0=erow[:],
            scalar1=1024.5, scalar2=None, op0=OP.is_le))
        kb.selfsync(VE)
        kb.emit(VE, lambda e: e.tensor_mul(out=srow[:], in0=t1row[:], in1=erow[:]))
        return kb.mark(VE)

    if DUMP == "maskfast":
        mask_ve = route_finish()
        kb.wait("gpsimd", mask_ve)
        return dump([(0, srow)])
    def cast_pair_g(toks, par, gcol):
        kb.wait(VE, toks[-1])
        if last_cast[par] is not None:
            kb.wait(VE, last_cast[par])
        for dt in range(8):
            kb.emit(VE, lambda e, par=par, dt=dt, g=gcol: e.tensor_scalar(
                out=wqb[:, par, dt, :], in0=wqf[:, par, dt, :],
                scalar1=g[:, dt:dt+1], scalar2=None, op0=OP.mult))
        return kb.mark(VE)

    def load_blk(colblk, par, sem):
        if last_cast[par] is not None:
            kb.wait("sync", last_cast[par])
        return [kb.dma("sync", wqf[:, par, dt, :], dr["lhw_t"][colblk, dt], sem=sem)
                for dt in range(8)]

    def proj_pair2(par, cast_tok, dst_bf, qscale):
        kb.wait(TE, cast_tok)
        marks = []
        for j in range(4):
            use_bank(j)
            for dt in range(8):
                kb.emit(TE, lambda e, j=j, dt=dt, par=par: e.matmul(
                    out=C[j][:], lhsT=wqb[:, par, dt, :],
                    rhs=xB[:, dt, 512*j:512*(j+1)],
                    start=(dt == 0), stop=(dt == 7)))
            marks.append(kb.mark(TE))
        for j in range(4):
            kb.wait(VE, marks[j])
            if qscale is None:
                kb.emit(VE, lambda e, j=j, d=dst_bf: e.tensor_mul(
                    out=d[:, 512*j:512*(j+1)], in0=C[j][:],
                    in1=Rb[:, 512*j:512*(j+1)]))
            else:
                kb.emit(VE, lambda e, j=j, d=dst_bf, qs=qscale:
                        e.scalar_tensor_tensor(
                    out=d[:, 512*j:512*(j+1)], in0=C[j][:], scalar=qs,
                    in1=Rb[:, 512*j:512*(j+1)], op0=OP.mult, op1=OP.mult))
            bankfree[j] = kb.mark(VE)
        return kb.mark(VE), marks[-1]

    def proj_v2(par, cast_tok):
        kb.wait(TE, cast_tok)
        if prev_raw is not None:
            kb.wait(TE, prev_raw)
        for g in range(4):
            for k in range(4):
                s16 = 4 * g + k
                for dt in range(8):
                    kb.emit(TE, lambda e, s16=s16, k=k, dt=dt, par=par: e.matmul(
                        out=PAV[:, k, :], lhsT=xB[:, dt, 128*s16:128*(s16+1)],
                        rhs=wqb[:, par, dt, :], start=(dt == 0), stop=(dt == 7)))
            m = kb.mark(TE)
            kb.wait(VE, m)
            kb.wait(VE, g_rc)
            for k in range(4):
                s16 = 4 * g + k
                kb.emit(VE, lambda e, s16=s16, k=k: e.tensor_scalar(
                    out=VA[:, s16, 0, 0:64], in0=PAV[:, k, 0:64],
                    scalar1=rcol[:, s16:s16+1], scalar2=None, op0=OP.mult))
                kb.emit(VE, lambda e, s16=s16, k=k: e.tensor_scalar(
                    out=VA[:, s16, 1, 0:64], in0=PAV[:, k, 64:128],
                    scalar1=rcol[:, s16:s16+1], scalar2=None, op0=OP.mult))
            ve_m = kb.mark(VE)
            kb.wait(TE, ve_m)
        return kb.mark(TE), ve_m

    def head_tail(h_parity, ptile, prow):
        kb.selfsync(VE)
        kb.emit(VE, lambda e: e.reciprocal(out=RDEN[:], in_=RAW[:, :, 64]))
        kb.selfsync(VE)
        for s2 in range(16):
            kb.emit(VE, lambda e, s2=s2: e.tensor_scalar(
                out=RAW[:, s2, 0:64], in0=RAW[:, s2, 0:64],
                scalar1=RDEN[:, s2:s2+1], scalar2=None, op0=OP.mult))
        norm_m = kb.mark(VE)
        kb.wait(VE, norm_m)
        kb.wait(TE, norm_m)
        for s2 in range(16):
            k = s2 % 4
            use_bank(k)
            if prow:
                kb.emit(VE, lambda e, s2=s2: e.tensor_copy(
                    out=PADB[:, 64:128], in_=RAW[:, s2, 0:64]))
                ve_p = kb.mark(VE)
                kb.wait(TE, ve_p)
                kb.emit(TE, lambda e, k=k: e.transpose(
                    out=C[k][:, 0:128], in_=PADB[:], identity=idf[:]))
            else:
                kb.emit(TE, lambda e, s2=s2, k=k: e.transpose(
                    out=C[k][0:64, 0:128], in_=RAW[:, s2, 0:64],
                    identity=idf[:]))
            tm2 = kb.mark(TE)
            kb.wait(VE, tm2)
            kb.emit(VE, lambda e, s2=s2, k=k, prow=prow, ptile=ptile:
                    e.tensor_copy(
                out=ATTF[prow:prow+64, ptile, 128*s2:128*(s2+1)],
                in_=C[k][prow:prow+64, 0:128]))
            bankfree[k] = kb.mark(VE)
        return kb.mark(VE)

    prev_last_exp = None
    for p4 in range(4):
        tq = load_blk(p4, 0, "LWQ")
        tk = load_blk(4 + p4, 1, "LWK")
        cq = cast_pair_g(tq, 0, gl)
        ck = cast_pair_g(tk, 1, gl)
        se_q, te_q = proj_pair2(0, cq, SBQ, 0.125)
        last_cast[0] = te_q
        se_k, te_k = proj_pair2(1, ck, SBK, None)
        last_cast[1] = te_k
        tv = load_blk(8 + p4, 0, "LWV")
        cv = cast_pair_g(tv, 0, gl)
        te_v, ve_v = proj_v2(0, cv)
        last_cast[0] = te_v
        for hh in range(2):
            h = 2 * p4 + hh
            rows = slice(64 * hh, 64 * hh + 64)
            kb.wait(TE, se_q); kb.wait(TE, se_k); kb.wait(TE, ve_v)
            if prev_last_exp is not None:
                kb.wait(TE, prev_last_exp)
            exp_m = {}
            av_m = {}
            for w in range(16):
                par = w % 2
                kw = w % 4
                use_bank(kw)
                kb.emit(TE, lambda e, w=w, kw=kw, rows=rows: e.matmul(
                    out=C[kw][:, 0:128], lhsT=SBK[rows, 128*w:128*(w+1)],
                    rhs=SBQ[rows, 128*w:128*(w+1)], start=True, stop=True))
                qkm = kb.mark(TE)
                kb.wait(SE, qkm)
                if w >= 2:
                    kb.wait(SE, av_m[w - 2])
                kb.emit(SE, lambda e, w=w, kw=kw, par=par: e.activation(
                    out=PR[:, par, 128*w:128*(w+1)], in_=C[kw][:, 0:128],
                    func=AF.Exp, bias=-32.0))
                exp_m[w] = kb.mark(SE)
                bankfree[kw] = exp_m[w]
                kb.wait(TE, exp_m[w])
                kb.emit(TE, lambda e, w=w, par=par, hh=hh: e.matmul(
                    out=PAV[:, w, 0:66], lhsT=PR[:, par, 128*w:128*(w+1)],
                    rhs=VA[:, w, hh, :], start=True, stop=True))
                av_m[w] = kb.mark(TE)
                if w % 4 == 3:
                    route_step()
            kb.wait(VE, av_m[15])
            for s2 in range(16):
                kb.emit(VE, lambda e, s2=s2: e.tensor_copy(
                    out=RAW[:, s2, :], in_=PAV[:, s2, 0:66]))
            raw_m = kb.mark(VE)
            prev_raw = raw_m
            kb.wait(TE, raw_m)
            head_tail(hh, h // 2, 64 * (h % 2))
            kb.wait(SE, raw_m)
            prev_last_exp = exp_m[15]
    if DUMP == "lattf":
        return dump([(128*t2, ATTF[:, t2, :]) for t2 in range(4)])
    kb.barrier()
    lw_tok = None
    for t4 in range(4):
        tok = kb.dma("sync", WOF[:], dr["wol_t"][t4], sem="LWO")
        kb.wait(VE, tok)
        if lw_tok is not None:
            kb.wait(VE, lw_tok)
        kb.emit(VE, lambda e, t4=t4: e.tensor_copy(out=wolB[:, t4, :], in_=WOF[:]))
        lw_tok = kb.mark(VE)
        kb.wait("sync", lw_tok)
    kb.wait(TE, lw_tok)
    for s2 in range(16):
        for dt in range(8):
            for ht in range(4):
                kb.emit(TE, lambda e, dt=dt, ht=ht, s2=s2: e.matmul(
                    out=PAV[:, dt, :], lhsT=wolB[:, ht, 128*dt:128*(dt+1)],
                    rhs=ATTF[:, ht, 128*s2:128*(s2+1)],
                    start=(ht == 0), stop=(ht == 3)))
        wo_m = kb.mark(TE)
        kb.wait(VE, wo_m)
        for dt in range(8):
            kb.emit(VE, lambda e, dt=dt, s2=s2: e.tensor_add(
                out=xT[:, dt, 128*s2:128*(s2+1)], in0=PAV[:, dt, :],
                in1=xT[:, dt, 128*s2:128*(s2+1)]))
        sum_m = kb.mark(VE)
        kb.wait(TE, sum_m)
        route_step()
    if DUMP == "postlight":
        return dump([(128*t, xT[:, t, :]) for t in range(8)])
    kb.barrier()
    prev_last_exp = None
    for p4 in range(4):
        tq = load_blk(12 + p4, 0, "LWQ")
        tk = load_blk(16 + p4, 1, "LWK")
        cq = cast_pair_g(tq, 0, gh)
        ck = cast_pair_g(tk, 1, gh)
        se_q, te_q = proj_pair2(0, cq, SBQ, 0.125)
        last_cast[0] = te_q
        se_k, te_k = proj_pair2(1, ck, SBK, None)
        last_cast[1] = te_k
        tv = load_blk(20 + p4, 0, "LWV")
        cv = cast_pair_g(tv, 0, gh)
        te_v, ve_v = proj_v2(0, cv)
        last_cast[0] = te_v
        for hh in range(2):
            h = 2 * p4 + hh
            rows = slice(64 * hh, 64 * hh + 64)
            kb.wait(TE, se_q); kb.wait(TE, se_k); kb.wait(TE, ve_v)
            if prev_last_exp is not None:
                kb.wait(TE, prev_last_exp)
            exp_m = {}
            av_done = {}
            for t in range(16):
                par = t % 2
                qk_marks = []
                for j in range(4):
                    use_bank(j)
                    kb.emit(TE, lambda e, t=t, j=j, rows=rows: e.matmul(
                        out=C[j][:], lhsT=SBK[rows, 128*t:128*(t+1)],
                        rhs=SBQ[rows, 512*j:512*(j+1)], start=True, stop=True))
                    qk_marks.append(kb.mark(TE))
                for j in range(4):
                    kb.wait(SE, qk_marks[j])
                    if t >= 2 and j == 0:
                        kb.wait(SE, av_done[t - 2])
                    kb.emit(SE, lambda e, t=t, j=j, par=par: e.activation(
                        out=PR[:, par, 512*j:512*(j+1)], in_=C[j][:],
                        func=AF.Exp, bias=-32.0))
                    exp_m[(t, j)] = kb.mark(SE)
                    bankfree[j] = exp_m[(t, j)]
                for s2 in range(16):
                    j = s2 // 4
                    kb.wait(TE, exp_m[(t, j)])
                    kb.emit(TE, lambda e, t=t, s2=s2, par=par, hh=hh: e.matmul(
                        out=PAV[:, s2, 0:66], lhsT=PR[:, par, 128*s2:128*(s2+1)],
                        rhs=VA[:, t, hh, :],
                        start=(t == 0 and s2 % 4 == 0),
                        stop=(t == 15 and s2 % 4 == 3)))
                av_done[t] = kb.mark(TE)
                if t % 4 == 3:
                    route_step()
            kb.wait(VE, av_done[15])
            for s2 in range(16):
                kb.emit(VE, lambda e, s2=s2: e.tensor_copy(
                    out=RAW[:, s2, :], in_=PAV[:, s2, 0:66]))
            raw_m = kb.mark(VE)
            prev_raw = raw_m
            kb.wait(TE, raw_m)
            head_tail(hh, 4 + h // 2, 64 * (h % 2))
            kb.wait(SE, raw_m)
            prev_last_exp = exp_m[(15, 0)]
    mask_ve = route_finish()
    kb.selfsync(VE)
    kb.wait(TE, mask_ve)
    mb_m = []
    for j in range(4):
        use_bank(j)
        kb.emit(TE, lambda e, j=j: e.matmul(
            out=C[j][:, 0:512], lhsT=ones128[:], rhs=srow[:, 512*j:512*(j+1)],
            start=True, stop=True))
        mb_m.append(kb.mark(TE))
    for j in range(4):
        kb.wait(VE, mb_m[j])
        kb.emit(VE, lambda e, j=j: e.tensor_copy(out=MB[:, 512*j:512*(j+1)],
                                                 in_=C[j][:, 0:512]))
        bankfree[j] = kb.mark(VE)
    ve_mb = kb.mark(VE)
    if DUMP == "maskd":
        kb.wait("gpsimd", ve_mb)
        return dump([(0, MB)])
    if DUMP == "hattf":
        return dump([(128*t2, ATTF[:, 4+t2, :]) for t2 in range(4)])
    kb.barrier()
    hw_tok = None
    for t4 in range(4):
        tok = kb.dma("sync", WOF[:], dr["woh_t"][t4], sem="LWO")
        kb.wait(VE, tok)
        if hw_tok is not None:
            kb.wait(VE, hw_tok)
        kb.emit(VE, lambda e, t4=t4: e.tensor_copy(out=wolB[:, t4, :], in_=WOF[:]))
        hw_tok = kb.mark(VE)
        kb.wait("sync", hw_tok)
    kb.wait(TE, hw_tok)
    for s2 in range(16):
        for dt in range(8):
            for ht in range(4):
                kb.emit(TE, lambda e, dt=dt, ht=ht, s2=s2: e.matmul(
                    out=PAV[:, dt, :], lhsT=wolB[:, ht, 128*dt:128*(dt+1)],
                    rhs=ATTF[:, 4 + ht, 128*s2:128*(s2+1)],
                    start=(ht == 0), stop=(ht == 3)))
        wo_m = kb.mark(TE)
        kb.wait(VE, wo_m)
        for dt in range(8):
            kb.emit(VE, lambda e, dt=dt, s2=s2: e.tensor_mul(
                out=SUMT[:, dt, :], in0=PAV[:, dt, :],
                in1=MB[:, 128*s2:128*(s2+1)]))
        kb.selfsync(VE)
        for dt in range(8):
            kb.emit(VE, lambda e, dt=dt, s2=s2: e.tensor_add(
                out=xT[:, dt, 128*s2:128*(s2+1)], in0=SUMT[:, dt, :],
                in1=xT[:, dt, 128*s2:128*(s2+1)]))
        sum_m = kb.mark(VE)
        kb.wait(TE, sum_m)
    kb.barrier()
    for s2 in range(16):
        transpose4(lambda kk, s2=s2: xT[:, kk, 128*s2:128*(s2+1)],
                   lambda kk: XTOK[:, 128*kk:128*(kk+1)], idf)
        ln_slice_stats(s2)
    kb.barrier()
    for t in range(8):
        kb.emit(VE, lambda e, t=t: e.tensor_copy(out=xB[:, t, :], in_=xT[:, t, :]))
    kb.barrier()

    if DBG_STAGE == 2:
        st = [kb.dma("gpsimd", dr["out"][128*t:128*(t+1), :], xT[:, t, :], sem="GOUT")
              for t in range(8)]
        kb.wait("gpsimd", st[-1])
        return finish_emit(nc, es, kb)

    SXC = rcol
    b24s = carve(ARF, 6144, 24)
    b24f = carve(ARF, 6144 + 24, 24)
    bvr = stg2[0:1, 0:1024]
    x2buf = carve(ARF, 4096, 2048).rearrange("p (a b) -> p a b", a=2, b=1024)
    t_b = kb.dma("gpsimd", b24f[:], dr["b24"], sem="GSM")
    t_bv = kb.dma("gpsimd", bvr[:], dr["bvrow"], sem="GSM")
    kb.wait(VE, t_bv)
    kb.emit(VE, lambda e: e.tensor_scalar(out=b24s[:], in0=b24f[:],
        scalar1=1.0/1024.0, scalar2=None, op0=OP.mult))
    ve_b = kb.mark(VE)
    for s2 in range(16):
        transpose4(lambda kk, s2=s2: xT[:, kk, 128*s2:128*(s2+1)],
                   lambda kk: XTOK[:, 128*kk:128*(kk+1)], idf)
        ve_t = kb.mark(VE)
        kb.wait("gpsimd", ve_t)
        tok = kb.dma("gpsimd", dr["x2s"][128*s2:128*(s2+1), :], XTOK[:], sem="GAS")
        kb.wait(VE, tok)
    kb.barrier()

    def l1_proj_block(blk, par, cast_tok, dst, bias_col, scale):
        kb.wait(TE, cast_tok)
        marks = []
        for j in range(4):
            use_bank(j)
            for dt in range(8):
                kb.emit(TE, lambda e, j=j, dt=dt, par=par: e.matmul(
                    out=C[j][:], lhsT=wqb[:, par, dt, :],
                    rhs=xB[:, dt, 512*j:512*(j+1)],
                    start=(dt == 0), stop=(dt == 7)))
            marks.append(kb.mark(TE))
        for j in range(4):
            kb.wait(VE, marks[j])
            kb.emit(VE, lambda e, j=j, d=dst, bc=bias_col, sc=scale:
                    e.tensor_scalar(
                out=d[:, 512*j:512*(j+1)], in0=C[j][:], scalar1=bc,
                scalar2=sc, op0=OP.add, op1=OP.mult))
            bankfree[j] = kb.mark(VE)
        return kb.mark(VE), marks[-1]

    def load_w1(colblk, par):
        if last_cast[par] is not None:
            kb.wait("sync", last_cast[par])
        return [kb.dma("sync", wqf[:, par, dt, :], dr["W1T_t"][colblk, dt],
                       sem=("LWQ" if par == 0 else "LWK"))
                for dt in range(8)]

    kb.wait(VE, ve_b)
    for blk in range(8):
        par = blk % 2
        t_w = load_w1(blk, par)
        c_w = cast_pair_w(t_w, par)
        se_m, te_m = l1_proj_block(blk, par, c_w, QT[:, blk, :],
                                   b24s[:, blk:blk+1], 1.0/1024.0)
        last_cast[par] = te_m
    for blk in range(8):
        par = blk % 2
        t_w = load_w1(8 + blk, par)
        c_w = cast_pair_w(t_w, par)
        se_m, te_m = l1_proj_block(8 + blk, par, c_w, kt_ap(blk),
                                   b24f[:, 8+blk:8+blk+1], 1.0)
        last_cast[par] = te_m
    kb.barrier()
    prev_exp = None
    for t in range(16):
        qk_marks = []
        for j in range(4):
            use_bank(j)
            for dt in range(8):
                kb.emit(TE, lambda e, t=t, j=j, dt=dt: e.matmul(
                    out=C[j][:], lhsT=kt_ap(dt)[:, 128*t:128*(t+1)],
                    rhs=QT[:, dt, 512*j:512*(j+1)],
                    start=(dt == 0), stop=(dt == 7)))
            qk_marks.append(kb.mark(TE))
        kb.wait(VE, qk_marks[-1])
        for j in range(4):
            kb.emit(VE, lambda e, j=j: e.reduce_max(
                out=colsc[:, j:j+1], in_=C[j][:], axis=AX, negate=True))
        kb.selfsync(VE)
        kb.emit(VE, lambda e: e.tensor_tensor(out=colsc[:, 0:1],
            in0=colsc[:, 0:1], in1=colsc[:, 1:2], op=OP.min))
        kb.emit(VE, lambda e: e.tensor_tensor(out=colsc[:, 2:3],
            in0=colsc[:, 2:3], in1=colsc[:, 3:4], op=OP.min))
        kb.selfsync(VE)
        kb.emit(VE, lambda e: e.tensor_tensor(out=colsc[:, 0:1],
            in0=colsc[:, 0:1], in1=colsc[:, 2:3], op=OP.min))
        nm = kb.mark(VE)
        kb.wait(SE, nm)
        for j in range(4):
            kb.emit(SE, lambda e, t=t, j=j: e.activation(
                out=SBT[:, t, 512*j:512*(j+1)], in_=C[j][:], func=AF.Exp,
                bias=colsc[:, 0:1],
                accum_out=colsc[:, 4+(j % 2):5+(j % 2)] if False else None))
            bankfree[j] = kb.mark(SE)
        se_e = kb.mark(SE)
        kb.wait(VE, se_e)
        kb.selfsync(VE)
        for j in range(4):
            kb.emit(VE, lambda e, t=t, j=j: e.reduce_sum(
                out=colsc[:, j:j+1], in_=SBT[:, t, 512*j:512*(j+1)], axis=AX))
        kb.selfsync(VE)
        kb.emit(VE, lambda e: e.tensor_add(out=colsc[:, 0:1], in0=colsc[:, 0:1],
                                           in1=colsc[:, 1:2]))
        kb.emit(VE, lambda e: e.tensor_add(out=colsc[:, 2:3], in0=colsc[:, 2:3],
                                           in1=colsc[:, 3:4]))
        kb.selfsync(VE)
        kb.emit(VE, lambda e, t=t: e.tensor_add(out=RDEN[:, t:t+1],
            in0=colsc[:, 0:1], in1=colsc[:, 2:3]))
        kb.selfsync(VE)
    kb.selfsync(VE)
    kb.emit(VE, lambda e: e.reciprocal(out=RDEN[:], in_=RDEN[:]))
    kb.selfsync(VE)
    ve_den = kb.mark(VE)
    kb.barrier()
    prev_vproj = None
    for vb in range(8):
        par = vb % 2
        t_w = load_w1(16 + vb, par)
        c_w = cast_pair_w(t_w, par)
        kb.wait(TE, c_w)
        kb.wait(TE, ve_den)
        for g in range(4):
            for k in range(4):
                s16 = 4 * g + k
                for dt in range(8):
                    kb.emit(TE, lambda e, s16=s16, k=k, dt=dt, par=par: e.matmul(
                        out=PAV[:, k, :], lhsT=xB[:, dt, 128*s16:128*(s16+1)],
                        rhs=wqb[:, par, dt, :], start=(dt == 0), stop=False))
                kb.emit(TE, lambda e, s16=s16, k=k, vb=vb: e.matmul(
                    out=PAV[:, k, :], lhsT=ones128[:],
                    rhs=bvr[:, 128*vb:128*(vb+1)], start=False, stop=True))
            m = kb.mark(TE)
            kb.wait(VE, m)
            for k in range(4):
                s16 = 4 * g + k
                kb.emit(VE, lambda e, s16=s16, k=k, vb=vb: e.tensor_scalar(
                    out=vt_ap(s16)[:, 128*vb:128*(vb+1)], in0=PAV[:, k, :],
                    scalar1=RDEN[:, s16:s16+1], scalar2=None, op0=OP.mult))
            ve_m = kb.mark(VE)
            kb.wait(TE, ve_m)
        last_cast[par] = kb.mark(TE)
    kb.barrier()
    x2toks = {}
    x2free = {}
    for s2 in range(2):
        x2toks[s2] = kb.dma("sync", x2buf[:, s2 % 2, :],
                            dr["x2s"][128*s2:128*(s2+1), :],
                            sem=("LWO" if s2 % 2 == 0 else "LWV"))
    for s2 in range(16):
        for half in range(2):
            for jt in range(16):
                kb.emit(TE, lambda e, s2=s2, half=half, jt=jt: e.matmul(
                    out=PAV[:, 4*half:4*half+4, :].rearrange("p a b -> p (a b)"),
                    lhsT=SBT[:, jt, 128*s2:128*(s2+1)],
                    rhs=vt_ap(jt)[:, 512*half:512*(half+1)],
                    start=(jt == 0), stop=(jt == 15)))
        av_m = kb.mark(TE)
        kb.wait(VE, av_m)
        kb.wait(VE, x2toks[s2])
        for half in range(2):
            kb.emit(VE, lambda e, s2=s2, half=half: e.tensor_add(
                out=XTOK[:, 512*half:512*(half+1)],
                in0=PAV[:, 4*half:4*half+4, :].rearrange("p a b -> p (a b)"),
                in1=x2buf[:, s2 % 2, 512*half:512*(half+1)]))
        sum_m = kb.mark(VE)
        x2free[s2] = sum_m
        if s2 + 2 < 16:
            kb.wait("sync", sum_m)
            x2toks[s2 + 2] = kb.dma("sync", x2buf[:, s2 % 2, :],
                dr["x2s"][128*(s2+2):128*(s2+3), :],
                sem=("LWO" if s2 % 2 == 0 else "LWV"))
        kb.wait(TE, sum_m)
        ln3_slice(s2)
    kb.barrier()

    WSTa = carve(ARF, 0, 4096)
    WSTb = CF32[:, 0:4096]
    t_cw = kb.dma("gpsimd", convw4[:], dr["convw"], sem="GSM")
    t_cb = kb.dma("gpsimd", convb4[:], dr["convb"], sem="GSM")
    kb.wait(VE, t_cb)

    def quant_weights(dram_ap, ntiles, F, dst_fn, mcell, scell):
        asum = colsc[:, 5:6]
        kb.emit(VE, lambda e: e.memset(asum[:], 0.0))
        kb.selfsync(VE)
        toks = {}
        vem = {}
        for tt in range(min(2, ntiles)):
            buf = WSTa if tt % 2 == 0 else WSTb
            qq = "sync" if tt % 2 == 0 else "gpsimd"
            toks[tt] = kb.dma(qq, buf[:, 0:F], dram_ap[128*tt:128*(tt+1), :],
                              sem=("LWQ" if tt % 2 == 0 else "GAS"))
        for tt in range(ntiles):
            buf = WSTa if tt % 2 == 0 else WSTb
            kb.wait(VE, toks[tt])
            kb.selfsync(VE)
            kb.emit(VE, lambda e, buf=buf, F=F: e.reduce_sum(out=colsc[:, 0:1],
                in_=buf[:, 0:F], axis=AX, apply_absolute_value=True))
            kb.selfsync(VE)
            kb.emit(VE, lambda e: e.tensor_add(out=asum[:], in0=asum[:],
                                               in1=colsc[:, 0:1]))
            vem[tt] = kb.mark(VE)
            if tt + 2 < ntiles:
                qq = "sync" if tt % 2 == 0 else "gpsimd"
                kb.wait(qq, vem[tt])
                buf2 = WSTa if tt % 2 == 0 else WSTb
                toks[tt + 2] = kb.dma(qq, buf2[:, 0:F],
                    dram_ap[128*(tt+2):128*(tt+3), :],
                    sem=("LWQ" if tt % 2 == 0 else "GAS"))
        kb.selfsync(VE)
        ve_a = kb.mark(VE)
        kb.wait(TE, ve_a)
        kb.emit(TE, lambda e: e.matmul(out=PAV[0:1, 0, 0:1], lhsT=onescol[:],
                                       rhs=asum[:], start=True, stop=True))
        m1 = kb.mark(TE)
        kb.wait(VE, m1)
        N = ntiles * 128 * F
        kb.emit(VE, lambda e, N=N: e.tensor_scalar(out=colsc[0:1, 0:1],
            in0=PAV[0:1, 0, 0:1], scalar1=1.0/N, scalar2=1e-5,
            op0=OP.mult, op1=OP.max))
        kb.selfsync(VE)
        kb.emit(VE, lambda e: e.reciprocal(out=colsc[0:1, 1:2],
                                           in_=colsc[0:1, 0:1]))
        kb.selfsync(VE)
        ve_c = kb.mark(VE)
        kb.wait(TE, ve_c)
        kb.emit(TE, lambda e: e.matmul(out=PAV[:, 1, 0:2], lhsT=ones128[:],
                                       rhs=colsc[0:1, 0:2], start=True, stop=True))
        m2 = kb.mark(TE)
        kb.wait(VE, m2)
        kb.emit(VE, lambda e, mc=mcell: e.tensor_copy(out=C4[:, mc:mc+1],
                                                      in_=PAV[:, 1, 0:1]))
        kb.emit(VE, lambda e, sc=scell: e.tensor_copy(out=C4[:, sc:sc+1],
                                                      in_=PAV[:, 1, 1:2]))
        kb.selfsync(VE)
        kb.wait("sync", ve_a)
        kb.wait("gpsimd", ve_a)
        toks = {}
        for tt in range(min(2, ntiles)):
            buf = WSTa if tt % 2 == 0 else WSTb
            qq = "sync" if tt % 2 == 0 else "gpsimd"
            toks[tt] = kb.dma(qq, buf[:, 0:F], dram_ap[128*tt:128*(tt+1), :],
                              sem=("LWQ" if tt % 2 == 0 else "GAS"))
        for tt in range(ntiles):
            buf = WSTa if tt % 2 == 0 else WSTb
            kb.wait(VE, toks[tt])
            kb.emit(VE, lambda e, tt=tt, buf=buf, F=F, sc=scell: e._custom_dve(
                RNDQ, out=dst_fn(tt), in0=buf[:, 0:F],
                s0=C4[:, sc:sc+1], imm2=MAGIC))
            kb.selfsync(VE)
            kb.emit(VE, lambda e, tt=tt, F=F: e.tensor_scalar(
                out=dst_fn(tt), in0=dst_fn(tt), scalar1=1.0, scalar2=-1.0,
                op0=OP.min, op1=OP.max))
            vq = kb.mark(VE)
            if tt + 2 < ntiles:
                qq = "sync" if tt % 2 == 0 else "gpsimd"
                kb.wait(qq, vq)
                buf2 = WSTa if tt % 2 == 0 else WSTb
                toks[tt + 2] = kb.dma(qq, buf2[:, 0:F],
                    dram_ap[128*(tt+2):128*(tt+3), :],
                    sem=("LWQ" if tt % 2 == 0 else "GAS"))
        return kb.mark(VE)

    if DUMP == "preq":
        return dump([(0, SXC)])
    quant_weights(dr["w1T"], 8, 4096, lambda tt: w1q[:, tt, :], 4, 5)
    kb.barrier()
    if DUMP == "wq1":
        return dump([(128*t2, w1q[:, t2, :]) for t2 in range(8)])
    quant_weights(dr["w2T"], 32, 1024, lambda tt: w2q[:, tt, :], 0, 1)
    kb.barrier()
    if DUMP == "wq2":
        return dump([(0, C4)])

    sq4_free = [None]

    def out_section(o_lo, OL, ioff, t_r, src_lo=0, nrows=None):
        if nrows is None:
            nrows = OL
        for dto in range(8):
            ps = dto % 4
            for ct in range(32):
                kb.emit(TE, lambda e, dto=dto, ct=ct, ps=ps, OL=OL, ioff=ioff:
                        e.matmul(
                    out=PAV[:, 2*ps:2*ps+1, :]
                        .rearrange("p a b -> p (a b)")[:, 0:OL],
                    lhsT=w2q[:, ct, 128*dto:128*(dto+1)],
                    rhs=INTH[:, ct, ioff:ioff+OL],
                    start=(ct == 0), stop=(ct == 31)))
            tm_o = kb.mark(TE)
            kb.wait(VE, tm_o)
            kb.emit(VE, lambda e, ps=ps, OL=OL: e.tensor_copy(
                out=PADB[:, 0:OL],
                in_=PAV[:, 2*ps:2*ps+1, :].rearrange("p a b -> p (a b)")[:, 0:OL]))
            ve_o = kb.mark(VE)
            use_bank(2)
            kb.wait(TE, ve_o)
            kb.emit(TE, lambda e, OL=OL: e.transpose(
                out=C[2][0:OL, 0:128], in_=PADB[:, 0:OL], identity=idf[:]))
            tm_t = kb.mark(TE)
            kb.wait(VE, tm_t)
            if t_r is not None:
                kb.wait(VE, t_r)
            kb.emit(VE, lambda e, dto=dto, OL=OL: e.scalar_tensor_tensor(
                out=XT4[0:OL, 128*dto:128*(dto+1)], in0=C[2][0:OL, 0:128],
                scalar=colsc[0:OL, 2:3], in1=XT4[0:OL, 128*dto:128*(dto+1)],
                op0=OP.mult, op1=OP.add))
            bankfree[2] = kb.mark(VE)
        kb.selfsync(VE)
        kb.emit(VE, lambda e, OL=OL: e.reduce_sum(out=colsc[0:OL, 0:1],
            in_=XT4[0:OL, :], axis=AX))
        kb.selfsync(VE)
        kb.emit(VE, lambda e, OL=OL: e.tensor_scalar(out=colsc[0:OL, 1:2],
            in0=colsc[0:OL, 0:1],
            scalar1=1.0/1024.0, scalar2=None, op0=OP.mult))
        kb.selfsync(VE)
        kb.emit(VE, lambda e, OL=OL: e.tensor_scalar(out=XC4[0:OL, :],
            in0=XT4[0:OL, :], scalar1=colsc[0:OL, 1:2], scalar2=None,
            op0=OP.subtract))
        ve_c4 = kb.mark(VE)
        kb.wait(SE, ve_c4)
        kb.emit(SE, lambda e, OL=OL: e.activation(out=SQ4[0:OL, :],
            in_=XC4[0:OL, :], func=AF.Square))
        se_s4 = kb.mark(SE)
        kb.wait(VE, se_s4)
        kb.emit(VE, lambda e, OL=OL: e.reduce_sum(out=colsc[0:OL, 2:3],
            in_=SQ4[0:OL, :], axis=AX))
        sq4_free[0] = kb.mark(VE)
        kb.selfsync(VE)
        kb.emit(VE, lambda e, OL=OL: e.tensor_scalar(out=colsc[0:OL, 2:3],
            in0=colsc[0:OL, 2:3],
            scalar1=1.0/1024.0, scalar2=1e-5, op0=OP.mult, op1=OP.add))
        ve_v4 = kb.mark(VE)
        kb.wait(SE, ve_v4)
        kb.emit(SE, lambda e, OL=OL: e.activation(out=colsc[0:OL, 3:4],
            in_=colsc[0:OL, 2:3], func=AF.Ln))
        kb.selfsync(SE)
        kb.emit(SE, lambda e, OL=OL: e.activation(out=colsc[0:OL, 3:4],
            in_=colsc[0:OL, 3:4], func=AF.Exp, scale=-0.5))
        se_r4 = kb.mark(SE)
        kb.wait(VE, se_r4)
        kb.emit(VE, lambda e, OL=OL: e.tensor_scalar(out=XC4[0:OL, :],
            in0=XC4[0:OL, :], scalar1=colsc[0:OL, 3:4], scalar2=None,
            op0=OP.mult))
        ve_o4 = kb.mark(VE)
        kb.wait("gpsimd", ve_o4)
        tko = kb.dma("gpsimd", dr["out"][o_lo:o_lo+nrows, :],
                     XC4[src_lo:src_lo+nrows, :], sem="GOUT")
        kb.wait(VE, tko)
        return tko

    def sh_section(moff, OL):
        use_bank(0)
        kb.emit(TE, lambda e, moff=moff, OL=OL: e.transpose(
            out=C[0][0:OL, 0:128], in_=M1B[:, moff:moff+OL], identity=idf[:]))
        tm_m = kb.mark(TE)
        kb.wait(VE, tm_m)
        kb.emit(VE, lambda e, OL=OL: e.reduce_max(out=colsc[0:OL, 0:1],
            in_=C[0][0:OL, 0:128], axis=AX))
        bankfree[0] = kb.mark(VE)
        kb.selfsync(VE)
        kb.emit(VE, lambda e, OL=OL: e.tensor_scalar_max(out=colsc[0:OL, 0:1],
            in0=colsc[0:OL, 0:1], scalar1=1e-5))
        kb.selfsync(VE)
        kb.emit(VE, lambda e, OL=OL: e.reciprocal(out=colsc[0:OL, 1:2],
            in_=colsc[0:OL, 0:1]))
        kb.selfsync(VE)
        kb.emit(VE, lambda e, OL=OL: e.tensor_scalar(out=colsc[0:OL, 1:2],
            in0=colsc[0:OL, 1:2], scalar1=127.0, scalar2=None, op0=OP.mult))
        kb.emit(VE, lambda e, OL=OL: e.tensor_scalar(out=colsc[0:OL, 2:3],
            in0=colsc[0:OL, 0:1], scalar1=1.0/127.0, scalar2=None, op0=OP.mult))
        kb.selfsync(VE)
        kb.emit(VE, lambda e, OL=OL: e.tensor_mul(out=colsc[0:OL, 2:3],
            in0=colsc[0:OL, 2:3], in1=C4[0:OL, 0:1]))
        kb.selfsync(VE)
        return kb.mark(VE)

    def shb_tile(OL, src=None):
        if src is None:
            src = colsc[0:OL, 1:2]
        kb.selfsync(VE)
        use_bank(1)
        ve_sh = kb.mark(VE)
        kb.wait(TE, ve_sh)
        kb.emit(TE, lambda e, OL=OL, src=src: e.transpose(
            out=C[1][0:1, 0:OL], in_=src, identity=idf[0:OL, 0:OL]))
        tm_s = kb.mark(TE)
        kb.wait(VE, tm_s)
        kb.emit(VE, lambda e, OL=OL: e.tensor_copy(out=SHB[0:1, 0:OL],
                                                   in_=C[1][0:1, 0:OL]))
        bankfree[1] = kb.mark(VE)
        kb.selfsync(VE)
        ve_r = kb.mark(VE)
        use_bank(1)
        kb.wait(TE, ve_r)
        kb.emit(TE, lambda e, OL=OL: e.matmul(out=C[1][:, 0:OL], lhsT=ones128[:],
            rhs=SHB[0:1, 0:OL], start=True, stop=True))
        tm_b = kb.mark(TE)
        kb.wait(VE, tm_b)
        kb.emit(VE, lambda e, OL=OL: e.tensor_copy(out=SHB[:, 0:OL],
                                                   in_=C[1][:, 0:OL]))
        bankfree[1] = kb.mark(VE)
        kb.selfsync(VE)
        return kb.mark(VE)

    kb.emit(VE, lambda e: e.memset(HBUF[:, :, 0:2], 0.0))
    ve_h0 = kb.mark(VE)
    kb.wait(VE, ve_h0)
    kb.wait(TE, ve_h0)
    last_out = None
    for c in range(16):
        h_lo = 128 * c
        if sq4_free[0] is not None:
            kb.wait("sync", sq4_free[0])
        t_x3 = kb.dma("sync", X3B[:], dr["x3s"][h_lo:h_lo+128, :], sem="LWO")
        kb.wait(VE, t_x3)
        kb.emit(VE, lambda e, c=c: e._custom_dve(RNDQ, out=XC4[:], in0=X3B[:],
            s0=SXC[:, c:c+1], imm2=MAGIC))
        ve_ix = kb.mark(VE)
        kb.wait(VE, ve_ix)
        kb.wait(TE, ve_ix)
        if DUMP == "xc40" and c == 0:
            return dump([(0, XC4)])
        if DUMP == "xc41" and c == 1:
            return dump([(0, XC4), (128, X3B)])
        if DUMP == "w1qt0" and c == 0:
            kb.emit(VE, lambda e: e.tensor_copy(out=X3B[:], in_=w1q[:, 0, 0:1024]))
            kb.selfsync(VE)
            return dump([(0, X3B)])
        for k8 in range(8):
            j = k8 % 4
            use_bank(j)
            kb.emit(TE, lambda e, k8=k8, j=j: e.transpose(
                out=C[j][:, 0:128], in_=XC4[:, 128*k8:128*(k8+1)],
                identity=idf[:]))
            tmx = kb.mark(TE)
            kb.wait(VE, tmx)
            kb.emit(VE, lambda e, k8=k8, j=j: e.tensor_copy(
                out=XTB4[:, k8, :], in_=C[j][:, 0:128]))
            bankfree[j] = kb.mark(VE)
        ve_xt = kb.mark(VE)
        kb.wait(TE, ve_xt)
        if DUMP == "xtb40" and c == 0:
            kb.emit(VE, lambda e: e.tensor_copy(
                out=X3B[:], in_=XTB4.rearrange("p a b -> p (a b)")))
            kb.selfsync(VE)
            return dump([(0, X3B)])
        kb.emit(VE, lambda e: e.memset(M1B[:], 0.0))
        kb.emit(VE, lambda e, c=c: e.reciprocal(out=colsc[:, 4:5],
                                                in_=SXC[:, c:c+1]))
        kb.selfsync(VE)
        kb.emit(VE, lambda e: e.tensor_mul(out=colsc[:, 4:5], in0=colsc[:, 4:5],
                                           in1=C4[:, 4:5]))
        kb.selfsync(VE)
        shb_tile(128, colsc[0:128, 4:5])
        ve_hc = kb.mark(VE)
        NC = 129 if c == 15 else 128
        co = 1 if c == 0 else 0
        def absmax(ct, NC):
            kb.emit(VE, lambda e, ct=ct, NC=NC: e.tensor_tensor(
                out=M1B[:, 0:NC], in0=M1B[:, 0:NC], in1=HBUF[:, ct, 0:NC],
                op=OP.max))
            kb.selfsync(VE)
            kb.emit(VE, lambda e, ct=ct, NC=NC: e.scalar_tensor_tensor(
                out=M1B[:, 0:NC], in0=HBUF[:, ct, 0:NC], scalar=-1.0,
                in1=M1B[:, 0:NC], op0=OP.mult, op1=OP.max))
            kb.selfsync(VE)
        sm = {}
        for ct in range(32):
            ps = ct % 8
            CV = CONVO if ct % 2 == 0 else CONVO2
            for dt in range(8):
                kb.emit(TE, lambda e, ct=ct, dt=dt, ps=ps: e.matmul(
                    out=PAV[:, 2*ps:2*ps+1, :].rearrange("p a b -> p (a b)"),
                    lhsT=w1q[:, dt, 128*ct:128*(ct+1)], rhs=XTB4[:, dt, :],
                    start=(dt == 0), stop=(dt == 7)))
            tm_h = kb.mark(TE)
            kb.wait(VE, tm_h)
            if ct == 0:
                kb.wait(VE, ve_hc)
            kb.emit(VE, lambda e, ct=ct, ps=ps: e.tensor_tensor(
                out=HBUF[:, ct, 2:130],
                in0=PAV[:, 2*ps:2*ps+1, :].rearrange("p a b -> p (a b)"),
                in1=SHB[:, 0:128], op=OP.mult))
            hv = kb.mark(VE)
            kb.wait(TE, hv)
            kb.selfsync(VE)
            if ct >= 2:
                kb.wait(VE, sm[ct - 2])
            kb.emit(VE, lambda e, ct=ct, CV=CV: e.tensor_scalar(
                out=CV[:, 0:128], in0=HBUF[:, ct, 0:128],
                scalar1=convw4[:, ct, 0:1], scalar2=None, op0=OP.mult))
            kb.selfsync(VE)
            kb.emit(VE, lambda e, ct=ct, CV=CV: e.scalar_tensor_tensor(
                out=CV[:, 0:128], in0=HBUF[:, ct, 1:129],
                scalar=convw4[:, ct, 1:2], in1=CV[:, 0:128],
                op0=OP.mult, op1=OP.add))
            kb.selfsync(VE)
            kb.emit(VE, lambda e, ct=ct, CV=CV: e.scalar_tensor_tensor(
                out=CV[:, 0:128], in0=HBUF[:, ct, 2:130],
                scalar=convw4[:, ct, 2:3], in1=CV[:, 0:128],
                op0=OP.mult, op1=OP.add))
            kb.selfsync(VE)
            kb.emit(VE, lambda e, ct=ct, CV=CV: e.tensor_scalar(
                out=CV[:, 0:128], in0=CV[:, 0:128],
                scalar1=convb4[:, ct:ct+1], scalar2=None, op0=OP.add))
            if c == 15:
                kb.emit(VE, lambda e, ct=ct, CV=CV: e.tensor_scalar(
                    out=CV[:, 128:129], in0=HBUF[:, ct, 128:129],
                    scalar1=convw4[:, ct, 0:1], scalar2=None, op0=OP.mult))
                kb.selfsync(VE)
                kb.emit(VE, lambda e, ct=ct, CV=CV: e.scalar_tensor_tensor(
                    out=CV[:, 128:129], in0=HBUF[:, ct, 129:130],
                    scalar=convw4[:, ct, 1:2], in1=CV[:, 128:129],
                    op0=OP.mult, op1=OP.add))
                kb.selfsync(VE)
                kb.emit(VE, lambda e, ct=ct, CV=CV: e.tensor_scalar(
                    out=CV[:, 128:129], in0=CV[:, 128:129],
                    scalar1=convb4[:, ct:ct+1], scalar2=None, op0=OP.add))
            cm = kb.mark(VE)
            kb.wait(SE, cm)
            kb.emit(SE, lambda e, ct=ct, NC=NC, CV=CV: e.activation(
                out=HBUF[:, ct, 0:NC], in_=CV[:, 0:NC], func=AF.Silu))
            sm[ct] = kb.mark(SE)
            if ct >= 1:
                kb.wait(VE, sm[ct - 1])
                absmax(ct - 1, NC)
        kb.wait(VE, sm[31])
        absmax(31, NC)
        mm_m = kb.mark(VE)
        kb.wait(TE, mm_m)
        if DUMP == "hck0" and c == 0:
            return dump([(0, HBUF[:, 0:7, :].rearrange("p a b -> p (a b)"))])
        if DUMP == "hck1" and c == 1:
            return dump([(0, HBUF[:, 0:7, :].rearrange("p a b -> p (a b)"))])
        def do_out(o_lo, moff, r_lo, n_out):
            OL = 128
            sh_section(moff, OL)
            if DUMP == "sh0" and c == 0:
                kb.barrier()
                return dump([(0, colsc)])
            shb_tile(OL)
            if DUMP == "shb0" and c == 0:
                kb.barrier()
                return dump([(0, SHB)])
            for ct in range(32):
                kb.emit(VE, lambda e, ct=ct, moff=moff, OL=OL: e._custom_dve(
                    RNDM, out=INTH[:, ct, moff:moff+OL],
                    in0=HBUF[:, ct, moff:moff+OL], in1=SHB[:, 0:OL], imm2=MAGIC))
            ve_ih = kb.mark(VE)
            kb.wait(TE, ve_ih)
            kb.wait("sync", ve_ih)
            if DUMP == "rndm0" and c == 0:
                kb.barrier()
                return dump([(0, SHB)])
            t_r = kb.dma("sync", XT4[:], dr["x3s"][r_lo:r_lo+128, :], sem="LWV")
            return out_section(o_lo, OL, moff, t_r, src_lo=0, nrows=n_out)
        if c == 0:
            r = do_out(0, 1, 0, 127)
            if DUMP in ("sh0", "shb0", "rndm0"):
                return r
            last_out = r
            if DUMP == "oc0":
                kb.barrier()
                return dump([(0, XT4)])
        elif c < 15:
            last_out = do_out(128*c - 1, 0, 128*c - 1, 128)
        else:
            last_out = do_out(128*15 - 1, 0, 128*15 - 1, 128)
            last_out = do_out(1920, 1, 1920, 128)
        if c < 15:
            kb.selfsync(VE)
            kb.emit(VE, lambda e: e.tensor_copy(
                out=HBUF[:, :, 0:2], in_=HBUF[:, :, 128:130]))
            ve_halo = kb.mark(VE)
            kb.wait(VE, ve_halo)
            kb.wait(TE, ve_halo)
    kb.wait("gpsimd", last_out)
    kb.barrier()

    if DBG_STAGE == 3:
        for t2 in range(16):
            tok = kb.dma("sync", XTOK[:], dr["x3s"][128*t2:128*(t2+1), :],
                         sem="LWO")
            kb.wait("gpsimd", tok)
            tok2 = kb.dma("gpsimd", dr["out"][128*t2:128*(t2+1), :], XTOK[:],
                          sem="GOUT")
            kb.wait("sync", tok2)
        kb.wait("gpsimd", ("GOUT", kb.cnt["GOUT"]))

    return finish_emit(nc, es, kb)

def kernel(**inputs):
    in_maps = host_prep(inputs)
    nc = build_nc()
    mybir.codegen_inst_isa_subclasses(nc)
    res = run_bass_kernel_spmd(nc, in_maps, core_ids=list(range(NCORES)))
    outs = [np.asarray(res.results[i]["out"]) for i in range(NCORES)]
    if DBG_STAGE in (1, 2):
        return np.stack([o.T for o in outs], 0)
    return np.stack(outs, 0)


# version 05
# speedup vs baseline: 1.1006x; 1.0558x over previous
import math
import numpy as np
import ml_dtypes

import concourse.bass as bass
import concourse.mybir as mybir
from concourse.bass_utils import run_bass_kernel_spmd
import concourse.dve_ops as dve_ops
from concourse.dve_op import DveOp
from concourse.dve_spec import Spec, Src0, Src1, C0, C1, C2, One, lower, maxx, minn

F32 = mybir.dt.float32
BF16 = mybir.dt.bfloat16
AX = mybir.AxisListType.X
OP = mybir.AluOpType
AF = mybir.ActivationFunctionType

MAGIC = 12582912.0
NCORES = 8
S = 2048

def _register(name, spec, reference):
    for o in dve_ops.OPS:
        if o.name == name:
            return o
    ver = "v3"
    lowered = lower(spec, ver=ver, perf_en=False, name=name)
    op = DveOp(name, spec, subdim=False, uops_sha={ver: lowered.sha(ver)},
               reference=reference)
    dve_ops.OPS.append(op)
    dve_ops._SUB_OPCODE_FOR_NAME[name] = (
        dve_ops._CUSTOM_DVE_ROW_BASE + len(dve_ops.OPS) - 1)
    dve_ops.CUSTOM_DVE_SPECS[name] = spec
    dve_ops.CUSTOM_DVE_REFS[name] = reference
    return op

RNDQ = _register("ANT_RNDQ", Spec(body=(Src0 * C0 + C2) - C2),
                 lambda in0, in1, s0, s1, imm2: np.round(in0 * s0))
RNDM = _register("ANT_RNDM", Spec(body=(Src0 * Src1 + C2) - C2),
                 lambda in0, in1, s0, s1, imm2: np.round(in0 * in1))

ROPED = [h for h in range(16) if (h % 4) in (2, 3)]

def _head_perm64():
    order = []
    for half in range(2):
        base = 32 * half
        order += [base + 2 * f for f in range(16)]
        order += [base + 2 * f + 1 for f in range(16)]
    return order

def _tile_lhw(inputs, kvw):
    f32 = np.float32
    wql = np.asarray(inputs["L0_ffn_wq_light"], f32)
    wkvl = np.asarray(inputs["L0_ffn_wkv_light"], f32)
    wqh = np.asarray(inputs["L0_ffn_wq_heavy"], f32)
    wkh = kvw[:, :64].reshape(512, 1024).T
    wvh = kvw[:, 64:].reshape(512, 1024).T
    cols = np.concatenate([wql, wkvl[:, :512], wkvl[:, 512:],
                           wqh, wkh, wvh], axis=1)
    return np.ascontiguousarray(cols.reshape(8, 128, 24, 128).transpose(2, 0, 1, 3))

def host_prep(inputs):
    f32 = np.float32
    x = np.asarray(inputs["L0_attn_x"], f32)
    wqkv = np.asarray(inputs["L0_attn_w_qkv"], f32).copy()
    perm = _head_perm64()
    for h in ROPED:
        for sec in (0, 1024):
            base = sec + h * 64
            wqkv[:, base:base + 64] = wqkv[:, [base + p for p in perm]]
    qkvw_t = np.ascontiguousarray(
        wqkv.reshape(8, 128, 24, 128).transpose(2, 0, 1, 3))
    wo = np.asarray(inputs["L0_attn_w_o"], f32)
    lxi = np.asarray(inputs["L0_attn_log_xi"], f32).reshape(1, 16)

    import jax
    import jax.numpy as jnp
    with jax.default_device(jax.devices('cpu')[0]):
        pos_j = jnp.arange(S, dtype=jnp.float32)
        inv_freq_j = 10000.0 ** (-(jnp.arange(0, 64, 2, dtype=jnp.float32)
                                   / 64))
        theta_j = pos_j[:, None] * inv_freq_j[None, :]
        cos_t = np.asarray(jnp.cos(theta_j))
        sin_t = np.asarray(jnp.sin(theta_j))
    rows_c = np.zeros((128, S), f32)
    rows_s = np.zeros((128, S), f32)
    for r in range(128):
        rr = r % 64
        fr = 16 * (rr // 32) + (rr % 16)
        sg = np.float32(-1.0 if (rr % 32) < 16 else 1.0)
        rows_c[r] = cos_t[:, fr]
        rows_s[r] = sin_t[:, fr] * sg
    cos_hi, sin_hi = rows_c, rows_s
    cos_lo = np.zeros_like(cos_hi)
    sin_lo = np.zeros_like(sin_hi)

    JJ = np.zeros((128, 16), f32)
    for t in range(16):
        JJ[:, t] = 128 * t + np.arange(128)
    ident = np.eye(128, dtype=f32)
    identb = np.eye(128, dtype=ml_dtypes.bfloat16)
    triu = np.triu(np.ones((128, 128))).astype(ml_dtypes.bfloat16)

    gl = np.ascontiguousarray(
        np.asarray(inputs["L0_ffn_gamma_light"], f32).reshape(8, 128).T)
    gh = np.ascontiguousarray(
        np.asarray(inputs["L0_ffn_gamma_heavy"], f32).reshape(8, 128).T)
    qrt = np.ascontiguousarray(
        np.asarray(inputs["L0_ffn_q_routing_token"], f32).reshape(8, 128).T)
    kvw = np.asarray(inputs["L0_ffn_wkv_heavy"], f32).reshape(8, 128, 1024)
    shared = dict(
        qkvw_t=qkvw_t, wo=wo, lxi=lxi,
        cos_hi=cos_hi, cos_lo=cos_lo, sin_hi=sin_hi, sin_lo=sin_lo,
        JJ=JJ, ident=ident, identb=identb, triu=triu,
        gl=gl, gh=gh, qrt=qrt,
        lhw_t=_tile_lhw(inputs, kvw),
        wol_t=np.ascontiguousarray(
            np.asarray(inputs["L0_ffn_wo_light"], f32).reshape(4, 128, 1024)),
        woh_t=np.ascontiguousarray(
            np.asarray(inputs["L0_ffn_wo_heavy"], f32).reshape(4, 128, 1024)),
        W1T_t=np.ascontiguousarray(np.asarray(inputs["L1_attn_W"], f32).T
            .reshape(8, 128, 24, 128).transpose(2, 0, 1, 3)),
        bvrow=np.ascontiguousarray(
            np.asarray(inputs["L1_attn_b"], f32)[None, 2048:]),
        b24=np.ascontiguousarray(
            np.asarray(inputs["L1_attn_b"], f32).reshape(24, 128).T),
        w1T=np.ascontiguousarray(np.asarray(inputs["L1_ffn_w1"], f32).T),
        w2T=np.ascontiguousarray(np.asarray(inputs["L1_ffn_w2"], f32).T),
        convw=np.ascontiguousarray(
            np.asarray(inputs["L1_ffn_conv_w"], f32)[:, 0, :]
            .reshape(32, 128, 3).transpose(1, 0, 2)),
        convb=np.ascontiguousarray(
            np.asarray(inputs["L1_ffn_conv_b"], f32).reshape(32, 128).T),
    )
    in_maps = []
    for b in range(NCORES):
        m = dict(shared)
        m["xT"] = np.ascontiguousarray(x[b].T)
        in_maps.append(m)
    return in_maps

ENGINES = ("sync", "tensor", "vector", "scalar", "gpsimd")
ESEM = {"sync": "LDX", "gpsimd": "GPM", "tensor": "TE", "vector": "VE",
        "scalar": "SE"}
SEM_NAMES = ("LDX", "LWQ", "LWK", "LWV", "LWO", "GSM", "GAS", "GOUT", "GPM", "TE", "VE", "SE")

class KB:
    def __init__(self):
        self.ops = {e: [] for e in ENGINES}
        self.cnt = {s: 0 for s in SEM_NAMES}
        self.sems = {}

    def emit(self, eng, fn):
        self.ops[eng].append([fn, None])

    def dma(self, q, dst, src, sem=None, slow=False):
        if sem is None:
            sem = "LWQ" if q == "sync" else "GAS"
        self.cnt[sem] += 16
        tgt = self.cnt[sem]
        self.ops[q].append([
            lambda e, d=dst, s=src, sl=slow: e.dma_start(
                out=d, in_=s, allow_slow_non_contiguous=sl), (sem, 16)])
        return (sem, tgt)

    def wait(self, eng, tok):
        sem, v = tok
        def f(e, s=sem, vv=v):
            return e.wait_ge(self.sems[s], vv)
        f._is_wait = True
        self.emit(eng, f)

    def mark(self, eng):
        sem = ESEM[eng]
        step = 16 if sem not in ("TE", "VE", "SE", "GPM") else 1
        lst = self.ops[eng]
        i = len(lst) - 1
        while i >= 0 and self._is_wait(lst[i][0]):
            i -= 1
        if i >= 0 and lst[i][1] is not None and lst[i][1][0] == sem:
            return (sem, self.cnt[sem])
        self.cnt[sem] += step
        tgt = self.cnt[sem]
        if lst and lst[-1][1] is None and not self._is_wait(lst[-1][0]):
            lst[-1][1] = (sem, step)
        else:
            self.ops[eng].append([lambda e: e.nop(nofuse=True), (sem, step)])
        return (sem, tgt)

    @staticmethod
    def _is_wait(fn):
        return getattr(fn, "_is_wait", False)

    def selfsync(self, eng):
        tok = self.mark(eng)
        self.wait(eng, tok)

    def barrier(self):
        toks = [self.mark(e) for e in ("tensor", "vector", "scalar")]
        for s in ("LDX", "LWQ", "LWK", "LWV", "LWO", "GSM", "GAS", "GOUT"):
            toks.append((s, self.cnt[s]))
        for e in ENGINES:
            for t in toks:
                if t[1] > 0:
                    self.wait(e, t)

    def replay(self, eng_name, eng):
        for fn, inc in self.ops[eng_name]:
            inst = fn(eng)
            if inc is not None:
                inst.then_inc(self.sems[inc[0]], inc[1])

def finish_emit(nc, es, kb):
    with es:
        with nc.Block() as block:
            @block.sync
            def _(eng):
                kb.replay("sync", eng)
            @block.tensor
            def _(eng):
                kb.replay("tensor", eng)
            @block.vector
            def _(eng):
                kb.replay("vector", eng)
            @block.scalar
            def _(eng):
                kb.replay("scalar", eng)
            @block.gpsimd
            def _(eng):
                kb.replay("gpsimd", eng)
    return nc

DUMP = None
SKIP1 = False
DBG_STAGE = 4

def build_nc():
    nc = bass.Bass("TRN2", target_bir_lowering=False)
    dr = {}
    def dram(name, shape, dt=F32, kind="ExternalInput"):
        dr[name] = nc.dram_tensor(name, shape, dt, kind=kind).ap()
        return dr[name]
    dram("xT", [1024, 2048])
    dram("qkvw_t", [24, 8, 128, 128])
    dram("wo", [1024, 1024])
    dram("lxi", [1, 16])
    for n in ("cos_hi", "cos_lo", "sin_hi", "sin_lo"):
        dram(n, [128, 2048])
    dram("JJ", [128, 16])
    dram("ident", [128, 128]); dram("identb", [128, 128], BF16)
    dram("triu", [128, 128], BF16)
    dram("gl", [128, 8]); dram("gh", [128, 8]); dram("qrt", [128, 8])
    dram("lhw_t", [24, 8, 128, 128])
    dram("wol_t", [4, 128, 1024]); dram("woh_t", [4, 128, 1024])
    dram("W1T_t", [24, 8, 128, 128]); dram("b24", [128, 24])
    dram("bvrow", [1, 1024])
    dram("w1T", [1024, 4096]); dram("w2T", [4096, 1024])
    dram("convw", [128, 32, 3]); dram("convb", [128, 32])
    dram("x3s", [2048, 1024], F32, kind="Internal")
    dram("x2s", [2048, 1024], F32, kind="Internal")
    out = dram("out", [1024, 2048] if DBG_STAGE in (1, 2) else [2048, 1024],
               F32, kind="ExternalOutput")

    from contextlib import ExitStack
    es = ExitStack()
    def sb(name, shape, dt):
        return es.enter_context(nc.sbuf_tensor(name, shape, dt))

    xT   = sb("xT_s", [128, 8, 2048], F32)
    xT_h = xT
    xB   = sb("xB", [128, 8, 2048], BF16)
    COSS = sb("COSS", [128, 2, 2048], F32)
    ARF  = sb("ARF", [128, 7744], F32)
    AB   = sb("AB", [128, 32832], BF16)
    idf  = sb("idf", [128, 128], F32)
    idb  = sb("idb", [128, 128], BF16)
    triu = sb("triu_s", [128, 128], BF16)
    qrtB = sb("qrtB", [128, 8], BF16)

    def carve(arena, off, n, shape=None):
        ap = arena[:, off:off + n]
        if shape is not None:
            import string
            letters = string.ascii_lowercase
            dims = " ".join(letters[i] for i in range(len(shape)))
            kw = {letters[i]: shape[i] for i in range(len(shape))}
            ap = ap.rearrange(f"p ({dims}) -> p {dims}", **kw)
        return ap

    cosT = COSS[:, 0, :]
    sinT = COSS[:, 1, :]
    o = 0
    stg  = carve(ARF, o, 2048); o += 2048
    stg2 = carve(ARF, o, 2048); o += 2048
    wqf  = carve(ARF, o, 2048, (2, 8, 128)); o += 2048
    RAW  = carve(ARF, o, 1056, (16, 66)); o += 1056
    RDEN = carve(ARF, o, 16); o += 16
    JJ   = carve(ARF, o, 16); o += 16
    ivb  = carve(ARF, o, 16); o += 16
    biasH = carve(ARF, o, 128, (8, 16)); o += 128
    colsc = carve(ARF, o, 6); o += 6
    lxi  = carve(ARF, o, 16)[0:1, :]; o += 16
    ivr  = carve(ARF, o, 16)[0:1, :]; o += 16
    nivr = carve(ARF, o, 16)[0:1, :]; o += 16
    ones128 = carve(ARF, o, 128)[0:1, :]; o += 128
    onescol = carve(ARF, o, 2)[:, 0:1]; o += 2
    rcol = carve(ARF, o, 16); o += 16
    rcells = carve(ARF, o, 8); o += 8
    gl = carve(ARF, o, 8); o += 8
    gh = carve(ARF, o, 8); o += 8
    qrt = carve(ARF, o, 8); o += 8
    PADB = carve(ARF, o, 128); o += 128
    cc0 = carve(ARF, o, 2); o += 2
    cc1 = carve(ARF, o, 2); o += 2
    ccm32 = carve(ARF, o, 2); o += 2
    cln32 = carve(ARF, o, 2); o += 2
    assert o <= 7744, o
    nc.const_aps.register(F32, 0.0, cc0[:, 0:1])
    nc.const_aps.register(F32, 1.0, cc1[:, 0:1])
    nc.const_aps.register(F32, -32.0, ccm32[:, 0:1])
    nc.const_aps.register(F32, 3.4657359027997265, cln32[:, 0:1])
    SUMT = carve(ARF, 0, 1024, (8, 128))
    XTOK = carve(ARF, 1024, 1024)
    XC   = carve(ARF, 2048, 1024)
    SQ   = carve(ARF, 3072, 1024)
    WOF  = carve(ARF, 4096, 1024)
    o = 0
    wqb  = carve(AB, o, 2048, (2, 8, 128)); o += 2048
    SBQ  = carve(AB, o, 2048); o += 2048
    SBK  = carve(AB, o, 2048); o += 2048
    QXc  = carve(AB, o, 2048); o += 2048
    KXc  = carve(AB, o, 2048); o += 2048
    PR   = carve(AB, o, 4096, (2, 2048)); o += 4096
    VA   = carve(AB, o, 2112, (16, 2, 66)); o += 2112
    ATTF = carve(AB, o, 16384, (8, 2048)); o += 16384
    assert o <= 32832, o
    QX = QXc[0:65, :]
    KX = KXc[0:65, :]
    woB = carve(AB, 0, 8192, (8, 1024))
    wolB = carve(AB, 2048, 4096, (4, 1024))
    ATT = carve(AB, 8192, 1024, (8, 128))

    def sbuf_alias(name, shape, dtype, base_handle, off_bytes):
        h = nc.virtual_sbuf_tensor(name, shape, dtype)
        nc._tensor(h.name, h.shape, h.dtype, type="SB")
        mloc = nc.lookup_mloc(h)
        base = nc.lookup_mloc(base_handle).addr
        mloc.allocated = not nc.target_bir_lowering
        mloc.addr = base + off_bytes
        return h

    SBT = sbuf_alias("SBT", [128, 16, 2048], BF16, xT, 0)
    CB16 = sbuf_alias("CB16", [128, 8192], BF16, COSS, 0)
    QT = carve(AB, 2048, 16384, (8, 2048))
    KT_lo = carve(AB, 18432, 14336, (7, 2048))
    KT_hi = CB16[:, 0:2048]
    VT_lo = carve(AB, 18432, 14336, (14, 1024))
    VT_hi = CB16[:, 2048:4096].rearrange("p (a b) -> p a b", a=2, b=1024)

    AF16 = sbuf_alias("AF16", [128, 6144], BF16, ARF, 4112 * 4)
    HBUF = carve(ARF, 0, 4160, (32, 130))
    CF32 = sbuf_alias("CF32", [128, 4096], F32, COSS, 0)
    X3B = CF32[:, 0:1024]
    XT4 = CF32[:, 1024:2048]
    XC4 = CF32[:, 2048:3072]
    SQ4 = X3B
    CONVO = CF32[:, 3072:3204]
    CONVO2 = CF32[:, 3540:3672]
    M2B = CF32[:, 3672:3804]
    M1B = CF32[:, 3204:3336]
    SHB = CF32[:, 3336:3468]
    NXH = CF32[:, 3468:3532].rearrange("p (a b) -> p a b", a=32, b=2)
    C4 = carve(ARF, 7584, 8)
    INTX = AF16[:, 0:1024]
    XTB4 = AF16[:, 0:1024].rearrange("p (a b) -> p a b", a=8, b=128)
    INTH = AF16[:, 1024:5248].rearrange("p (a b) -> p a b", a=32, b=132)
    convw4 = carve(ARF, 7248, 96, (32, 3))
    convb4 = carve(ARF, 7344, 32)
    w1q = sbuf_alias("w1q", [128, 8, 4096], BF16, xT, 0)
    w2q = carve(AB, 0, 32768, (32, 1024))

    def kt_ap(dt):
        return KT_lo[:, dt, :] if dt < 7 else KT_hi

    def vt_ap(s16):
        return VT_lo[:, s16, :] if s16 < 14 else VT_hi[:, s16 - 14, :]

    C = [es.enter_context(nc.psum_tensor(f"C{j}", [128, 512], F32))
         for j in range(4)]
    PAV = es.enter_context(nc.psum_tensor("PAV", [128, 16, 128], F32))

    kb = KB()
    for s in SEM_NAMES:
        kb.sems[s] = es.enter_context(nc.semaphore(s))
    TE, VE, SE = "tensor", "vector", "scalar"

    bankfree = [None, None, None, None]

    def use_bank(j, eng=TE):
        if bankfree[j] is not None:
            kb.wait(eng, bankfree[j])

    def dump(rows):
        kb.barrier()
        toks = []
        for off, ap in rows:
            P, F = ap.shape[0], ap.shape[-1] if len(ap.shape) == 2 else None
            if F is None:
                F = 1
                for d in ap.shape[1:]:
                    F *= d
            toks.append(kb.dma("gpsimd", dr["out"][off:off+P, 0:F], ap, sem="GOUT"))
        kb.wait("gpsimd", toks[-1])
        return finish_emit(nc, es, kb)

    t_x = [kb.dma("sync", xT[:, t, :], dr["xT"][128*t:128*(t+1), :], sem="LDX")
           for t in range(8)]
    small = []
    small.append(kb.dma("gpsimd", cosT[:], dr["cos_hi"], sem="GSM"))
    small.append(kb.dma("gpsimd", sinT[:], dr["sin_hi"], sem="GSM"))
    small.append(kb.dma("gpsimd", stg[:], dr["cos_lo"], sem="GSM"))
    small.append(kb.dma("gpsimd", stg2[:], dr["sin_lo"], sem="GSM"))
    small.append(kb.dma("gpsimd", JJ[:], dr["JJ"], sem="GSM"))
    small.append(kb.dma("gpsimd", lxi[:], dr["lxi"], sem="GSM"))
    small.append(kb.dma("gpsimd", idf[:], dr["ident"], sem="GSM"))
    small.append(kb.dma("gpsimd", idb[:], dr["identb"], sem="GSM"))
    small.append(kb.dma("gpsimd", triu[:], dr["triu"], sem="GSM"))
    small.append(kb.dma("gpsimd", gl[:], dr["gl"], sem="GSM"))
    small.append(kb.dma("gpsimd", gh[:], dr["gh"], sem="GSM"))
    small.append(kb.dma("gpsimd", qrt[:], dr["qrt"], sem="GSM"))
    kb.emit(VE, lambda e: e.memset(cc0[:], 0.0))
    kb.emit(VE, lambda e: e.memset(cc1[:], 1.0))
    kb.emit(VE, lambda e: e.memset(ccm32[:], -32.0))
    kb.emit(VE, lambda e: e.memset(cln32[:], 3.4657359027997265))
    kb.emit(VE, lambda e: e.memset(ones128[:], 1.0))
    kb.emit(VE, lambda e: e.memset(onescol[:], 1.0))
    ve_cc = kb.mark(VE)
    kb.wait(VE, small[-1])
    kb.emit(VE, lambda e: e.tensor_add(out=cosT[:], in0=cosT[:], in1=stg[:]))
    kb.emit(VE, lambda e: e.tensor_add(out=sinT[:], in0=sinT[:], in1=stg2[:]))
    kb.wait(SE, small[-1])
    kb.wait(SE, ve_cc)
    kb.emit(SE, lambda e: e.activation(out=ivr[:], in_=lxi[:], func=AF.Exp,
                                       scale=-1.0))
    se_ivr = kb.mark(SE)
    kb.wait(VE, se_ivr)
    kb.emit(VE, lambda e: e.tensor_scalar(out=nivr[:], in0=ivr[:], scalar1=-1.0,
                                          scalar2=None, op0=OP.mult))
    kb.selfsync(VE)
    kb.wait(VE, t_x[-1])
    for t in range(8):
        kb.emit(VE, lambda e, t=t: e.tensor_copy(out=xB[:, t, :], in_=xT[:, t, :]))
    ve_init = kb.mark(VE)
    kb.wait(TE, ve_init)
    kb.emit(TE, lambda e: e.matmul(out=PAV[:, 0, 0:16], lhsT=ones128[:],
                                   rhs=ivr[:], start=True, stop=True))
    te_ivb = kb.mark(TE)
    kb.wait(VE, te_ivb)
    kb.emit(VE, lambda e: e.tensor_copy(out=ivb[:], in_=PAV[:, 0, 0:16]))
    kb.selfsync(VE)
    for h in range(1, 16, 2):
        kb.emit(VE, lambda e, h=h: e.tensor_scalar(
            out=biasH[:, h // 2, :], in0=JJ[:], scalar1=ivb[:, h:h+1], scalar2=None,
            op0=OP.mult))
    kb.emit(VE, lambda e: e.memset(VA[:, :, :, 64:66], 1.0))
    kb.emit(VE, lambda e: e.tensor_copy(out=qrtB[:], in_=qrt[:]))
    kb.barrier()

    RUN1 = not SKIP1
    SCALE = 0.125
    shuffle_mask = list(range(16, 32)) + list(range(16))
    last_cast = {0: None, 1: None}
    prev_pair_qk = None
    prev_last_exp = None

    def load_pair_w(colblk, par, sem):
        if last_cast[par] is not None:
            kb.wait("sync", last_cast[par])
        return [kb.dma("sync", wqf[:, par, dt, :], dr["qkvw_t"][colblk, dt], sem=sem)
                for dt in range(8)]

    def cast_pair_w(toks, par):
        kb.wait(VE, toks[-1])
        if last_cast[par] is not None:
            kb.wait(VE, last_cast[par])
        kb.emit(VE, lambda e, par=par: e.tensor_copy(
            out=wqb[:, par, :, :], in_=wqf[:, par, :, :]))
        return kb.mark(VE)

    def proj_pair(par, cast_tok, out_stg, scale, wait_se=None):
        kb.wait(TE, cast_tok)
        marks = []
        for j in range(4):
            use_bank(j)
            for dt in range(8):
                kb.emit(TE, lambda e, j=j, dt=dt, par=par: e.matmul(
                    out=C[j][:], lhsT=wqb[:, par, dt, :],
                    rhs=xB[:, dt, 512*j:512*(j+1)],
                    start=(dt == 0), stop=(dt == 7)))
            marks.append(kb.mark(TE))
        if wait_se is not None:
            kb.wait(SE, wait_se)
        for j in range(4):
            kb.wait(SE, marks[j])
            kb.emit(SE, lambda e, j=j, o=out_stg, sc=scale: e.activation(
                out=o[:, 512*j:512*(j+1)], in_=C[j][:], func=AF.Copy, scale=sc))
            bankfree[j] = kb.mark(SE)
        return kb.mark(SE), marks[-1]

    def rope(src, dst_bf):
        kb.selfsync(VE)
        kb.emit(VE, lambda e, s=src: e.stream_shuffle(out=stg2[:], in_=s[:],
                                                      mask=shuffle_mask))
        kb.selfsync(VE)
        kb.emit(VE, lambda e, s=src: e.tensor_mul(out=s[:], in0=s[:], in1=cosT[:]))
        kb.emit(VE, lambda e: e.tensor_mul(out=stg2[:], in0=stg2[:], in1=sinT[:]))
        kb.selfsync(VE)
        kb.emit(VE, lambda e, s=src, d=dst_bf: e.tensor_add(out=d[:], in0=s[:],
                                                            in1=stg2[:]))
        return kb.mark(VE)

    prev_rope_q = None
    prev_rope_k = None
    prev_raw = None
    for p in range(8 if RUN1 else 0):
        roped = (p % 2) == 1
        h_even, h_odd = 2 * p, 2 * p + 1
        tq = load_pair_w(p, 0, "LWQ")
        tk = load_pair_w(8 + p, 1, "LWK")
        cq = cast_pair_w(tq, 0)
        ck = cast_pair_w(tk, 1)
        se_q, te_q = proj_pair(0, cq, stg, SCALE, wait_se=prev_rope_k)
        last_cast[0] = te_q
        if DUMP == "proj0" and p == 0:
            return dump([(0, stg)])
        kb.wait(VE, se_q)
        if prev_pair_qk is not None:
            kb.wait(VE, prev_pair_qk)
        if roped:
            ve_q = rope(stg, SBQ)
        else:
            kb.emit(VE, lambda e: e.tensor_copy(out=SBQ[:], in_=stg[:]))
            ve_q = kb.mark(VE)
        prev_rope_q = ve_q
        se_k, te_k = proj_pair(1, ck, stg, 1.0, wait_se=ve_q)
        last_cast[1] = te_k
        kb.wait(VE, se_k)
        if roped:
            ve_k = rope(stg, SBK)
        else:
            kb.emit(VE, lambda e: e.tensor_copy(out=SBK[:], in_=stg[:]))
            ve_k = kb.mark(VE)
        prev_rope_k = ve_k
        kb.wait("gpsimd", ve_q)
        if prev_pair_qk is not None:
            kb.wait("gpsimd", prev_pair_qk)
        g_q = kb.dma("gpsimd", QX[0:64, :], SBQ[64:128, :])
        kb.wait("gpsimd", ve_k)
        g_k = kb.dma("gpsimd", KX[0:64, :], SBK[64:128, :])
        kb.emit("gpsimd", lambda e: e.iota(
            QX[64:65, :].rearrange("p (a b) -> p a b", a=16, b=128),
            pattern=[[128, 16], [0, 128]], channel_multiplier=0,
            allow_small_or_imprecise_dtypes=True))
        g_i = kb.mark("gpsimd")
        kb.wait(VE, g_i)
        kb.wait(VE, se_ivr)
        kb.emit(VE, lambda e, h=h_odd: e.tensor_scalar(
            out=QX[64:65, :], in0=QX[64:65, :], scalar1=nivr[0:1, h:h+1],
            scalar2=-32.0, op0=OP.mult, op1=OP.add))
        kb.emit(VE, lambda e: e.memset(KX[64:65, :], 1.0))
        ve_x = kb.mark(VE)

        if DUMP == "sbqk0" and p == 0:
            return dump([(0, SBQ), (128, SBK)])
        if DUMP == "sbqk1" and p == 1:
            return dump([(0, SBQ), (128, SBK)])
        if DUMP == "qxkx1" and p == 1:
            kb.wait("gpsimd", ve_x); kb.wait("gpsimd", g_k)
            return dump([(0, QX), (128, KX)])
        tv = load_pair_w(16 + p, 0, "LWV")
        cv = cast_pair_w(tv, 0)
        kb.wait(TE, cv)
        if prev_raw is not None:
            kb.wait(TE, prev_raw)
        last_v_te = None
        for g in range(4):
            for k in range(4):
                s16 = 4 * g + k
                for dt in range(8):
                    kb.emit(TE, lambda e, s16=s16, k=k, dt=dt: e.matmul(
                        out=PAV[:, k, :], lhsT=xB[:, dt, 128*s16:128*(s16+1)],
                        rhs=wqb[:, 0, dt, :], start=(dt == 0), stop=(dt == 7)))
            m = kb.mark(TE)
            kb.wait(VE, m)
            for k in range(4):
                s16 = 4 * g + k
                kb.emit(VE, lambda e, s16=s16, k=k, h=h_even: e.tensor_copy(
                    out=VA[:, s16, 0, 0:64], in_=PAV[:, k, 0:64]))
                kb.emit(VE, lambda e, s16=s16, k=k, h=h_odd: e.tensor_copy(
                    out=VA[:, s16, 1, 0:64], in_=PAV[:, k, 64:128]))
            ve_m = kb.mark(VE)
            kb.wait(TE, ve_m)
        last_cast[0] = kb.mark(TE)
        ve_v = ve_m

        if DUMP == "va0" and p == 0:
            return dump([(0, VA[:, :, 0, :].rearrange("p a b -> p (a b)")),
                         (128, VA[:, :, 1, :].rearrange("p a b -> p (a b)"))])
        for hh in range(2):
            h = 2 * p + hh
            odd = hh == 1
            if odd:
                kb.wait(TE, g_k); kb.wait(TE, ve_x)
            else:
                kb.wait(TE, ve_q); kb.wait(TE, ve_k)
            kb.wait(TE, ve_v)
            if prev_last_exp is not None:
                kb.wait(TE, prev_last_exp)
            exp_marks = {}
            av_done = {}
            for t in range(16):
                L = 2048 - 128 * t
                nch = (L + 511) // 512
                par = t % 2
                qk_marks = []
                for j in range(nch):
                    cl = min(512, L - 512 * j)
                    use_bank(j)
                    if odd:
                        kb.emit(TE, lambda e, t=t, j=j, cl=cl: e.matmul(
                            out=C[j][:, 0:cl], lhsT=KX[0:65, 128*t:128*(t+1)],
                            rhs=QX[0:65, 128*t+512*j:128*t+512*j+cl],
                            start=True, stop=True))
                    else:
                        kb.emit(TE, lambda e, t=t, j=j, cl=cl: e.matmul(
                            out=C[j][:, 0:cl], lhsT=SBK[0:64, 128*t:128*(t+1)],
                            rhs=SBQ[0:64, 128*t+512*j:128*t+512*j+cl],
                            start=True, stop=True))
                    qk_marks.append(kb.mark(TE))
                for j in range(nch):
                    cl = min(512, L - 512 * j)
                    kb.wait(SE, qk_marks[j])
                    if t >= 2 and j == 0:
                        kb.wait(SE, av_done[t - 2])
                    if odd:
                        kb.emit(SE, lambda e, t=t, j=j, cl=cl, par=par, h=h:
                                e.activation(
                            out=PR[:, par, 128*t+512*j:128*t+512*j+cl],
                            in_=C[j][:, 0:cl], func=AF.Exp,
                            bias=biasH[:, h // 2, t:t+1]))
                    else:
                        kb.emit(SE, lambda e, t=t, j=j, cl=cl, par=par:
                                e.activation(
                            out=PR[:, par, 128*t+512*j:128*t+512*j+cl],
                            in_=C[j][:, 0:cl], func=AF.Exp, bias=-32.0))
                    exp_marks[(t, j)] = kb.mark(SE)
                    bankfree[j] = exp_marks[(t, j)]
                kb.wait(VE, exp_marks[(t, 0)])
                kb.emit(VE, lambda e, t=t, par=par: e.tensor_mul(
                    out=PR[:, par, 128*t:128*(t+1)],
                    in0=PR[:, par, 128*t:128*(t+1)], in1=triu[:]))
                mask_m = kb.mark(VE)
                for s2 in range(t, 16):
                    j = (128 * (s2 - t)) // 512
                    kb.wait(TE, exp_marks[(t, j)])
                    if s2 == t:
                        kb.wait(TE, mask_m)
                    kb.emit(TE, lambda e, t=t, s2=s2, par=par, hh=hh: e.matmul(
                        out=PAV[:, s2, 0:66], lhsT=PR[:, par, 128*s2:128*(s2+1)],
                        rhs=VA[:, t, hh, :],
                        start=(t == 0 and s2 % 4 == 0),
                        stop=(t == s2 and s2 % 4 == 3)))
                av_done[t] = kb.mark(TE)
            kb.wait(VE, av_done[15])
            for s2 in range(16):
                kb.emit(VE, lambda e, s2=s2: e.tensor_copy(
                    out=RAW[:, s2, :], in_=PAV[:, s2, 0:66]))
            raw_m = kb.mark(VE)
            prev_raw = raw_m
            kb.wait(TE, raw_m)
            kb.selfsync(VE)
            kb.emit(VE, lambda e: e.reciprocal(out=RDEN[:], in_=RAW[:, :, 64]))
            kb.selfsync(VE)
            for s2 in range(16):
                kb.emit(VE, lambda e, s2=s2: e.tensor_scalar(
                    out=RAW[:, s2, 0:64], in0=RAW[:, s2, 0:64],
                    scalar1=RDEN[:, s2:s2+1], scalar2=None, op0=OP.mult))
            norm_m = kb.mark(VE)
            kb.wait(VE, norm_m)
            kb.wait(TE, norm_m)
            prow = 64 * (h % 2)
            ptile = h // 2
            for s2 in range(16):
                k = s2 % 4
                use_bank(k)
                if odd:
                    kb.emit(VE, lambda e, s2=s2: e.tensor_copy(
                        out=PADB[:, 64:128], in_=RAW[:, s2, 0:64]))
                    ve_p = kb.mark(VE)
                    kb.wait(TE, ve_p)
                    kb.emit(TE, lambda e, k=k: e.transpose(
                        out=C[k][:, 0:128], in_=PADB[:], identity=idf[:]))
                else:
                    kb.emit(TE, lambda e, s2=s2, k=k: e.transpose(
                        out=C[k][0:64, 0:128], in_=RAW[:, s2, 0:64],
                        identity=idf[:]))
                tm2 = kb.mark(TE)
                kb.wait(VE, tm2)
                kb.emit(VE, lambda e, s2=s2, k=k, prow=prow, ptile=ptile:
                        e.tensor_copy(
                    out=ATTF[prow:prow+64, ptile, 128*s2:128*(s2+1)],
                    in_=C[k][prow:prow+64, 0:128]))
                bankfree[k] = kb.mark(VE)
            kb.wait(SE, raw_m)
            prev_last_exp = exp_marks[(15, 0)]
            if DUMP == f"raw{h}":
                return dump([(0, RAW.rearrange("p a b -> p (a b)"))])
        prev_pair_qk = av_done[15]
        if DUMP == f"pair{p}":
            return dump([(0, ATTF[:, p, :])])

    kb.barrier()

    def transpose4(src_fn, dst_fn, idmat, n=8):
        for grp in range((n + 3) // 4):
            tg = []
            for k in range(min(4, n - 4 * grp)):
                kk = 4 * grp + k
                use_bank(k)
                kb.emit(TE, lambda e, kk=kk, k=k: e.transpose(
                    out=C[k][:, 0:128], in_=src_fn(kk), identity=idmat[:]))
                tg.append(kb.mark(TE))
            for k in range(min(4, n - 4 * grp)):
                kk = 4 * grp + k
                kb.wait(VE, tg[k])
                kb.emit(VE, lambda e, kk=kk, k=k: e.tensor_copy(
                    out=dst_fn(kk), in_=C[k][:, 0:128]))
                bankfree[k] = kb.mark(VE)
            ve_t = kb.mark(VE)
        return ve_t

    def ln_slice(s2, write_xtok_cb=None):
        transpose4(lambda kk: SUMT[:, kk, :],
                   lambda kk: XTOK[:, 128*kk:128*(kk+1)], idf)
        ln_slice_stats(s2, write_xtok_cb)

    def ln3_slice(s2):
        kb.selfsync(VE)
        kb.emit(VE, lambda e: e.reduce_sum(out=colsc[:, 0:1], in_=XTOK[:], axis=AX))
        kb.selfsync(VE)
        kb.emit(VE, lambda e: e.tensor_scalar(out=colsc[:, 1:2], in0=colsc[:, 0:1],
            scalar1=1.0/1024.0, scalar2=None, op0=OP.mult))
        kb.selfsync(VE)
        kb.emit(VE, lambda e: e.tensor_scalar(out=XC[:], in0=XTOK[:],
            scalar1=colsc[:, 1:2], scalar2=None, op0=OP.subtract))
        ve_c = kb.mark(VE)
        kb.wait(SE, ve_c)
        kb.emit(SE, lambda e: e.activation(out=SQ[:], in_=XC[:], func=AF.Square))
        se_sq = kb.mark(SE)
        kb.wait(VE, se_sq)
        kb.emit(VE, lambda e: e.reduce_sum(out=colsc[:, 2:3], in_=SQ[:], axis=AX))
        kb.selfsync(VE)
        kb.emit(VE, lambda e: e.tensor_scalar(out=colsc[:, 2:3], in0=colsc[:, 2:3],
            scalar1=1.0/1024.0, scalar2=1e-5, op0=OP.mult, op1=OP.add))
        ve_v2 = kb.mark(VE)
        kb.wait(SE, ve_v2)
        kb.emit(SE, lambda e: e.activation(out=colsc[:, 3:4], in_=colsc[:, 2:3],
                                           func=AF.Ln))
        kb.selfsync(SE)
        kb.emit(SE, lambda e: e.activation(out=colsc[:, 3:4], in_=colsc[:, 3:4],
                                           func=AF.Exp, scale=-0.5))
        se_r = kb.mark(SE)
        kb.wait(VE, se_r)
        kb.emit(VE, lambda e: e.tensor_scalar(out=XC[:], in0=XC[:],
            scalar1=colsc[:, 3:4], scalar2=None, op0=OP.mult))
        kb.selfsync(VE)
        kb.emit(VE, lambda e: e.reduce_max(out=colsc[:, 0:1], in_=XC[:], axis=AX,
                                           apply_absolute_value=True))
        kb.selfsync(VE)
        kb.emit(VE, lambda e: e.tensor_scalar_max(out=colsc[:, 0:1],
            in0=colsc[:, 0:1], scalar1=1e-5))
        kb.selfsync(VE)
        kb.emit(VE, lambda e: e.reciprocal(out=colsc[:, 1:2], in_=colsc[:, 0:1]))
        kb.selfsync(VE)
        kb.emit(VE, lambda e, s2=s2: e.tensor_scalar(out=SXC[:, s2:s2+1],
            in0=colsc[:, 1:2], scalar1=127.0, scalar2=None, op0=OP.mult))
        ve_x3 = kb.mark(VE)
        kb.wait("gpsimd", ve_x3)
        tok = kb.dma("gpsimd", dr["x3s"][128*s2:128*(s2+1), :], XC[:], sem="GAS")
        kb.wait(VE, tok)

    def ln_slice_stats(s2, write_xtok_cb=None):
        kb.selfsync(VE)
        kb.emit(VE, lambda e: e.reduce_sum(out=colsc[:, 0:1], in_=XTOK[:], axis=AX))
        kb.selfsync(VE)
        kb.emit(VE, lambda e: e.tensor_scalar(out=colsc[:, 1:2], in0=colsc[:, 0:1],
            scalar1=1.0/1024.0, scalar2=None, op0=OP.mult))
        kb.selfsync(VE)
        kb.emit(VE, lambda e: e.tensor_scalar(out=XC[:], in0=XTOK[:],
            scalar1=colsc[:, 1:2], scalar2=None, op0=OP.subtract))
        ve_c = kb.mark(VE)
        kb.wait(SE, ve_c)
        kb.emit(SE, lambda e: e.activation(out=SQ[:], in_=XC[:], func=AF.Square))
        se_sq = kb.mark(SE)
        kb.wait(VE, se_sq)
        kb.emit(VE, lambda e: e.reduce_sum(out=colsc[:, 2:3], in_=SQ[:], axis=AX))
        kb.selfsync(VE)
        kb.emit(VE, lambda e: e.tensor_scalar(out=colsc[:, 2:3], in0=colsc[:, 2:3],
            scalar1=1.0/1024.0, scalar2=1e-5, op0=OP.mult, op1=OP.add))
        ve_v2 = kb.mark(VE)
        kb.wait(SE, ve_v2)
        kb.emit(SE, lambda e: e.activation(out=colsc[:, 3:4], in_=colsc[:, 2:3],
                                           func=AF.Ln))
        kb.selfsync(SE)
        kb.emit(SE, lambda e: e.activation(out=colsc[:, 3:4], in_=colsc[:, 3:4],
                                           func=AF.Exp, scale=-0.5))
        se_r = kb.mark(SE)
        kb.wait(VE, se_r)
        kb.emit(VE, lambda e: e.tensor_scalar(out=XC[:], in0=XC[:],
            scalar1=colsc[:, 3:4], scalar2=None, op0=OP.mult))
        if write_xtok_cb is not None:
            write_xtok_cb()
        ve_n = kb.mark(VE)
        kb.wait(TE, ve_n)
        transpose4(lambda kk: XC[:, 128*kk:128*(kk+1)],
                   lambda kk: xT[:, kk, 128*s2:128*(s2+1)], idf)

    for t in range(8 if RUN1 else 0):
        tok = kb.dma("sync", WOF[:], dr["wo"][128*t:128*(t+1), :], sem="LWO")
        kb.wait(VE, tok)
        kb.emit(VE, lambda e, t=t: e.tensor_copy(out=woB[:, t, :], in_=WOF[:]))
        ve_w = kb.mark(VE)
        kb.wait("sync", ve_w)
    if RUN1:
        kb.wait(TE, ve_w)

    if DUMP == "attf":
        return dump([(128*t2, ATTF[:, t2, :]) for t2 in range(8)])
    for s2 in range(16 if RUN1 else 0):
        for dt in range(8):
            for ht in range(8):
                kb.emit(TE, lambda e, dt=dt, ht=ht, s2=s2: e.matmul(
                    out=PAV[:, dt, :], lhsT=woB[:, ht, 128*dt:128*(dt+1)],
                    rhs=ATTF[:, ht, 128*s2:128*(s2+1)],
                    start=(ht == 0), stop=(ht == 7)))
        wo_m = kb.mark(TE)
        kb.wait(VE, wo_m)
        for dt in range(8):
            kb.emit(VE, lambda e, dt=dt, s2=s2: e.tensor_add(
                out=SUMT[:, dt, :], in0=PAV[:, dt, :],
                in1=xT[:, dt, 128*s2:128*(s2+1)]))
        sum_m = kb.mark(VE)
        kb.wait(TE, sum_m)
        ln_slice(s2)
    kb.barrier()

    for t in range(8):
        kb.emit(VE, lambda e, t=t: e.tensor_copy(out=xB[:, t, :], in_=xT[:, t, :]))
    kb.barrier()

    if DBG_STAGE == 1:
        st = [kb.dma("gpsimd", dr["out"][128*t:128*(t+1), :], xT[:, t, :], sem="GOUT")
              for t in range(8)]
        kb.wait("gpsimd", st[-1])
        return finish_emit(nc, es, kb)

    Rb = COSS[:, 0, :]
    MB = COSS[:, 1, :]
    rrow = COSS[0:1, 0, :]
    srow = COSS[0:1, 1, :]
    t1row = stg[0:1, :]
    erow = stg2[0:1, :]
    sq_marks = []
    for dt in range(8):
        if dt > 0:
            kb.wait(SE, sq_marks[-1])
        kb.emit(SE, lambda e, dt=dt: e.activation(out=stg2[:], in_=xT[:, dt, :],
                                                  func=AF.Square))
        m_sq = kb.mark(SE)
        kb.wait(TE, m_sq)
        for j in range(4):
            kb.emit(TE, lambda e, dt=dt, j=j: e.matmul(
                out=C[j][0:1, 0:512], lhsT=onescol[:], rhs=stg2[:, 512*j:512*(j+1)],
                start=(dt == 0), stop=(dt == 7)))
        sq_marks.append(kb.mark(TE))
    kb.wait(SE, sq_marks[-1])
    for j in range(4):
        kb.emit(SE, lambda e, j=j: e.activation(
            out=erow[:, 512*j:512*(j+1)], in_=C[j][0:1, 0:512], func=AF.Ln))
    kb.selfsync(SE)
    for j in range(4):
        kb.emit(SE, lambda e, j=j: e.activation(
            out=rrow[:, 512*j:512*(j+1)], in_=erow[:, 512*j:512*(j+1)], func=AF.Exp,
            scale=-0.5))
    kb.selfsync(SE)
    for j in range(4):
        kb.emit(SE, lambda e, j=j: e.activation(
            out=rrow[:, 512*j:512*(j+1)], in_=rrow[:, 512*j:512*(j+1)],
            func=AF.Copy, scale=32.0))
    se_r2 = kb.mark(SE)
    for j in range(4):
        bankfree[j] = se_r2
    kb.wait(TE, se_r2)
    rb_m = []
    for j in range(4):
        kb.emit(TE, lambda e, j=j: e.matmul(
            out=C[j][:, 0:512], lhsT=ones128[:], rhs=rrow[:, 512*j:512*(j+1)],
            start=True, stop=True))
        rb_m.append(kb.mark(TE))
    for j in range(4):
        kb.wait(VE, rb_m[j])
        kb.emit(VE, lambda e, j=j: e.tensor_copy(out=Rb[:, 512*j:512*(j+1)],
                                                 in_=C[j][:, 0:512]))
        bankfree[j] = kb.mark(VE)
    ve_rb = kb.mark(VE)
    kb.wait("gpsimd", ve_rb)
    g_rc0 = kb.dma("gpsimd", PADB[0:16, 0:128], rrow[0:1, :])
    kb.wait(TE, g_rc0)
    kb.emit(TE, lambda e: e.transpose(out=PAV[:, 2, 0:16], in_=PADB[0:16, 0:128],
                                      identity=idf[0:16, 0:16]))
    m_rc = kb.mark(TE)
    kb.wait(VE, m_rc)
    kb.emit(VE, lambda e: e.tensor_copy(out=rcol[:], in_=PAV[:, 2, 0:16]))
    g_rc = kb.mark(VE)
    if DUMP == "rc0":
        kb.wait(VE, g_rc0)
        return dump([(0, PADB)])
    if DUMP == "rc1":
        return dump([(0, rcol)])
    if DUMP == "rms":
        return dump([(0, Rb)])
    s_marks = []
    for j in range(4):
        use_bank(j)
        for dt in range(8):
            kb.emit(TE, lambda e, j=j, dt=dt: e.matmul(
                out=C[j][0:1, 0:512], lhsT=qrtB[:, dt:dt+1],
                rhs=xB[:, dt, 512*j:512*(j+1)],
                start=(dt == 0), stop=(dt == 7)))
        s_marks.append(kb.mark(TE))
    for j in range(4):
        kb.wait(VE, s_marks[j])
        kb.emit(VE, lambda e, j=j: e.tensor_copy(
            out=srow[:, 512*j:512*(j+1)], in_=C[j][0:1, 0:512]))
        bankfree[j] = kb.mark(VE)
    kb.selfsync(VE)

    if DUMP == "srowd":
        return dump([(0, srow)])
    LOGK = float(np.log(np.float64(1152.0)))
    LOGK_HI = float(np.float32(LOGK)); LOGK_LO = float(np.float32(LOGK - np.float64(LOGK_HI)))
    A1 = float(np.float32(np.float32(LOGK_HI) + np.float32(
        np.float32(-np.log(np.float64(2048.0))))))
    cells = colsc
    c_nega, c_mt, c_tmp, c_lnv = [cells[0:1, k:k+1] for k in range(4)]
    c_maxs = rcells[0:1, 0:1]
    c_logk = rcells[0:1, 1:2]
    kb.emit(VE, lambda e: e.memset(c_logk[:], LOGK_HI))
    kb.selfsync(VE)
    kb.emit(VE, lambda e: e.tensor_scalar(out=c_logk[:], in0=c_logk[:],
        scalar1=LOGK_LO, scalar2=None, op0=OP.add))
    kb.emit(VE, lambda e: e.memset(c_nega[:], -A1))
    kb.emit(VE, lambda e: e.reduce_max(out=c_maxs[:], in_=srow[:], axis=AX))
    kb.selfsync(VE)

    route_state = {"iter": 0, "done": False}
    def route_step():
        k = route_state["iter"]
        if k >= 49 or route_state["done"]:
            return
        route_state["iter"] += 1
        kb.selfsync(VE)
        kb.emit(VE, lambda e: e.tensor_scalar(out=t1row[:], in0=srow[:],
            scalar1=c_nega[:], scalar2=None, op0=OP.min))
        kb.emit(VE, lambda e: e.tensor_scalar(out=c_mt[:], in0=c_nega[:],
            scalar1=c_maxs[:], scalar2=-1.0, op0=OP.min, op1=OP.mult))
        m1 = kb.mark(VE)
        kb.wait(SE, m1)
        kb.emit(SE, lambda e: e.activation(out=erow[:], in_=t1row[:],
            func=AF.Exp, bias=c_mt[:]))
        m2 = kb.mark(SE)
        kb.wait(VE, m2)
        kb.emit(VE, lambda e: e.reduce_sum(out=c_tmp[:], in_=erow[:], axis=AX))
        m3 = kb.mark(VE)
        kb.wait(SE, m3)
        kb.emit(SE, lambda e: e.activation(out=c_lnv[:], in_=c_tmp[:], func=AF.Ln))
        m4 = kb.mark(SE)
        kb.wait(VE, m4)
        kb.emit(VE, lambda e: e.tensor_tensor(out=c_tmp[:], in0=c_logk[:],
            in1=c_mt[:], op=OP.add))
        kb.selfsync(VE)
        kb.emit(VE, lambda e: e.tensor_tensor(out=c_tmp[:], in0=c_tmp[:],
            in1=c_lnv[:], op=OP.subtract))
        kb.selfsync(VE)
        kb.emit(VE, lambda e: e.tensor_scalar(out=c_nega[:], in0=c_tmp[:],
            scalar1=-1.0, scalar2=None, op0=OP.mult))

    def route_finish():
        while route_state["iter"] < 49:
            route_step()
        route_state["done"] = True
        kb.selfsync(VE)
        kb.emit(VE, lambda e: e.tensor_scalar(out=t1row[:], in0=srow[:],
            scalar1=c_nega[:], scalar2=0.0, op0=OP.subtract, op1=OP.is_ge))
        kb.selfsync(VE)
        kb.emit(VE, lambda e: e.tensor_tensor_scan(out=erow[:], data0=t1row[:],
            data1=t1row[:], initial=0.0, op0=OP.add, op1=OP.bypass))
        kb.selfsync(VE)
        kb.emit(VE, lambda e: e.tensor_scalar(out=erow[:], in0=erow[:],
            scalar1=1024.5, scalar2=None, op0=OP.is_le))
        kb.selfsync(VE)
        kb.emit(VE, lambda e: e.tensor_mul(out=srow[:], in0=t1row[:], in1=erow[:]))
        return kb.mark(VE)

    if DUMP == "maskfast":
        mask_ve = route_finish()
        kb.wait("gpsimd", mask_ve)
        return dump([(0, srow)])
    def cast_pair_g(toks, par, gcol):
        kb.wait(VE, toks[-1])
        if last_cast[par] is not None:
            kb.wait(VE, last_cast[par])
        for dt in range(8):
            kb.emit(VE, lambda e, par=par, dt=dt, g=gcol: e.tensor_scalar(
                out=wqb[:, par, dt, :], in0=wqf[:, par, dt, :],
                scalar1=g[:, dt:dt+1], scalar2=None, op0=OP.mult))
        return kb.mark(VE)

    def load_blk(colblk, par, sem):
        if last_cast[par] is not None:
            kb.wait("sync", last_cast[par])
        return [kb.dma("sync", wqf[:, par, dt, :], dr["lhw_t"][colblk, dt], sem=sem)
                for dt in range(8)]

    def proj_pair2(par, cast_tok, dst_bf, qscale):
        kb.wait(TE, cast_tok)
        marks = []
        for j in range(4):
            use_bank(j)
            for dt in range(8):
                kb.emit(TE, lambda e, j=j, dt=dt, par=par: e.matmul(
                    out=C[j][:], lhsT=wqb[:, par, dt, :],
                    rhs=xB[:, dt, 512*j:512*(j+1)],
                    start=(dt == 0), stop=(dt == 7)))
            marks.append(kb.mark(TE))
        for j in range(4):
            kb.wait(VE, marks[j])
            if qscale is None:
                kb.emit(VE, lambda e, j=j, d=dst_bf: e.tensor_mul(
                    out=d[:, 512*j:512*(j+1)], in0=C[j][:],
                    in1=Rb[:, 512*j:512*(j+1)]))
            else:
                kb.emit(VE, lambda e, j=j, d=dst_bf, qs=qscale:
                        e.scalar_tensor_tensor(
                    out=d[:, 512*j:512*(j+1)], in0=C[j][:], scalar=qs,
                    in1=Rb[:, 512*j:512*(j+1)], op0=OP.mult, op1=OP.mult))
            bankfree[j] = kb.mark(VE)
        return kb.mark(VE), marks[-1]

    def proj_v2(par, cast_tok):
        kb.wait(TE, cast_tok)
        if prev_raw is not None:
            kb.wait(TE, prev_raw)
        for g in range(4):
            for k in range(4):
                s16 = 4 * g + k
                for dt in range(8):
                    kb.emit(TE, lambda e, s16=s16, k=k, dt=dt, par=par: e.matmul(
                        out=PAV[:, k, :], lhsT=xB[:, dt, 128*s16:128*(s16+1)],
                        rhs=wqb[:, par, dt, :], start=(dt == 0), stop=(dt == 7)))
            m = kb.mark(TE)
            kb.wait(VE, m)
            kb.wait(VE, g_rc)
            for k in range(4):
                s16 = 4 * g + k
                kb.emit(VE, lambda e, s16=s16, k=k: e.tensor_scalar(
                    out=VA[:, s16, 0, 0:64], in0=PAV[:, k, 0:64],
                    scalar1=rcol[:, s16:s16+1], scalar2=None, op0=OP.mult))
                kb.emit(VE, lambda e, s16=s16, k=k: e.tensor_scalar(
                    out=VA[:, s16, 1, 0:64], in0=PAV[:, k, 64:128],
                    scalar1=rcol[:, s16:s16+1], scalar2=None, op0=OP.mult))
            ve_m = kb.mark(VE)
            kb.wait(TE, ve_m)
        return kb.mark(TE), ve_m

    def head_tail(h_parity, ptile, prow):
        kb.selfsync(VE)
        kb.emit(VE, lambda e: e.reciprocal(out=RDEN[:], in_=RAW[:, :, 64]))
        kb.selfsync(VE)
        for s2 in range(16):
            kb.emit(VE, lambda e, s2=s2: e.tensor_scalar(
                out=RAW[:, s2, 0:64], in0=RAW[:, s2, 0:64],
                scalar1=RDEN[:, s2:s2+1], scalar2=None, op0=OP.mult))
        norm_m = kb.mark(VE)
        kb.wait(VE, norm_m)
        kb.wait(TE, norm_m)
        for s2 in range(16):
            k = s2 % 4
            use_bank(k)
            if prow:
                kb.emit(VE, lambda e, s2=s2: e.tensor_copy(
                    out=PADB[:, 64:128], in_=RAW[:, s2, 0:64]))
                ve_p = kb.mark(VE)
                kb.wait(TE, ve_p)
                kb.emit(TE, lambda e, k=k: e.transpose(
                    out=C[k][:, 0:128], in_=PADB[:], identity=idf[:]))
            else:
                kb.emit(TE, lambda e, s2=s2, k=k: e.transpose(
                    out=C[k][0:64, 0:128], in_=RAW[:, s2, 0:64],
                    identity=idf[:]))
            tm2 = kb.mark(TE)
            kb.wait(VE, tm2)
            kb.emit(VE, lambda e, s2=s2, k=k, prow=prow, ptile=ptile:
                    e.tensor_copy(
                out=ATTF[prow:prow+64, ptile, 128*s2:128*(s2+1)],
                in_=C[k][prow:prow+64, 0:128]))
            bankfree[k] = kb.mark(VE)
        return kb.mark(VE)

    prev_last_exp = None
    for p4 in range(4):
        tq = load_blk(p4, 0, "LWQ")
        tk = load_blk(4 + p4, 1, "LWK")
        cq = cast_pair_g(tq, 0, gl)
        ck = cast_pair_g(tk, 1, gl)
        se_q, te_q = proj_pair2(0, cq, SBQ, 0.125)
        last_cast[0] = te_q
        se_k, te_k = proj_pair2(1, ck, SBK, None)
        last_cast[1] = te_k
        tv = load_blk(8 + p4, 0, "LWV")
        cv = cast_pair_g(tv, 0, gl)
        te_v, ve_v = proj_v2(0, cv)
        last_cast[0] = te_v
        for hh in range(2):
            h = 2 * p4 + hh
            rows = slice(64 * hh, 64 * hh + 64)
            kb.wait(TE, se_q); kb.wait(TE, se_k); kb.wait(TE, ve_v)
            if prev_last_exp is not None:
                kb.wait(TE, prev_last_exp)
            exp_m = {}
            av_m = {}
            for w in range(16):
                par = w % 2
                kw = w % 4
                use_bank(kw)
                kb.emit(TE, lambda e, w=w, kw=kw, rows=rows: e.matmul(
                    out=C[kw][:, 0:128], lhsT=SBK[rows, 128*w:128*(w+1)],
                    rhs=SBQ[rows, 128*w:128*(w+1)], start=True, stop=True))
                qkm = kb.mark(TE)
                kb.wait(SE, qkm)
                if w >= 2:
                    kb.wait(SE, av_m[w - 2])
                kb.emit(SE, lambda e, w=w, kw=kw, par=par: e.activation(
                    out=PR[:, par, 128*w:128*(w+1)], in_=C[kw][:, 0:128],
                    func=AF.Exp, bias=-32.0))
                exp_m[w] = kb.mark(SE)
                bankfree[kw] = exp_m[w]
                kb.wait(TE, exp_m[w])
                kb.emit(TE, lambda e, w=w, par=par, hh=hh: e.matmul(
                    out=PAV[:, w, 0:66], lhsT=PR[:, par, 128*w:128*(w+1)],
                    rhs=VA[:, w, hh, :], start=True, stop=True))
                av_m[w] = kb.mark(TE)
                if w % 4 == 3:
                    route_step()
            kb.wait(VE, av_m[15])
            for s2 in range(16):
                kb.emit(VE, lambda e, s2=s2: e.tensor_copy(
                    out=RAW[:, s2, :], in_=PAV[:, s2, 0:66]))
            raw_m = kb.mark(VE)
            prev_raw = raw_m
            kb.wait(TE, raw_m)
            head_tail(hh, h // 2, 64 * (h % 2))
            kb.wait(SE, raw_m)
            prev_last_exp = exp_m[15]
    if DUMP == "lattf":
        return dump([(128*t2, ATTF[:, t2, :]) for t2 in range(4)])
    kb.barrier()
    lw_tok = None
    for t4 in range(4):
        tok = kb.dma("sync", WOF[:], dr["wol_t"][t4], sem="LWO")
        kb.wait(VE, tok)
        if lw_tok is not None:
            kb.wait(VE, lw_tok)
        kb.emit(VE, lambda e, t4=t4: e.tensor_copy(out=wolB[:, t4, :], in_=WOF[:]))
        lw_tok = kb.mark(VE)
        kb.wait("sync", lw_tok)
    kb.wait(TE, lw_tok)
    for s2 in range(16):
        for dt in range(8):
            for ht in range(4):
                kb.emit(TE, lambda e, dt=dt, ht=ht, s2=s2: e.matmul(
                    out=PAV[:, dt, :], lhsT=wolB[:, ht, 128*dt:128*(dt+1)],
                    rhs=ATTF[:, ht, 128*s2:128*(s2+1)],
                    start=(ht == 0), stop=(ht == 3)))
        wo_m = kb.mark(TE)
        kb.wait(VE, wo_m)
        for dt in range(8):
            kb.emit(VE, lambda e, dt=dt, s2=s2: e.tensor_add(
                out=xT[:, dt, 128*s2:128*(s2+1)], in0=PAV[:, dt, :],
                in1=xT[:, dt, 128*s2:128*(s2+1)]))
        sum_m = kb.mark(VE)
        kb.wait(TE, sum_m)
        route_step()
    if DUMP == "postlight":
        return dump([(128*t, xT[:, t, :]) for t in range(8)])
    kb.barrier()
    prev_last_exp = None
    for p4 in range(4):
        tq = load_blk(12 + p4, 0, "LWQ")
        tk = load_blk(16 + p4, 1, "LWK")
        cq = cast_pair_g(tq, 0, gh)
        ck = cast_pair_g(tk, 1, gh)
        se_q, te_q = proj_pair2(0, cq, SBQ, 0.125)
        last_cast[0] = te_q
        se_k, te_k = proj_pair2(1, ck, SBK, None)
        last_cast[1] = te_k
        tv = load_blk(20 + p4, 0, "LWV")
        cv = cast_pair_g(tv, 0, gh)
        te_v, ve_v = proj_v2(0, cv)
        last_cast[0] = te_v
        for hh in range(2):
            h = 2 * p4 + hh
            rows = slice(64 * hh, 64 * hh + 64)
            kb.wait(TE, se_q); kb.wait(TE, se_k); kb.wait(TE, ve_v)
            if prev_last_exp is not None:
                kb.wait(TE, prev_last_exp)
            exp_m = {}
            av_done = {}
            for t in range(16):
                par = t % 2
                qk_marks = []
                for j in range(4):
                    use_bank(j)
                    kb.emit(TE, lambda e, t=t, j=j, rows=rows: e.matmul(
                        out=C[j][:], lhsT=SBK[rows, 128*t:128*(t+1)],
                        rhs=SBQ[rows, 512*j:512*(j+1)], start=True, stop=True))
                    qk_marks.append(kb.mark(TE))
                for j in range(4):
                    kb.wait(SE, qk_marks[j])
                    if t >= 2 and j == 0:
                        kb.wait(SE, av_done[t - 2])
                    kb.emit(SE, lambda e, t=t, j=j, par=par: e.activation(
                        out=PR[:, par, 512*j:512*(j+1)], in_=C[j][:],
                        func=AF.Exp, bias=-32.0))
                    exp_m[(t, j)] = kb.mark(SE)
                    bankfree[j] = exp_m[(t, j)]
                for s2 in range(16):
                    j = s2 // 4
                    kb.wait(TE, exp_m[(t, j)])
                    kb.emit(TE, lambda e, t=t, s2=s2, par=par, hh=hh: e.matmul(
                        out=PAV[:, s2, 0:66], lhsT=PR[:, par, 128*s2:128*(s2+1)],
                        rhs=VA[:, t, hh, :],
                        start=(t == 0 and s2 % 4 == 0),
                        stop=(t == 15 and s2 % 4 == 3)))
                av_done[t] = kb.mark(TE)
                if t % 4 == 3:
                    route_step()
            kb.wait(VE, av_done[15])
            for s2 in range(16):
                kb.emit(VE, lambda e, s2=s2: e.tensor_copy(
                    out=RAW[:, s2, :], in_=PAV[:, s2, 0:66]))
            raw_m = kb.mark(VE)
            prev_raw = raw_m
            kb.wait(TE, raw_m)
            head_tail(hh, 4 + h // 2, 64 * (h % 2))
            kb.wait(SE, raw_m)
            prev_last_exp = exp_m[(15, 0)]
    mask_ve = route_finish()
    kb.selfsync(VE)
    kb.wait(TE, mask_ve)
    mb_m = []
    for j in range(4):
        use_bank(j)
        kb.emit(TE, lambda e, j=j: e.matmul(
            out=C[j][:, 0:512], lhsT=ones128[:], rhs=srow[:, 512*j:512*(j+1)],
            start=True, stop=True))
        mb_m.append(kb.mark(TE))
    for j in range(4):
        kb.wait(VE, mb_m[j])
        kb.emit(VE, lambda e, j=j: e.tensor_copy(out=MB[:, 512*j:512*(j+1)],
                                                 in_=C[j][:, 0:512]))
        bankfree[j] = kb.mark(VE)
    ve_mb = kb.mark(VE)
    if DUMP == "maskd":
        kb.wait("gpsimd", ve_mb)
        return dump([(0, MB)])
    if DUMP == "hattf":
        return dump([(128*t2, ATTF[:, 4+t2, :]) for t2 in range(4)])
    kb.barrier()
    hw_tok = None
    for t4 in range(4):
        tok = kb.dma("sync", WOF[:], dr["woh_t"][t4], sem="LWO")
        kb.wait(VE, tok)
        if hw_tok is not None:
            kb.wait(VE, hw_tok)
        kb.emit(VE, lambda e, t4=t4: e.tensor_copy(out=wolB[:, t4, :], in_=WOF[:]))
        hw_tok = kb.mark(VE)
        kb.wait("sync", hw_tok)
    kb.wait(TE, hw_tok)
    for s2 in range(16):
        for dt in range(8):
            for ht in range(4):
                kb.emit(TE, lambda e, dt=dt, ht=ht, s2=s2: e.matmul(
                    out=PAV[:, dt, :], lhsT=wolB[:, ht, 128*dt:128*(dt+1)],
                    rhs=ATTF[:, 4 + ht, 128*s2:128*(s2+1)],
                    start=(ht == 0), stop=(ht == 3)))
        wo_m = kb.mark(TE)
        kb.wait(VE, wo_m)
        for dt in range(8):
            kb.emit(VE, lambda e, dt=dt, s2=s2: e.tensor_mul(
                out=SUMT[:, dt, :], in0=PAV[:, dt, :],
                in1=MB[:, 128*s2:128*(s2+1)]))
        kb.selfsync(VE)
        for dt in range(8):
            kb.emit(VE, lambda e, dt=dt, s2=s2: e.tensor_add(
                out=xT[:, dt, 128*s2:128*(s2+1)], in0=SUMT[:, dt, :],
                in1=xT[:, dt, 128*s2:128*(s2+1)]))
        sum_m = kb.mark(VE)
        kb.wait(TE, sum_m)
    kb.barrier()
    for s2 in range(16):
        transpose4(lambda kk, s2=s2: xT[:, kk, 128*s2:128*(s2+1)],
                   lambda kk: XTOK[:, 128*kk:128*(kk+1)], idf)
        ln_slice_stats(s2)
    kb.barrier()
    for t in range(8):
        kb.emit(VE, lambda e, t=t: e.tensor_copy(out=xB[:, t, :], in_=xT[:, t, :]))
    kb.barrier()

    if DBG_STAGE == 2:
        st = [kb.dma("gpsimd", dr["out"][128*t:128*(t+1), :], xT[:, t, :], sem="GOUT")
              for t in range(8)]
        kb.wait("gpsimd", st[-1])
        return finish_emit(nc, es, kb)

    SXC = rcol
    b24s = carve(ARF, 6144, 24)
    b24f = carve(ARF, 6144 + 24, 24)
    bvr = stg2[0:1, 0:1024]
    x2buf = carve(ARF, 4096, 2048).rearrange("p (a b) -> p a b", a=2, b=1024)
    t_b = kb.dma("gpsimd", b24f[:], dr["b24"], sem="GSM")
    t_bv = kb.dma("gpsimd", bvr[:], dr["bvrow"], sem="GSM")
    kb.wait(VE, t_bv)
    kb.emit(VE, lambda e: e.tensor_scalar(out=b24s[:], in0=b24f[:],
        scalar1=1.0/1024.0, scalar2=None, op0=OP.mult))
    ve_b = kb.mark(VE)
    for s2 in range(16):
        transpose4(lambda kk, s2=s2: xT[:, kk, 128*s2:128*(s2+1)],
                   lambda kk: XTOK[:, 128*kk:128*(kk+1)], idf)
        ve_t = kb.mark(VE)
        kb.wait("gpsimd", ve_t)
        tok = kb.dma("gpsimd", dr["x2s"][128*s2:128*(s2+1), :], XTOK[:], sem="GAS")
        kb.wait(VE, tok)
    kb.barrier()

    def l1_proj_block(blk, par, cast_tok, dst, bias_col, scale):
        kb.wait(TE, cast_tok)
        marks = []
        for j in range(4):
            use_bank(j)
            for dt in range(8):
                kb.emit(TE, lambda e, j=j, dt=dt, par=par: e.matmul(
                    out=C[j][:], lhsT=wqb[:, par, dt, :],
                    rhs=xB[:, dt, 512*j:512*(j+1)],
                    start=(dt == 0), stop=(dt == 7)))
            marks.append(kb.mark(TE))
        for j in range(4):
            kb.wait(VE, marks[j])
            kb.emit(VE, lambda e, j=j, d=dst, bc=bias_col, sc=scale:
                    e.tensor_scalar(
                out=d[:, 512*j:512*(j+1)], in0=C[j][:], scalar1=bc,
                scalar2=sc, op0=OP.add, op1=OP.mult))
            bankfree[j] = kb.mark(VE)
        return kb.mark(VE), marks[-1]

    def load_w1(colblk, par):
        if last_cast[par] is not None:
            kb.wait("sync", last_cast[par])
        return [kb.dma("sync", wqf[:, par, dt, :], dr["W1T_t"][colblk, dt],
                       sem=("LWQ" if par == 0 else "LWK"))
                for dt in range(8)]

    kb.wait(VE, ve_b)
    for blk in range(8):
        par = blk % 2
        t_w = load_w1(blk, par)
        c_w = cast_pair_w(t_w, par)
        se_m, te_m = l1_proj_block(blk, par, c_w, QT[:, blk, :],
                                   b24s[:, blk:blk+1], 1.0/1024.0)
        last_cast[par] = te_m
    for blk in range(8):
        par = blk % 2
        t_w = load_w1(8 + blk, par)
        c_w = cast_pair_w(t_w, par)
        se_m, te_m = l1_proj_block(8 + blk, par, c_w, kt_ap(blk),
                                   b24f[:, 8+blk:8+blk+1], 1.0)
        last_cast[par] = te_m
    kb.barrier()
    prev_exp = None
    for t in range(16):
        qk_marks = []
        for j in range(4):
            use_bank(j)
            for dt in range(8):
                kb.emit(TE, lambda e, t=t, j=j, dt=dt: e.matmul(
                    out=C[j][:], lhsT=kt_ap(dt)[:, 128*t:128*(t+1)],
                    rhs=QT[:, dt, 512*j:512*(j+1)],
                    start=(dt == 0), stop=(dt == 7)))
            qk_marks.append(kb.mark(TE))
        kb.wait(VE, qk_marks[-1])
        for j in range(4):
            kb.emit(VE, lambda e, j=j: e.reduce_max(
                out=colsc[:, j:j+1], in_=C[j][:], axis=AX, negate=True))
        kb.selfsync(VE)
        kb.emit(VE, lambda e: e.tensor_tensor(out=colsc[:, 0:1],
            in0=colsc[:, 0:1], in1=colsc[:, 1:2], op=OP.min))
        kb.emit(VE, lambda e: e.tensor_tensor(out=colsc[:, 2:3],
            in0=colsc[:, 2:3], in1=colsc[:, 3:4], op=OP.min))
        kb.selfsync(VE)
        kb.emit(VE, lambda e: e.tensor_tensor(out=colsc[:, 0:1],
            in0=colsc[:, 0:1], in1=colsc[:, 2:3], op=OP.min))
        nm = kb.mark(VE)
        kb.wait(SE, nm)
        for j in range(4):
            kb.emit(SE, lambda e, t=t, j=j: e.activation(
                out=SBT[:, t, 512*j:512*(j+1)], in_=C[j][:], func=AF.Exp,
                bias=colsc[:, 0:1],
                accum_out=colsc[:, 4+(j % 2):5+(j % 2)] if False else None))
            bankfree[j] = kb.mark(SE)
        se_e = kb.mark(SE)
        kb.wait(VE, se_e)
        kb.selfsync(VE)
        for j in range(4):
            kb.emit(VE, lambda e, t=t, j=j: e.reduce_sum(
                out=colsc[:, j:j+1], in_=SBT[:, t, 512*j:512*(j+1)], axis=AX))
        kb.selfsync(VE)
        kb.emit(VE, lambda e: e.tensor_add(out=colsc[:, 0:1], in0=colsc[:, 0:1],
                                           in1=colsc[:, 1:2]))
        kb.emit(VE, lambda e: e.tensor_add(out=colsc[:, 2:3], in0=colsc[:, 2:3],
                                           in1=colsc[:, 3:4]))
        kb.selfsync(VE)
        kb.emit(VE, lambda e, t=t: e.tensor_add(out=RDEN[:, t:t+1],
            in0=colsc[:, 0:1], in1=colsc[:, 2:3]))
        kb.selfsync(VE)
    kb.selfsync(VE)
    kb.emit(VE, lambda e: e.reciprocal(out=RDEN[:], in_=RDEN[:]))
    kb.selfsync(VE)
    ve_den = kb.mark(VE)
    kb.barrier()
    prev_vproj = None
    for vb in range(8):
        par = vb % 2
        t_w = load_w1(16 + vb, par)
        c_w = cast_pair_w(t_w, par)
        kb.wait(TE, c_w)
        kb.wait(TE, ve_den)
        for g in range(4):
            for k in range(4):
                s16 = 4 * g + k
                for dt in range(8):
                    kb.emit(TE, lambda e, s16=s16, k=k, dt=dt, par=par: e.matmul(
                        out=PAV[:, k, :], lhsT=xB[:, dt, 128*s16:128*(s16+1)],
                        rhs=wqb[:, par, dt, :], start=(dt == 0), stop=False))
                kb.emit(TE, lambda e, s16=s16, k=k, vb=vb: e.matmul(
                    out=PAV[:, k, :], lhsT=ones128[:],
                    rhs=bvr[:, 128*vb:128*(vb+1)], start=False, stop=True))
            m = kb.mark(TE)
            kb.wait(VE, m)
            for k in range(4):
                s16 = 4 * g + k
                kb.emit(VE, lambda e, s16=s16, k=k, vb=vb: e.tensor_scalar(
                    out=vt_ap(s16)[:, 128*vb:128*(vb+1)], in0=PAV[:, k, :],
                    scalar1=RDEN[:, s16:s16+1], scalar2=None, op0=OP.mult))
            ve_m = kb.mark(VE)
            kb.wait(TE, ve_m)
        last_cast[par] = kb.mark(TE)
    kb.barrier()
    x2toks = {}
    x2free = {}
    for s2 in range(2):
        x2toks[s2] = kb.dma("sync", x2buf[:, s2 % 2, :],
                            dr["x2s"][128*s2:128*(s2+1), :],
                            sem=("LWO" if s2 % 2 == 0 else "LWV"))
    for s2 in range(16):
        for half in range(2):
            for jt in range(16):
                kb.emit(TE, lambda e, s2=s2, half=half, jt=jt: e.matmul(
                    out=PAV[:, 4*half:4*half+4, :].rearrange("p a b -> p (a b)"),
                    lhsT=SBT[:, jt, 128*s2:128*(s2+1)],
                    rhs=vt_ap(jt)[:, 512*half:512*(half+1)],
                    start=(jt == 0), stop=(jt == 15)))
        av_m = kb.mark(TE)
        kb.wait(VE, av_m)
        kb.wait(VE, x2toks[s2])
        for half in range(2):
            kb.emit(VE, lambda e, s2=s2, half=half: e.tensor_add(
                out=XTOK[:, 512*half:512*(half+1)],
                in0=PAV[:, 4*half:4*half+4, :].rearrange("p a b -> p (a b)"),
                in1=x2buf[:, s2 % 2, 512*half:512*(half+1)]))
        sum_m = kb.mark(VE)
        x2free[s2] = sum_m
        if s2 + 2 < 16:
            kb.wait("sync", sum_m)
            x2toks[s2 + 2] = kb.dma("sync", x2buf[:, s2 % 2, :],
                dr["x2s"][128*(s2+2):128*(s2+3), :],
                sem=("LWO" if s2 % 2 == 0 else "LWV"))
        kb.wait(TE, sum_m)
        ln3_slice(s2)
    kb.barrier()

    WSTa = carve(ARF, 0, 4096)
    WSTb = CF32[:, 0:4096]
    t_cw = kb.dma("gpsimd", convw4[:], dr["convw"], sem="GSM")
    t_cb = kb.dma("gpsimd", convb4[:], dr["convb"], sem="GSM")
    kb.wait(VE, t_cb)

    def quant_weights(dram_ap, ntiles, F, dst_fn, mcell, scell):
        asum = colsc[:, 5:6]
        kb.emit(VE, lambda e: e.memset(asum[:], 0.0))
        kb.selfsync(VE)
        toks = {}
        vem = {}
        for tt in range(min(2, ntiles)):
            buf = WSTa if tt % 2 == 0 else WSTb
            qq = "sync" if tt % 2 == 0 else "gpsimd"
            toks[tt] = kb.dma(qq, buf[:, 0:F], dram_ap[128*tt:128*(tt+1), :],
                              sem=("LWQ" if tt % 2 == 0 else "GAS"))
        for tt in range(ntiles):
            buf = WSTa if tt % 2 == 0 else WSTb
            kb.wait(VE, toks[tt])
            kb.selfsync(VE)
            kb.emit(VE, lambda e, buf=buf, F=F: e.reduce_sum(out=colsc[:, 0:1],
                in_=buf[:, 0:F], axis=AX, apply_absolute_value=True))
            kb.selfsync(VE)
            kb.emit(VE, lambda e: e.tensor_add(out=asum[:], in0=asum[:],
                                               in1=colsc[:, 0:1]))
            vem[tt] = kb.mark(VE)
            if tt + 2 < ntiles:
                qq = "sync" if tt % 2 == 0 else "gpsimd"
                kb.wait(qq, vem[tt])
                buf2 = WSTa if tt % 2 == 0 else WSTb
                toks[tt + 2] = kb.dma(qq, buf2[:, 0:F],
                    dram_ap[128*(tt+2):128*(tt+3), :],
                    sem=("LWQ" if tt % 2 == 0 else "GAS"))
        kb.selfsync(VE)
        ve_a = kb.mark(VE)
        kb.wait(TE, ve_a)
        kb.emit(TE, lambda e: e.matmul(out=PAV[0:1, 0, 0:1], lhsT=onescol[:],
                                       rhs=asum[:], start=True, stop=True))
        m1 = kb.mark(TE)
        kb.wait(VE, m1)
        N = ntiles * 128 * F
        kb.emit(VE, lambda e, N=N: e.tensor_scalar(out=colsc[0:1, 0:1],
            in0=PAV[0:1, 0, 0:1], scalar1=1.0/N, scalar2=1e-5,
            op0=OP.mult, op1=OP.max))
        kb.selfsync(VE)
        kb.emit(VE, lambda e: e.reciprocal(out=colsc[0:1, 1:2],
                                           in_=colsc[0:1, 0:1]))
        kb.selfsync(VE)
        ve_c = kb.mark(VE)
        kb.wait(TE, ve_c)
        kb.emit(TE, lambda e: e.matmul(out=PAV[:, 1, 0:2], lhsT=ones128[:],
                                       rhs=colsc[0:1, 0:2], start=True, stop=True))
        m2 = kb.mark(TE)
        kb.wait(VE, m2)
        kb.emit(VE, lambda e, mc=mcell: e.tensor_copy(out=C4[:, mc:mc+1],
                                                      in_=PAV[:, 1, 0:1]))
        kb.emit(VE, lambda e, sc=scell: e.tensor_copy(out=C4[:, sc:sc+1],
                                                      in_=PAV[:, 1, 1:2]))
        kb.selfsync(VE)
        kb.wait("sync", ve_a)
        kb.wait("gpsimd", ve_a)
        toks = {}
        for tt in range(min(2, ntiles)):
            buf = WSTa if tt % 2 == 0 else WSTb
            qq = "sync" if tt % 2 == 0 else "gpsimd"
            toks[tt] = kb.dma(qq, buf[:, 0:F], dram_ap[128*tt:128*(tt+1), :],
                              sem=("LWQ" if tt % 2 == 0 else "GAS"))
        for tt in range(ntiles):
            buf = WSTa if tt % 2 == 0 else WSTb
            kb.wait(VE, toks[tt])
            kb.emit(VE, lambda e, tt=tt, buf=buf, F=F, sc=scell: e._custom_dve(
                RNDQ, out=dst_fn(tt), in0=buf[:, 0:F],
                s0=C4[:, sc:sc+1], imm2=MAGIC))
            kb.selfsync(VE)
            kb.emit(VE, lambda e, tt=tt, F=F: e.tensor_scalar(
                out=dst_fn(tt), in0=dst_fn(tt), scalar1=1.0, scalar2=-1.0,
                op0=OP.min, op1=OP.max))
            vq = kb.mark(VE)
            if tt + 2 < ntiles:
                qq = "sync" if tt % 2 == 0 else "gpsimd"
                kb.wait(qq, vq)
                buf2 = WSTa if tt % 2 == 0 else WSTb
                toks[tt + 2] = kb.dma(qq, buf2[:, 0:F],
                    dram_ap[128*(tt+2):128*(tt+3), :],
                    sem=("LWQ" if tt % 2 == 0 else "GAS"))
        return kb.mark(VE)

    if DUMP == "preq":
        return dump([(0, SXC)])
    quant_weights(dr["w1T"], 8, 4096, lambda tt: w1q[:, tt, :], 4, 5)
    kb.barrier()
    if DUMP == "wq1":
        return dump([(128*t2, w1q[:, t2, :]) for t2 in range(8)])
    quant_weights(dr["w2T"], 32, 1024, lambda tt: w2q[:, tt, :], 0, 1)
    kb.barrier()
    if DUMP == "wq2":
        return dump([(0, C4)])

    sq4_free = [None]

    def out_section(o_lo, OL, ioff, t_r, src_lo=0, nrows=None):
        if nrows is None:
            nrows = OL
        for dto in range(8):
            ps = dto % 4
            for ct in range(32):
                kb.emit(TE, lambda e, dto=dto, ct=ct, ps=ps, OL=OL, ioff=ioff:
                        e.matmul(
                    out=PAV[:, 2*ps:2*ps+1, :]
                        .rearrange("p a b -> p (a b)")[:, 0:OL],
                    lhsT=w2q[:, ct, 128*dto:128*(dto+1)],
                    rhs=INTH[:, ct, ioff:ioff+OL],
                    start=(ct == 0), stop=(ct == 31)))
            tm_o = kb.mark(TE)
            kb.wait(VE, tm_o)
            kb.emit(VE, lambda e, ps=ps, OL=OL: e.tensor_copy(
                out=PADB[:, 0:OL],
                in_=PAV[:, 2*ps:2*ps+1, :].rearrange("p a b -> p (a b)")[:, 0:OL]))
            ve_o = kb.mark(VE)
            use_bank(2)
            kb.wait(TE, ve_o)
            kb.emit(TE, lambda e, OL=OL: e.transpose(
                out=C[2][0:OL, 0:128], in_=PADB[:, 0:OL], identity=idf[:]))
            tm_t = kb.mark(TE)
            kb.wait(VE, tm_t)
            if t_r is not None:
                kb.wait(VE, t_r)
            kb.emit(VE, lambda e, dto=dto, OL=OL: e.scalar_tensor_tensor(
                out=XT4[0:OL, 128*dto:128*(dto+1)], in0=C[2][0:OL, 0:128],
                scalar=colsc[0:OL, 2:3], in1=XT4[0:OL, 128*dto:128*(dto+1)],
                op0=OP.mult, op1=OP.add))
            bankfree[2] = kb.mark(VE)
        kb.selfsync(VE)
        kb.emit(VE, lambda e, OL=OL: e.reduce_sum(out=colsc[0:OL, 0:1],
            in_=XT4[0:OL, :], axis=AX))
        kb.selfsync(VE)
        kb.emit(VE, lambda e, OL=OL: e.tensor_scalar(out=colsc[0:OL, 1:2],
            in0=colsc[0:OL, 0:1],
            scalar1=1.0/1024.0, scalar2=None, op0=OP.mult))
        kb.selfsync(VE)
        kb.emit(VE, lambda e, OL=OL: e.tensor_scalar(out=XC4[0:OL, :],
            in0=XT4[0:OL, :], scalar1=colsc[0:OL, 1:2], scalar2=None,
            op0=OP.subtract))
        ve_c4 = kb.mark(VE)
        kb.wait(SE, ve_c4)
        kb.emit(SE, lambda e, OL=OL: e.activation(out=SQ4[0:OL, :],
            in_=XC4[0:OL, :], func=AF.Square))
        se_s4 = kb.mark(SE)
        kb.wait(VE, se_s4)
        kb.emit(VE, lambda e, OL=OL: e.reduce_sum(out=colsc[0:OL, 2:3],
            in_=SQ4[0:OL, :], axis=AX))
        sq4_free[0] = kb.mark(VE)
        kb.selfsync(VE)
        kb.emit(VE, lambda e, OL=OL: e.tensor_scalar(out=colsc[0:OL, 2:3],
            in0=colsc[0:OL, 2:3],
            scalar1=1.0/1024.0, scalar2=1e-5, op0=OP.mult, op1=OP.add))
        ve_v4 = kb.mark(VE)
        kb.wait(SE, ve_v4)
        kb.emit(SE, lambda e, OL=OL: e.activation(out=colsc[0:OL, 3:4],
            in_=colsc[0:OL, 2:3], func=AF.Ln))
        kb.selfsync(SE)
        kb.emit(SE, lambda e, OL=OL: e.activation(out=colsc[0:OL, 3:4],
            in_=colsc[0:OL, 3:4], func=AF.Exp, scale=-0.5))
        se_r4 = kb.mark(SE)
        kb.wait(VE, se_r4)
        kb.emit(VE, lambda e, OL=OL: e.tensor_scalar(out=XC4[0:OL, :],
            in0=XC4[0:OL, :], scalar1=colsc[0:OL, 3:4], scalar2=None,
            op0=OP.mult))
        ve_o4 = kb.mark(VE)
        kb.wait("gpsimd", ve_o4)
        tko = kb.dma("gpsimd", dr["out"][o_lo:o_lo+nrows, :],
                     XC4[src_lo:src_lo+nrows, :], sem="GOUT")
        kb.wait(VE, tko)
        return tko

    def sh_section(moff, OL):
        use_bank(0)
        kb.emit(TE, lambda e, moff=moff, OL=OL: e.transpose(
            out=C[0][0:OL, 0:128], in_=M1B[:, moff:moff+OL], identity=idf[:]))
        tm_m = kb.mark(TE)
        kb.wait(VE, tm_m)
        kb.emit(VE, lambda e, OL=OL: e.reduce_max(out=colsc[0:OL, 0:1],
            in_=C[0][0:OL, 0:128], axis=AX))
        bankfree[0] = kb.mark(VE)
        use_bank(0)
        kb.emit(TE, lambda e, moff=moff, OL=OL: e.transpose(
            out=C[0][0:OL, 0:128], in_=M2B[:, moff:moff+OL], identity=idf[:]))
        tm_m2 = kb.mark(TE)
        kb.wait(VE, tm_m2)
        kb.emit(VE, lambda e, OL=OL: e.reduce_max(out=colsc[0:OL, 3:4],
            in_=C[0][0:OL, 0:128], axis=AX))
        bankfree[0] = kb.mark(VE)
        kb.selfsync(VE)
        kb.emit(VE, lambda e, OL=OL: e.tensor_tensor(out=colsc[0:OL, 0:1],
            in0=colsc[0:OL, 0:1], in1=colsc[0:OL, 3:4], op=OP.max))
        kb.selfsync(VE)
        kb.emit(VE, lambda e, OL=OL: e.tensor_scalar_max(out=colsc[0:OL, 0:1],
            in0=colsc[0:OL, 0:1], scalar1=1e-5))
        kb.selfsync(VE)
        kb.emit(VE, lambda e, OL=OL: e.reciprocal(out=colsc[0:OL, 1:2],
            in_=colsc[0:OL, 0:1]))
        kb.selfsync(VE)
        kb.emit(VE, lambda e, OL=OL: e.tensor_scalar(out=colsc[0:OL, 1:2],
            in0=colsc[0:OL, 1:2], scalar1=127.0, scalar2=None, op0=OP.mult))
        kb.emit(VE, lambda e, OL=OL: e.tensor_scalar(out=colsc[0:OL, 2:3],
            in0=colsc[0:OL, 0:1], scalar1=1.0/127.0, scalar2=None, op0=OP.mult))
        kb.selfsync(VE)
        kb.emit(VE, lambda e, OL=OL: e.tensor_mul(out=colsc[0:OL, 2:3],
            in0=colsc[0:OL, 2:3], in1=C4[0:OL, 0:1]))
        kb.selfsync(VE)
        return kb.mark(VE)

    def shb_tile(OL, src=None):
        if src is None:
            src = colsc[0:OL, 1:2]
        kb.selfsync(VE)
        use_bank(1)
        ve_sh = kb.mark(VE)
        kb.wait(TE, ve_sh)
        kb.emit(TE, lambda e, OL=OL, src=src: e.transpose(
            out=C[1][0:1, 0:OL], in_=src, identity=idf[0:OL, 0:OL]))
        tm_s = kb.mark(TE)
        kb.wait(VE, tm_s)
        kb.emit(VE, lambda e, OL=OL: e.tensor_copy(out=SHB[0:1, 0:OL],
                                                   in_=C[1][0:1, 0:OL]))
        bankfree[1] = kb.mark(VE)
        kb.selfsync(VE)
        ve_r = kb.mark(VE)
        use_bank(1)
        kb.wait(TE, ve_r)
        kb.emit(TE, lambda e, OL=OL: e.matmul(out=C[1][:, 0:OL], lhsT=ones128[:],
            rhs=SHB[0:1, 0:OL], start=True, stop=True))
        tm_b = kb.mark(TE)
        kb.wait(VE, tm_b)
        kb.emit(VE, lambda e, OL=OL: e.tensor_copy(out=SHB[:, 0:OL],
                                                   in_=C[1][:, 0:OL]))
        bankfree[1] = kb.mark(VE)
        kb.selfsync(VE)
        return kb.mark(VE)

    kb.emit(VE, lambda e: e.memset(HBUF[:, :, 0:2], 0.0))
    ve_h0 = kb.mark(VE)
    kb.wait(VE, ve_h0)
    kb.wait(TE, ve_h0)
    last_out = None
    for c in range(16):
        h_lo = 128 * c
        if sq4_free[0] is not None:
            kb.wait("sync", sq4_free[0])
        t_x3 = kb.dma("sync", X3B[:], dr["x3s"][h_lo:h_lo+128, :], sem="LWO")
        kb.wait(VE, t_x3)
        kb.emit(VE, lambda e, c=c: e._custom_dve(RNDQ, out=XC4[:], in0=X3B[:],
            s0=SXC[:, c:c+1], imm2=MAGIC))
        ve_ix = kb.mark(VE)
        kb.wait(VE, ve_ix)
        kb.wait(TE, ve_ix)
        if DUMP == "xc40" and c == 0:
            return dump([(0, XC4)])
        if DUMP == "xc41" and c == 1:
            return dump([(0, XC4), (128, X3B)])
        if DUMP == "w1qt0" and c == 0:
            kb.emit(VE, lambda e: e.tensor_copy(out=X3B[:], in_=w1q[:, 0, 0:1024]))
            kb.selfsync(VE)
            return dump([(0, X3B)])
        for k8 in range(8):
            j = k8 % 4
            use_bank(j)
            kb.emit(TE, lambda e, k8=k8, j=j: e.transpose(
                out=C[j][:, 0:128], in_=XC4[:, 128*k8:128*(k8+1)],
                identity=idf[:]))
            tmx = kb.mark(TE)
            kb.wait(VE, tmx)
            kb.emit(VE, lambda e, k8=k8, j=j: e.tensor_copy(
                out=XTB4[:, k8, :], in_=C[j][:, 0:128]))
            bankfree[j] = kb.mark(VE)
        ve_xt = kb.mark(VE)
        kb.wait(TE, ve_xt)
        if DUMP == "xtb40" and c == 0:
            kb.emit(VE, lambda e: e.tensor_copy(
                out=X3B[:], in_=XTB4.rearrange("p a b -> p (a b)")))
            kb.selfsync(VE)
            return dump([(0, X3B)])
        kb.emit(VE, lambda e: e.memset(M1B[:], 0.0))
        kb.emit(VE, lambda e: e.memset(M2B[:], 0.0))
        kb.emit(VE, lambda e, c=c: e.reciprocal(out=colsc[:, 4:5],
                                                in_=SXC[:, c:c+1]))
        kb.selfsync(VE)
        kb.emit(VE, lambda e: e.tensor_mul(out=colsc[:, 4:5], in0=colsc[:, 4:5],
                                           in1=C4[:, 4:5]))
        kb.selfsync(VE)
        shb_tile(128, colsc[0:128, 4:5])
        ve_hc = kb.mark(VE)
        NC = 129 if c == 15 else 128
        co = 1 if c == 0 else 0
        def absmax(ct, NC):
            kb.emit(VE, lambda e, ct=ct, NC=NC: e.tensor_tensor(
                out=M1B[:, 0:NC], in0=M1B[:, 0:NC], in1=HBUF[:, ct, 0:NC],
                op=OP.max))
            kb.emit(VE, lambda e, ct=ct, NC=NC: e.scalar_tensor_tensor(
                out=M2B[:, 0:NC], in0=HBUF[:, ct, 0:NC], scalar=-1.0,
                in1=M2B[:, 0:NC], op0=OP.mult, op1=OP.max))
        sm = {}
        for ct in range(32):
            ps = ct % 8
            CV = CONVO if ct % 2 == 0 else CONVO2
            for dt in range(8):
                kb.emit(TE, lambda e, ct=ct, dt=dt, ps=ps: e.matmul(
                    out=PAV[:, 2*ps:2*ps+1, :].rearrange("p a b -> p (a b)"),
                    lhsT=w1q[:, dt, 128*ct:128*(ct+1)], rhs=XTB4[:, dt, :],
                    start=(dt == 0), stop=(dt == 7)))
            tm_h = kb.mark(TE)
            kb.wait(VE, tm_h)
            if ct == 0:
                kb.wait(VE, ve_hc)
            kb.emit(VE, lambda e, ct=ct, ps=ps: e.tensor_tensor(
                out=HBUF[:, ct, 2:130],
                in0=PAV[:, 2*ps:2*ps+1, :].rearrange("p a b -> p (a b)"),
                in1=SHB[:, 0:128], op=OP.mult))
            hv = kb.mark(VE)
            kb.wait(TE, hv)
            kb.selfsync(VE)
            if ct >= 2:
                kb.wait(VE, sm[ct - 2])
            kb.emit(VE, lambda e, ct=ct, CV=CV: e.tensor_scalar(
                out=CV[:, 0:128], in0=HBUF[:, ct, 0:128],
                scalar1=convw4[:, ct, 0:1], op0=OP.mult,
                scalar2=convb4[:, ct:ct+1], op1=OP.add))
            kb.selfsync(VE)
            kb.emit(VE, lambda e, ct=ct, CV=CV: e.scalar_tensor_tensor(
                out=CV[:, 0:128], in0=HBUF[:, ct, 1:129],
                scalar=convw4[:, ct, 1:2], in1=CV[:, 0:128],
                op0=OP.mult, op1=OP.add))
            kb.selfsync(VE)
            kb.emit(VE, lambda e, ct=ct, CV=CV: e.scalar_tensor_tensor(
                out=CV[:, 0:128], in0=HBUF[:, ct, 2:130],
                scalar=convw4[:, ct, 2:3], in1=CV[:, 0:128],
                op0=OP.mult, op1=OP.add))
            if c == 15:
                kb.emit(VE, lambda e, ct=ct, CV=CV: e.tensor_scalar(
                    out=CV[:, 128:129], in0=HBUF[:, ct, 128:129],
                    scalar1=convw4[:, ct, 0:1], op0=OP.mult,
                    scalar2=convb4[:, ct:ct+1], op1=OP.add))
                kb.selfsync(VE)
                kb.emit(VE, lambda e, ct=ct, CV=CV: e.scalar_tensor_tensor(
                    out=CV[:, 128:129], in0=HBUF[:, ct, 129:130],
                    scalar=convw4[:, ct, 1:2], in1=CV[:, 128:129],
                    op0=OP.mult, op1=OP.add))
            cm = kb.mark(VE)
            kb.wait(SE, cm)
            kb.emit(SE, lambda e, ct=ct, NC=NC, CV=CV: e.activation(
                out=HBUF[:, ct, 0:NC], in_=CV[:, 0:NC], func=AF.Silu))
            sm[ct] = kb.mark(SE)
            if ct >= 1:
                kb.wait(VE, sm[ct - 1])
                absmax(ct - 1, NC)
        ve_am = kb.mark(VE)
        kb.wait(VE, ve_am)
        kb.wait(VE, sm[31])
        kb.selfsync(VE)
        absmax(31, NC)
        mm_m = kb.mark(VE)
        kb.wait(TE, mm_m)
        if DUMP == "hck0" and c == 0:
            return dump([(0, HBUF[:, 0:7, :].rearrange("p a b -> p (a b)"))])
        if DUMP == "hck1" and c == 1:
            return dump([(0, HBUF[:, 0:7, :].rearrange("p a b -> p (a b)"))])
        def do_out(o_lo, moff, r_lo, n_out):
            OL = 128
            sh_section(moff, OL)
            if DUMP == "sh0" and c == 0:
                kb.barrier()
                return dump([(0, colsc)])
            shb_tile(OL)
            if DUMP == "shb0" and c == 0:
                kb.barrier()
                return dump([(0, SHB)])
            for ct in range(32):
                kb.emit(VE, lambda e, ct=ct, moff=moff, OL=OL: e._custom_dve(
                    RNDM, out=INTH[:, ct, moff:moff+OL],
                    in0=HBUF[:, ct, moff:moff+OL], in1=SHB[:, 0:OL], imm2=MAGIC))
            ve_ih = kb.mark(VE)
            kb.wait(TE, ve_ih)
            kb.wait("sync", ve_ih)
            if DUMP == "rndm0" and c == 0:
                kb.barrier()
                return dump([(0, SHB)])
            t_r = kb.dma("sync", XT4[:], dr["x3s"][r_lo:r_lo+128, :], sem="LWV")
            return out_section(o_lo, OL, moff, t_r, src_lo=0, nrows=n_out)
        if c == 0:
            r = do_out(0, 1, 0, 127)
            if DUMP in ("sh0", "shb0", "rndm0"):
                return r
            last_out = r
            if DUMP == "oc0":
                kb.barrier()
                return dump([(0, XT4)])
        elif c < 15:
            last_out = do_out(128*c - 1, 0, 128*c - 1, 128)
        else:
            last_out = do_out(128*15 - 1, 0, 128*15 - 1, 128)
            last_out = do_out(1920, 1, 1920, 128)
        if c < 15:
            kb.selfsync(VE)
            kb.emit(VE, lambda e: e.tensor_copy(
                out=HBUF[:, :, 0:2], in_=HBUF[:, :, 128:130]))
            ve_halo = kb.mark(VE)
            kb.wait(VE, ve_halo)
            kb.wait(TE, ve_halo)
    kb.wait("gpsimd", last_out)
    kb.barrier()

    if DBG_STAGE == 3:
        for t2 in range(16):
            tok = kb.dma("sync", XTOK[:], dr["x3s"][128*t2:128*(t2+1), :],
                         sem="LWO")
            kb.wait("gpsimd", tok)
            tok2 = kb.dma("gpsimd", dr["out"][128*t2:128*(t2+1), :], XTOK[:],
                          sem="GOUT")
            kb.wait("sync", tok2)
        kb.wait("gpsimd", ("GOUT", kb.cnt["GOUT"]))

    return finish_emit(nc, es, kb)

def kernel(**inputs):
    in_maps = host_prep(inputs)
    nc = build_nc()
    mybir.codegen_inst_isa_subclasses(nc)
    res = run_bass_kernel_spmd(nc, in_maps, core_ids=list(range(NCORES)))
    outs = [np.asarray(res.results[i]["out"]) for i in range(NCORES)]
    if DBG_STAGE in (1, 2):
        return np.stack([o.T for o in outs], 0)
    return np.stack(outs, 0)
